# version 06
# baseline (speedup 1.0000x reference)
import numpy as np
from contextlib import ExitStack

import concourse.bass as bass
import concourse.mybir as mybir
from concourse.bass_utils import run_bass_kernel_spmd

F32 = mybir.dt.float32
F16 = mybir.dt.float16
BF16 = mybir.dt.bfloat16
AX = mybir.AxisListType.X
OP = mybir.AluOpType
ACT = mybir.ActivationFunctionType

B, T, DM = 4, 4096, 2048
R = 2048
NT = 16
RH = R + 3
DIN = 4096
H = 64
KD = 16
M_IN = 66
IFF = 8192
MFF = 64
KDN = 64
MDN = 16
KWO = 32
MWO = 16
NCORE = 8
FB2 = [(0, 512), (512, 512), (1024, 512), (1536, 512), (2048, 3)]

C_NEGHALF = 0

DBG = False


def build_program():
    nc = bass.Bass("TRN2", target_bir_lowering=False,
                   detect_race_conditions=False)

    def dram_in(name, shape, dtype=F32):
        return nc.dram_tensor(name, list(shape), dtype, kind="ExternalInput")

    def dram_scr(name, shape, dtype=F32):
        if DBG:
            return nc.dram_tensor(name, list(shape), dtype,
                                  kind="ExternalOutput")
        return nc.dram_tensor(name, list(shape), dtype)

    x_own = dram_in("x_own", [R, DM])
    x_halo = dram_in("x_halo", [3, DM])
    w_in_t = dram_in("w_in_t", [KD, M_IN, 128, 128])
    w_out_t = dram_in("w_out_t", [KWO, MWO, 128, 128])
    w_gate_t = dram_in("w_gate_t", [KD, MFF, 128, 128])
    w_up_t = dram_in("w_up_t", [KD, MFF, 128, 128])
    w_down_t = dram_in("w_down_t", [KDN, MDN, 128, 128])
    cw_cols = dram_in("cw_cols", [128, 33 * 4])
    cb_cols = dram_in("cb_cols", [128, 33])
    normw_cols = dram_in("normw_cols", [128, KWO])
    lnw_bc = dram_in("lnw_bc", [128, IFF])
    dtb_col = dram_in("dtb_col", [64, 1])
    alog_col = dram_in("alog_col", [64, 1])
    d_bc = dram_in("d_bc", [128, H])
    smask = dram_in("smask", [64, 1])
    ident = dram_in("ident", [128, 128])
    identb = dram_in("identb", [128, 128], BF16)
    maskneg = dram_in("maskneg", [128, 128])
    ones_r = dram_in("ones_r", [1, 128])
    ones_c = dram_in("ones_c", [128, 1])
    selmat = dram_in("selmat", [8, 8, 128])
    cconst = dram_in("cconst", [128, 4])

    out = nc.dram_tensor("out", [R, DM], F32, kind="ExternalOutput")

    z_scr = dram_scr("z_scr", [32, 128, R])
    xh_scr = dram_scr("xh_scr", [NT, 128, DIN])
    bc_scr = dram_scr("bc_scr", [NT, 128, 128])
    qx_scr = dram_scr("qx_scr", [KD, 128, R], BF16)
    h_scr = dram_scr("h_scr", [NT, 128, IFF])
    qh_scr = dram_scr("qh_scr", [KDN, 128, R], BF16)
    qwd_scr = dram_scr("qwd_scr", [KDN, MDN, 128, 128], BF16)
    yr_scr = dram_scr("yr_scr", [NT, KWO, 128, 128])
    s_scr = dram_scr("s_scr", [NT, 128, DM])
    sloc_scr = nc.dram_tensor("sloc_scr", [NT, 64, DIN], F32)
    sloc_dbg = (nc.dram_tensor("sloc_dbg", [NT, 64, DIN], F32,
                               kind="ExternalOutput") if DBG else None)
    cc_out = nc.dram_tensor("cc_out", [128, DIN], F32)

    ctx = ExitStack()
    with ctx:
        e = ctx.enter_context

        sb_ident = e(nc.sbuf_tensor("sb_ident", [128, 128], F32))
        sb_identb = e(nc.sbuf_tensor("sb_identb", [128, 128], BF16))
        sb_ones_r = e(nc.sbuf_tensor("sb_ones_r", [1, 128], F32))
        sb_ones_c = e(nc.sbuf_tensor("sb_ones_c", [128, 1], F32))
        sb_cconst = e(nc.sbuf_tensor("sb_cconst", [128, 4], F32))
        sb_cw = e(nc.sbuf_tensor("sb_cw", [128, 33 * 4], F32))
        sb_cb = e(nc.sbuf_tensor("sb_cb", [128, 33], F32))
        sb_normw = e(nc.sbuf_tensor("sb_normw", [128, KWO], F32))
        sb_dtb = e(nc.sbuf_tensor("sb_dtb", [64, 1], F32))
        sb_alog = e(nc.sbuf_tensor("sb_alog", [64, 1], F32))
        sb_dbc = e(nc.sbuf_tensor("sb_dbc", [128, H], F32))
        sb_smask = e(nc.sbuf_tensor("sb_smask", [64, 1], F32))
        sb_acol = e(nc.sbuf_tensor("sb_acol", [64, 1], F32))
        sb_dt = e(nc.sbuf_tensor("sb_dt", [64, R], F32))
        sb_lall = e(nc.sbuf_tensor("sb_lall", [64, R], F32))
        sb_lat = e(nc.sbuf_tensor("sb_lat", [64, 128], F32))
        sb_zero64 = e(nc.sbuf_tensor("sb_zero64", [64, 128], F32))
        sb_lcum = e(nc.sbuf_tensor("sb_lcum", [64, 2], F32))
        sb_lcum_pre = e(nc.sbuf_tensor("sb_lcum_pre", [64, NT], F32))
        sb_sxc = e(nc.sbuf_tensor("sb_sxc", [128, NT], F32))
        sb_s2c = e(nc.sbuf_tensor("sb_s2c", [128, NT], F32))
        sb_swinv = e(nc.sbuf_tensor("sb_swinv", [128, 4], F32))
        sb_wsum = e(nc.sbuf_tensor("sb_wsum", [128, 4], F32))
        sb_gam = e(nc.sbuf_tensor("sb_gam", [128, 64], F32))

        pb = [e(nc.psum_tensor(f"pb{i}", [128, 512], F32)) for i in range(6)]
        pbh = e(nc.psum_tensor("pbh", [128, 1024], BF16))

        block = e(nc.Block())

        ENGS = ("sync", "vector", "scalar", "tensor", "gpsimd")
        esem = {n: e(nc.semaphore(f"se_{n}")) for n in ENGS}
        ecnt = {n: 0 for n in ENGS}
        dsem = {n: e(nc.semaphore(f"sd_{n}")) for n in
                ("sync", "scalar", "gpsimd")}
        dcnt = {n: 0 for n in ("sync", "scalar", "gpsimd")}
        sB = e(nc.semaphore("sBar"))
        sC = e(nc.semaphore("sColl"))
        sW2 = e(nc.semaphore("sW2"))
        sW4 = e(nc.semaphore("sW4"))
        sWC = e(nc.semaphore("sWC"))
        sWP = e(nc.semaphore("sWP"))
        sWD = e(nc.semaphore("sWD"))
        sOut = e(nc.semaphore("sOut"))
        nbar = [0]
        prev = [None]

        def step(ename, fn, dmas=0):
            @getattr(block, ename)
            def _(eng):
                if prev[0] is not None:
                    sem, val = prev[0]
                    eng.wait_ge(sem, val)
                last = fn(eng)
                if dmas:
                    dcnt[ename] += 16 * dmas
                    prev[0] = (dsem[ename], dcnt[ename])
                else:
                    if last is None:
                        eng.drain()
                        last = eng.nop()
                    ecnt[ename] += 1
                    last.then_inc(esem[ename], 1)
                    prev[0] = (esem[ename], ecnt[ename])

        def sfence(eng, ename, ins):
            ecnt[ename] += 1
            ins.then_inc(esem[ename], 1)
            eng.wait_ge(esem[ename], ecnt[ename])

        def dma(eng, ename, out_ap, in_ap):
            ins = eng.dma_start(out=out_ap, in_=in_ap)
            ins.then_inc(dsem[ename], 16)
            return ins

        def barrier():
            nbar[0] += 1
            tgt = 5 * nbar[0]
            for name in ENGS:
                @getattr(block, name)
                def _(eng, tgt=tgt):
                    if prev[0] is not None:
                        sem, val = prev[0]
                        eng.wait_ge(sem, val)
                    eng.drain()
                    eng.sem_inc(sB, 1)
                    eng.wait_ge(sB, tgt)
            prev[0] = None

        def transpose_batch(pairs, bf=False):
            W = 8
            for w0 in range(0, len(pairs), W):
                wave = pairs[w0:w0 + W]

                def tr(tensor, wave=wave):
                    for j, (src, dst) in enumerate(wave):
                        P, Q = src.shape
                        if bf:
                            tensor.transpose(pbh[:Q, j * 128:j * 128 + P],
                                             src, sb_identb[:P, :P])
                        else:
                            bank = pb[2 + j // 4]
                            slot = (j % 4) * 128
                            tensor.transpose(bank[:Q, slot:slot + P], src,
                                             sb_ident[:P, :P])
                    return None
                step("tensor", tr)

                def ev(vector, wave=wave):
                    last = None
                    for j, (src, dst) in enumerate(wave):
                        P, Q = src.shape
                        if bf:
                            last = vector.tensor_copy(
                                dst, pbh[:Q, j * 128:j * 128 + P])
                        else:
                            bank = pb[2 + j // 4]
                            slot = (j % 4) * 128
                            last = vector.tensor_copy(
                                dst, bank[:Q, slot:slot + P])
                    return last
                step("vector", ev)

        def bcast_col(col_ap, dst_ap, tmprow, P=128):
            def t1(tensor):
                tensor.transpose(pb[2][:1, 0:128], col_ap, sb_ident[:, :])
                return None
            step("tensor", t1)

            def v1(vector):
                return vector.tensor_copy(tmprow[:1, 0:128], pb[2][:1, 0:128])
            step("vector", v1)

            def t2(tensor, P=P):
                tensor.matmul(pb[3][:P, 0:128], sb_ones_r[:, :P],
                              tmprow[:1, 0:128], start=True, stop=True)
                return None
            step("tensor", t2)

            def v2(vector, P=P):
                return vector.tensor_copy(dst_ap, pb[3][:P, 0:128])
            step("vector", v2)

        def p0(sync):
            last = None
            for dst, src in [
                    (sb_ident[:], ident.ap()), (sb_identb[:], identb.ap()),
                    (sb_ones_r[:], ones_r.ap()),
                    (sb_ones_c[:], ones_c.ap()), (sb_cconst[:], cconst.ap()),
                    (sb_cw[:], cw_cols.ap()), (sb_cb[:], cb_cols.ap()),
                    (sb_normw[:], normw_cols.ap()), (sb_dtb[:], dtb_col.ap()),
                    (sb_alog[:], alog_col.ap()), (sb_dbc[:], d_bc.ap()),
                    (sb_smask[:], smask.ap())]:
                last = dma(sync, "sync", dst, src)
            return last
        step("sync", p0, dmas=12)

        def p0v(vector):
            vector.memset(sb_zero64[:], 0.0)
            vector.memset(sb_lcum[:], 0.0)
            return vector.memset(sb_wsum[:], 0.0)
        step("vector", p0v)

        def p0s(scalar):
            return scalar.activation(sb_acol[:], sb_alog[:], ACT.Exp)
        step("scalar", p0s)

        def p0v2(vector):
            return vector.tensor_scalar(sb_acol[:], sb_acol[:], -1.0, None,
                                        OP.mult)
        step("vector", p0v2)

        f12 = ExitStack()
        xnfm = [f12.enter_context(nc.sbuf_tensor(f"xnfm{k}", [128, RH], F32))
                for k in range(KD)]
        f1 = ExitStack()
        xtile = f1.enter_context(nc.sbuf_tensor("xtile", [128, DM], F32))
        f1w = f1.enter_context(nc.sbuf_tensor("f1w", [128, DM], F32))
        f1q = f1.enter_context(nc.sbuf_tensor("f1q", [128, DM], BF16))
        f1h = f1.enter_context(nc.sbuf_tensor("f1h", [128, DM], F16))
        f1qs = f1.enter_context(nc.sbuf_tensor("f1qs", [128, DM], BF16))
        f1v = f1.enter_context(nc.sbuf_tensor("f1v", [128, DM], F32))
        f1r = f1.enter_context(nc.sbuf_tensor("f1r", [128, DM], F32))
        f1c = f1.enter_context(nc.sbuf_tensor("f1c", [128, 8], F32))

        for i in range(NT + 1):
            P = 128 if i < NT else 3

            def ld(scalar, i=i, P=P):
                src = (x_own.ap()[i * 128:(i + 1) * 128, :] if i < NT
                       else x_halo.ap()[:, :])
                return dma(scalar, "scalar", xtile[:P, :], src)
            step("scalar", ld, dmas=1)

            def v1(vector, P=P):
                ins = vector.reduce_sum(f1c[:P, 0:1], xtile[:P, :], AX,
                                        negate=True)
                sfence(vector, "vector", ins)
                ins = vector.tensor_scalar(f1c[:P, 1:2], f1c[:P, 0:1],
                                           1.0 / DM, None, OP.mult)
                sfence(vector, "vector", ins)
                return vector.tensor_scalar(f1w[:P, :], xtile[:P, :],
                                            f1c[:P, 1:2], None, OP.add)
            step("vector", v1)

            def s1(scalar, P=P):
                return scalar.activation(xtile[:P, :], f1w[:P, :], ACT.Square,
                                         accum_out=f1c[:P, 2:3])
            step("scalar", s1)

            def v2(vector, P=P):
                return vector.tensor_scalar(f1c[:P, 3:4], f1c[:P, 2:3],
                                            1.0 / DM, 1e-5, OP.mult, OP.add)
            step("vector", v2)

            def s2(scalar, P=P):
                scalar.activation(f1c[:P, 4:5], f1c[:P, 3:4], ACT.Ln)
                return scalar.activation(
                    f1c[:P, 5:6], f1c[:P, 4:5], ACT.Exp,
                    scale=sb_cconst[:P, C_NEGHALF:C_NEGHALF + 1])
            step("scalar", s2)

            def v3(vector, i=i, P=P):
                vector.tensor_scalar(f1w[:P, :], f1w[:P, :], f1c[:P, 5:6],
                                     None, OP.mult)
                if P == 128:
                    ins = vector.reduce_max(f1c[:P, 6:7], f1w[:P, :], AX,
                                            apply_absolute_value=True)
                    sfence(vector, "vector", ins)
                    vector.tensor_scalar(f1c[:P, 6:7], f1c[:P, 6:7], 1e-5,
                                         None, OP.max)
                    ins = vector.reciprocal(f1c[:P, 7:8], f1c[:P, 6:7])
                    sfence(vector, "vector", ins)
                    vector.tensor_scalar(f1c[:P, 7:8], f1c[:P, 7:8], 127.0,
                                         None, OP.mult)
                    ins = vector.tensor_scalar(sb_sxc[:P, i:i + 1],
                                               f1c[:P, 6:7], 1.0 / 127.0,
                                               None, OP.mult)
                    sfence(vector, "vector", ins)
                    vector.tensor_scalar(f1v[:P, :], f1w[:P, :], f1c[:P, 7:8],
                                         None, OP.mult)
                    vector.tensor_scalar(f1h[:P, :], f1v[:P, :], 1536.0,
                                         None, OP.add)
                    vector.tensor_scalar(f1r[:P, :], f1h[:P, :], 1536.0,
                                         None, OP.subtract)
                    vector.tensor_tensor(xtile[:P, :], f1v[:P, :],
                                         f1r[:P, :], OP.subtract)
                    vector.scalar_tensor_tensor(f1v[:P, :], xtile[:P, :], 0.5,
                                                f1r[:P, :], OP.is_gt, OP.add)
                    vector.scalar_tensor_tensor(f1r[:P, :], xtile[:P, :],
                                                -0.5, f1v[:P, :], OP.is_ge,
                                                OP.add)
                    return vector.tensor_scalar(f1q[:P, :], f1r[:P, :], 1.0,
                                                None, OP.subtract)
                return None
            step("vector", v3)

            col = 3 + i * 128 if i < NT else 0
            transpose_batch([(f1w[:P, k * 128:(k + 1) * 128],
                              xnfm[k][:, col:col + P]) for k in range(KD)])

            if i < NT:
                transpose_batch([(f1q[:, k * 128:(k + 1) * 128],
                                  f1qs[:, k * 128:(k + 1) * 128])
                                 for k in range(KD)], bf=True)

                def st(gpsimd, i=i):
                    return dma(
                        gpsimd, "gpsimd",
                        qx_scr.ap()[:, :, i * 128:(i + 1) * 128].rearrange(
                            "k p q -> p k q"),
                        f1qs[:, :].rearrange("p (k q) -> p k q", k=KD))
                step("gpsimd", st, dmas=1)

        f1.close()
        barrier()

        f2 = ExitStack()
        wbuf = [f2.enter_context(nc.sbuf_tensor(f"wbuf{j}", [128, KD * 128],
                                                F32)) for j in range(2)]
        f2s0 = f2.enter_context(nc.sbuf_tensor("f2s0", [128, RH], F32))
        f2s = [f2s0, f2s0]
        f2c = [f2.enter_context(nc.sbuf_tensor(f"f2c{j}", [128, R], F32))
               for j in range(2)]
        f2t = f2.enter_context(nc.sbuf_tensor("f2t", [128, 2048], F32))
        NPAW = 4
        paw = f2.enter_context(nc.sbuf_tensor("paw", [128, NPAW * 128], F32))

        pa_srcs = []
        for wi, wt in enumerate((w_gate_t, w_up_t, w_down_t)):
            flat = wt.ap().rearrange("a b p q -> (a b) p q")
            ntile = KD * MFF if wi < 2 else KDN * MDN
            for j in range(0, ntile, NPAW):
                pa_srcs.append((wi, flat[j:j + NPAW]))

        FB2Z = [(3, 512), (515, 512), (1027, 512), (1539, 512)]

        def f2_blocks(m):
            return FB2 if 32 <= m <= 64 else FB2Z
        mmcum = [0]
        for _m in range(M_IN):
            mmcum.append(mmcum[-1] + len(f2_blocks(_m)))
        bank_of = {}
        for _m in range(M_IN):
            _rot = [0, 1, 4, 5] if _m < 32 else [0, 1]
            for _fi in range(len(f2_blocks(_m))):
                bank_of[mmcum[_m] + _fi] = _rot[_fi % len(_rot)]
        s2mm = e(nc.semaphore("s2mm"))
        s2ev = e(nc.semaphore("s2ev"))
        s2cv = e(nc.semaphore("s2cv"))
        s2si = e(nc.semaphore("s2si"))
        s2tr = e(nc.semaphore("s2tr"))
        s2te = e(nc.semaphore("s2te"))
        sPAr = e(nc.semaphore("sPAr"))
        sPAv = e(nc.semaphore("sPAv"))
        NPA = len(pa_srcs)
        g_after = {}
        pa_after = {}
        gc = [dcnt["gpsimd"]]
        _gcc = gc[0]
        _pa = 0
        for _m in range(M_IN):
            if _m < 32:
                _gcc += 16
                g_after[("z", _m)] = _gcc
            elif 33 <= _m <= 64:
                _gcc += 16
                g_after[("xh", _m - 33)] = _gcc
            elif _m == 65:
                _gcc += 16
                g_after[("bc", 0)] = _gcc
            while _pa < min(NPA, (NPA * (_m + 1)) // M_IN):
                _gcc += 16
                pa_after[_pa] = _gcc
                _pa += 1

        @block.sync
        def _(sync):
            for m in range(M_IN):
                if m >= 2:
                    sync.wait_ge(s2mm, mmcum[m - 1])
                sync.dma_start(
                    out=wbuf[m % 2][:, :].rearrange("p (k q) -> p k q", k=KD),
                    in_=w_in_t.ap()[:, m].rearrange("k p q -> p k q"),
                ).then_inc(dsem["sync"], 16)
                dcnt["sync"] += 16
        w2_base = dcnt["sync"] - 16 * M_IN

        last_mm = {}

        @block.tensor
        def _(tensor):
            for m in range(M_IN):
                if 33 <= m <= 64:
                    mc1 = m - 33
                    tensor.wait_ge(s2si, mc1 + 1)
                    if mc1 >= 1:
                        tensor.wait_ge(s2te, mc1)
                    else:
                        need = max(last_mm.get(4, -1), last_mm.get(5, -1))
                        if need >= 0:
                            tensor.wait_ge(s2ev, need + 1)
                    for j in range(16):
                        bank = pb[2 + j // 4]
                        slot = (j % 4) * 128
                        tensor.transpose(
                            bank[:, slot:slot + 128],
                            f2c[(m - 1) % 2][:, j * 128:(j + 1) * 128],
                            sb_ident[:, :])
                    tensor.drain()
                    tensor.nop().then_inc(s2tr, 1)
                for fi, (f0, fw) in enumerate(f2_blocks(m)):
                    gidx = mmcum[m] + fi
                    b = bank_of[gidx]
                    if fi == 0:
                        tensor.wait_ge(dsem["sync"], w2_base + 16 * (m + 1))
                    if b in last_mm:
                        tensor.wait_ge(s2ev, last_mm[b] + 1)
                    if m == 65 and b in (4, 5):
                        tensor.wait_ge(s2te, 32)
                    last = None
                    for k in range(KD):
                        last = tensor.matmul(
                            pb[b][:, 0:fw],
                            wbuf[m % 2][:, k * 128:(k + 1) * 128],
                            xnfm[k][:, f0:f0 + fw],
                            start=(k == 0), stop=(k == KD - 1))
                    last.then_inc(s2mm, 1)
                    last_mm[b] = gidx

        pa_v = [0]

        @block.vector
        def _(vector):
            for m in range(M_IN):
                if 33 <= m <= 64:
                    mc1 = m - 33
                    vector.wait_ge(s2tr, mc1 + 1)
                    if mc1 >= 1:
                        vector.wait_ge(dsem["gpsimd"],
                                       g_after[("xh", mc1 - 1)])
                    for j in range(4):
                        ins = vector.tensor_copy(
                            f2t[:, j * 512:(j + 1) * 512], pb[2 + j][:, 0:512])
                    ins.then_inc(s2te, 1)
                P_EV = 128 if m < 65 else 64
                for fi, (f0, fw) in enumerate(f2_blocks(m)):
                    gidx = mmcum[m] + fi
                    vector.wait_ge(s2mm, gidx + 1)
                    if fi == 0 and m >= 1 and (m - 1) < 32:
                        vector.wait_ge(dsem["gpsimd"], g_after[("z", m - 1)])
                    vector.tensor_copy(f2s0[:P_EV, f0:f0 + fw],
                                       pb[bank_of[gidx]][:P_EV, 0:fw]
                                       ).then_inc(s2ev, 1)
                if 32 <= m < 65:
                    mc = m - 32
                    if mc >= 2 and mc - 2 < 32:
                        vector.wait_ge(s2tr, mc - 1)
                    vector.tensor_scalar(f2c[m % 2][:, :],
                                         f2s[m % 2][:, 0:R],
                                         sb_cw[:, mc * 4:mc * 4 + 1], None,
                                         OP.mult)
                    last = None
                    for k in (1, 2, 3):
                        last = vector.scalar_tensor_tensor(
                            f2c[m % 2][:, :], f2s[m % 2][:, k:k + R],
                            sb_cw[:, mc * 4 + k:mc * 4 + k + 1],
                            f2c[m % 2][:, :], OP.mult, OP.add)
                    last.then_inc(s2cv, 1)
                while pa_v[0] < min(NPA, (NPA * (m + 1)) // M_IN):
                    i = pa_v[0]
                    pa_v[0] += 1
                    wi = pa_srcs[i][0]
                    vector.wait_ge(sPAr, i + 1)
                    vector.tensor_tensor(sb_wsum[:, wi:wi + 1],
                                         sb_wsum[:, wi:wi + 1],
                                         sb_wsum[:, 3:4],
                                         OP.add).then_inc(sPAv, 1)

        pa_s = [0]

        @block.scalar
        def _(scalar):
            def pa_emit_s(upto):
                while pa_s[0] < min(NPA, upto):
                    i = pa_s[0]
                    pa_s[0] += 1
                    scalar.wait_ge(dsem["gpsimd"], pa_after[i])
                    if i >= 1:
                        scalar.wait_ge(sPAv, i)
                    scalar.activation(paw[:, :], paw[:, :], ACT.Abs,
                                      accum_out=sb_wsum[:, 3:4]).then_inc(
                                          sPAr, 1)

            for m in range(M_IN):
                if 32 <= m < 65:
                    mc = m - 32
                    scalar.wait_ge(s2cv, mc + 1)
                    scalar.activation(f2c[m % 2][:, :],
                                      f2c[m % 2][:, :], ACT.Silu,
                                      bias=sb_cb[:, mc:mc + 1]).then_inc(
                                          s2si, 1)
                pa_emit_s((NPA * (m + 1)) // M_IN)
            pa_emit_s(NPA)

        pa_g = [0]

        @block.gpsimd
        def _(gpsimd):
            def pa_emit(upto):
                while pa_g[0] < min(NPA, upto):
                    i = pa_g[0]
                    pa_g[0] += 1
                    if i >= 1:
                        gpsimd.wait_ge(sPAr, i)
                    wi, src = pa_srcs[i]
                    gpsimd.dma_start(
                        out=paw[:, :].rearrange("p (n q) -> p n q", n=NPAW),
                        in_=src.rearrange("n p q -> p n q"),
                    ).then_inc(dsem["gpsimd"], 16)
                    gc[0] += 16
                    assert pa_after[i] == gc[0]

            for m in range(M_IN):
                if m < 32:
                    gpsimd.wait_ge(s2ev, mmcum[m + 1])
                    gpsimd.dma_start(out=z_scr.ap()[m],
                                     in_=f2s[m % 2][:, 3:3 + R]).then_inc(
                                         dsem["gpsimd"], 16)
                    gc[0] += 16
                    assert g_after[("z", m)] == gc[0]
                elif 33 <= m <= 64:
                    mc = m - 33
                    gpsimd.wait_ge(s2te, mc + 1)
                    gpsimd.dma_start(
                        out=xh_scr.ap()[:, :, mc * 128:(mc + 1) * 128]
                        .rearrange("c p q -> p c q"),
                        in_=f2t[:, :].rearrange("p (c q) -> p c q", c=NT),
                    ).then_inc(dsem["gpsimd"], 16)
                    gc[0] += 16
                    assert g_after[("xh", mc)] == gc[0]
                elif m == 65:
                    gpsimd.wait_ge(s2si, 33)
                    gpsimd.dma_start(
                        out=bc_scr.ap()[:, :, :].rearrange("c p q -> p c q"),
                        in_=f2c[0][:, :].rearrange("p (c q) -> p c q", c=NT),
                    ).then_inc(dsem["gpsimd"], 16)
                    gc[0] += 16
                    assert g_after[("bc", 0)] == gc[0]
                pa_emit((NPA * (m + 1)) // M_IN)
            pa_emit(NPA)
            dcnt["gpsimd"] = gc[0]
            gpsimd.wait_ge(dsem["gpsimd"], gc[0])

        prev[0] = None
        barrier()

        def dtact(scalar):
            scalar.activation(f2c[1][:64, 0:R], f2s0[:64, 3:3 + R],
                              ACT.Exp, bias=sb_dtb[:, 0:1])
            return scalar.activation(sb_dt[:, :], f2c[1][:64, 0:R],
                                     ACT.Ln, bias=1.0)
        step("scalar", dtact)

        def wfin_mm(tensor):
            tensor.matmul(pb[4][:1, 0:3], sb_wsum[:, 0:3], sb_ones_c[:, 0:1],
                          start=True, stop=True)
            return None
        def wfin_mm2(tensor):
            tensor.matmul(pb[4][:1, 0:3], sb_ones_c[:, 0:1], sb_wsum[:, 0:3],
                          start=True, stop=True)
            return None
        step("tensor", wfin_mm2)

        def wfin_v(vector):
            return vector.tensor_scalar(f2s[0][:1, 0:3], pb[4][:1, 0:3],
                                        1.0 / (IFF * DM), 1e-5, OP.mult,
                                        OP.max)
        step("vector", wfin_v)

        def wfin_bc(tensor):
            tensor.matmul(pb[4][:, 64:67], sb_ones_r[:, :], f2s[0][:1, 0:3],
                          start=True, stop=True)
            return None
        step("tensor", wfin_bc)

        def wfin_v2(vector):
            ins = vector.tensor_copy(sb_wsum[:, 0:3], pb[4][:, 64:67])
            sfence(vector, "vector", ins)
            return vector.reciprocal(sb_swinv[:, 0:3], sb_wsum[:, 0:3])
        step("vector", wfin_v2)

        f2.close()
        f12.close()
        barrier()

        f3a = ExitStack()
        sb_state = f3a.enter_context(nc.sbuf_tensor("sb_state", [64, DIN],
                                                    F32))
        xtm = [f3a.enter_context(nc.sbuf_tensor(f"xtm{j}", [128, DIN], F32))
               for j in range(2)]
        bcB = [f3a.enter_context(nc.sbuf_tensor(f"bcB{j}", [64, 128], F32))
               for j in range(2)]
        f3ld = f3a.enter_context(nc.sbuf_tensor("f3ld", [128, 192], F32))
        f3fac = f3a.enter_context(nc.sbuf_tensor("f3fac", [128, 64], F32))
        f3btm = f3a.enter_context(nc.sbuf_tensor("f3btm", [128, 64], F32))
        f3bs = f3a.enter_context(nc.sbuf_tensor("f3bs", [128, 512], F32))
        f3row = f3a.enter_context(nc.sbuf_tensor("f3row", [1, 192], F32))
        f3crep = f3a.enter_context(nc.sbuf_tensor("f3crep", [64, 64], F32))

        def st0(vector):
            return vector.memset(sb_state[:, :], 0.0)
        step("vector", st0)

        def pf0(sync):
            dma(sync, "sync", xtm[0][:, :], xh_scr.ap()[0])
            return dma(sync, "sync", bcB[0][:, :], bc_scr.ap()[0, 0:64])
        step("sync", pf0, dmas=2)

        for c in range(NT):
            cur, nxt = c % 2, (c + 1) % 2
            if c + 1 < NT:
                def pf(sync, c=c, nxt=nxt):
                    dma(sync, "sync", xtm[nxt][:, :], xh_scr.ap()[c + 1])
                    return dma(sync, "sync", bcB[nxt][:, :],
                               bc_scr.ap()[c + 1, 0:64])
                step("sync", pf, dmas=2)

            def cum(vector, c=c):
                vector.tensor_scalar(sb_lat[:, :],
                                     sb_dt[:, c * 128:(c + 1) * 128],
                                     sb_acol[:, 0:1], None, OP.mult)
                ins = vector.tensor_tensor_scan(
                    sb_lall[:, c * 128:(c + 1) * 128],
                    sb_lat[:, :], sb_zero64[:, :],
                    0.0, OP.add, OP.add)
                sfence(vector, "vector", ins)
                vector.tensor_copy(sb_lcum_pre[:, c:c + 1], sb_lcum[:, 0:1])
                return vector.tensor_tensor(
                    sb_lcum[:, 0:1], sb_lcum[:, 0:1],
                    sb_lall[:, c * 128 + 127:c * 128 + 128], OP.add)
            step("vector", cum)

            def trs(tensor, c=c, cur=cur):
                tensor.transpose(pb[2][:128, 0:64],
                                 sb_lall[:, c * 128:(c + 1) * 128],
                                 sb_ident[:64, :64])
                tensor.transpose(pb[2][:128, 64:128],
                                 sb_dt[:, c * 128:(c + 1) * 128],
                                 sb_ident[:64, :64])
                tensor.transpose(pb[2][:128, 128:192], bcB[cur][:, :],
                                 sb_ident[:64, :64])
                tensor.transpose(pb[3][:1, 0:64],
                                 sb_lall[:, c * 128 + 127:c * 128 + 128],
                                 sb_ident[:64, :64])
                return None
            step("tensor", trs)

            def ev1(vector, c=c):
                vector.tensor_copy(f3ld[:, 0:128], pb[2][:, 0:128])
                vector.tensor_copy(f3btm[:, :], pb[2][:, 128:192])
                return vector.tensor_copy(f3row[:1, 0:64], pb[3][:1, 0:64])
            step("vector", ev1)

            def crepe(scalar, c=c):
                return scalar.activation(f3row[:1, 64:128], f3row[:1, 0:64],
                                         ACT.Exp)
            step("scalar", crepe)

            def lbc(tensor, c=c):
                tensor.matmul(pb[3][:, 64:128], sb_ones_r[:, :],
                              f3row[:1, 0:64], start=True, stop=True)
                tensor.matmul(pb[3][0:64, 128:192], sb_ones_r[:, 0:64],
                              f3row[:1, 64:128], start=True, stop=True)
                return None
            step("tensor", lbc)

            def fac(vector, c=c):
                vector.tensor_copy(f3crep[:, :], pb[3][0:64, 128:192])
                return vector.tensor_tensor(f3fac[:, :], pb[3][:, 64:128],
                                            f3ld[:, 0:64], OP.subtract)
            step("vector", fac)

            def face(scalar, c=c):
                return scalar.activation(f3fac[:, :], f3fac[:, :], ACT.Exp)
            step("scalar", face)

            def fac2(vector, c=c):
                return vector.tensor_tensor(f3fac[:, :], f3fac[:, :],
                                            f3ld[:, 64:128], OP.mult)
            step("vector", fac2)

            for g in range(8):
                def bs(vector, g=g, cur=cur):
                    last = None
                    for j in range(8):
                        hh = g * 8 + j
                        last = vector.tensor_scalar(
                            f3bs[:, j * 64:(j + 1) * 64], f3btm[:, 0:64],
                            f3fac[:, hh:hh + 1], None, OP.mult)
                    return last
                step("vector", bs)

                def mm3(tensor, g=g, cur=cur):
                    for j in range(8):
                        hh = g * 8 + j
                        tensor.matmul(
                            pb[4][0:64, j * 64:(j + 1) * 64],
                            f3bs[:, j * 64:(j + 1) * 64],
                            xtm[cur][:, hh * 64:(hh + 1) * 64],
                            start=True, stop=True)
                    return None
                step("tensor", mm3)

                def supd(vector, g=g):
                    last = None
                    for j in range(8):
                        hh = g * 8 + j
                        last = vector.scalar_tensor_tensor(
                            sb_state[:, hh * 64:(hh + 1) * 64],
                            sb_state[:, hh * 64:(hh + 1) * 64],
                            f3crep[:, hh:hh + 1],
                            pb[4][0:64, j * 64:(j + 1) * 64],
                            OP.mult, OP.add)
                    return last
                step("vector", supd)

            def stst(gpsimd, c=c):
                ins = dma(gpsimd, "gpsimd", sloc_scr.ap()[c], sb_state[:, :])
                if DBG:
                    ins = dma(gpsimd, "gpsimd", sloc_dbg.ap()[c],
                              sb_state[:, :])
                return ins
            step("gpsimd", stst, dmas=2 if DBG else 1)

        def coll(gpsimd):
            gpsimd.collective_compute(
                "AllGather", OP.bypass,
                replica_groups=[[2 * i, 2 * i + 1] for i in range(4)],
                ins=[sloc_scr.ap()[NT - 1]],
                outs=[cc_out.ap()[:, :]],
            ).then_inc(sC, 1)
            gpsimd.drain()
            return None
        step("gpsimd", coll)

        f3a.close()
        barrier()

        f4 = ExitStack()
        qxr = [f4.enter_context(nc.sbuf_tensor(f"qxr{k}", [128, R], BF16))
               for k in range(KD)]
        wgb = [f4.enter_context(nc.sbuf_tensor(f"wgb{j}", [128, KD * 128],
                                               F32)) for j in range(2)]
        wub = [f4.enter_context(nc.sbuf_tensor(f"wub{j}", [128, KD * 128],
                                               F32)) for j in range(2)]
        qwg = [f4.enter_context(nc.sbuf_tensor(f"qwg{j}", [128, KD * 128],
                                               BF16)) for j in range(2)]
        qwu = [f4.enter_context(nc.sbuf_tensor(f"qwu{j}", [128, KD * 128],
                                               BF16)) for j in range(2)]
        f4g = [f4.enter_context(nc.sbuf_tensor(f"f4g{j}", [128, 512], F32))
               for j in range(2)]
        f4u = [f4.enter_context(nc.sbuf_tensor(f"f4u{j}", [128, 512], F32))
               for j in range(2)]
        f4h = [f4.enter_context(nc.sbuf_tensor(f"f4h{j}", [128, 512], F32))
               for j in range(2)]
        f4t = [f4.enter_context(nc.sbuf_tensor(f"f4t{j}", [128, 512], F32))
               for j in range(2)]
        f4w = f4.enter_context(nc.sbuf_tensor("f4w", [128, 512], F32))
        f4qs = f4.enter_context(nc.sbuf_tensor("f4qs", [128, 512], F32))
        sxbc = f4.enter_context(nc.sbuf_tensor("sxbc", [128, R], F32))
        f4row = f4.enter_context(nc.sbuf_tensor("f4row", [1, 128], F32))

        @block.sync
        def _(sync):
            for k in range(KD):
                sync.dma_start(out=qxr[k][:, :], in_=qx_scr.ap()[k]).then_inc(
                    dsem["sync"], 16)
                dcnt["sync"] += 16
            for m in range(MFF):
                if m >= 2:
                    sync.wait_ge(sW4, m - 1)
                sync.dma_start(
                    out=wgb[m % 2][:, :].rearrange("p (k q) -> p k q", k=KD),
                    in_=w_gate_t.ap()[:, m].rearrange("k p q -> p k q"),
                ).then_inc(dsem["sync"], 16)
                sync.dma_start(
                    out=wub[m % 2][:, :].rearrange("p (k q) -> p k q", k=KD),
                    in_=w_up_t.ap()[:, m].rearrange("k p q -> p k q"),
                ).then_inc(dsem["sync"], 16)
                dcnt["sync"] += 32
        w4_base = dcnt["sync"] - 32 * MFF

        for i in range(NT):
            bcast_col(sb_sxc[:, i:i + 1], sxbc[:, i * 128:(i + 1) * 128],
                      f4row)

        NI4 = MFF * 4
        s4mm = e(nc.semaphore("s4mm"))
        s4ep = e(nc.semaphore("s4ep"))
        s4si = e(nc.semaphore("s4si"))
        s4hm = e(nc.semaphore("s4hm"))
        s4tr = e(nc.semaphore("s4tr"))
        s4te = e(nc.semaphore("s4te"))
        g4base = dcnt["gpsimd"]

        @block.tensor
        def _(tensor):
            def tr4(j):
                tensor.wait_ge(s4hm, j + 1)
                if j >= 2:
                    tensor.wait_ge(s4te, j - 1)
                for q in range(4):
                    tensor.transpose(pb[2 + j % 2][:, q * 128:(q + 1) * 128],
                                     f4h[j % 2][:, q * 128:(q + 1) * 128],
                                     sb_ident[:, :])
                tensor.drain()
                tensor.nop().then_inc(s4tr, 1)

            for m in range(MFF):
                for f in range(4):
                    it = m * 4 + f
                    if it >= 1:
                        tr4(it - 1)
                    if f == 0:
                        tensor.wait_ge(dsem["sync"], w4_base + 32 * (m + 1))
                        tensor.wait_ge(sW4, m + 1)
                    if it >= 2:
                        tensor.wait_ge(s4ep, it - 1)
                    gb = pb[(it % 2) * 4]
                    ub = pb[1 + (it % 2) * 4]
                    for k in range(KD):
                        tensor.matmul(
                            gb[:, :], qwg[m % 2][:, k * 128:(k + 1) * 128],
                            qxr[k][:, f * 512:(f + 1) * 512],
                            start=(k == 0), stop=(k == KD - 1))
                    last = None
                    for k in range(KD):
                        last = tensor.matmul(
                            ub[:, :], qwu[m % 2][:, k * 128:(k + 1) * 128],
                            qxr[k][:, f * 512:(f + 1) * 512],
                            start=(k == 0), stop=(k == KD - 1))
                    last.then_inc(s4mm, 1)
            tr4(NI4 - 1)

        @block.vector
        def _(vector):
            for m in range(MFF):
                vector.wait_ge(dsem["sync"], w4_base + 32 * (m + 1))
                if m >= 2:
                    vector.wait_ge(s4mm, 4 * (m - 1))
                last = None
                for src, qdst, wi in ((wgb[m % 2], qwg[m % 2], 0),
                                      (wub[m % 2], qwu[m % 2], 1)):
                    for blk in range(4):
                        sl = slice(blk * 512, (blk + 1) * 512)
                        vector.tensor_scalar(f4w[:, :], src[:, sl],
                                             sb_swinv[:, wi:wi + 1], None,
                                             OP.mult)
                        vector.tensor_scalar(f4qs[:, :], f4w[:, :], -0.5,
                                             None, OP.is_le)
                        last = vector.scalar_tensor_tensor(
                            qdst[:, sl], f4w[:, :], 0.5, f4qs[:, :],
                            OP.is_ge, OP.subtract)
                last.then_inc(sW4, 1)
                for f in range(4):
                    it = m * 4 + f
                    vector.wait_ge(s4mm, it + 1)
                    gb = pb[(it % 2) * 4]
                    ub = pb[1 + (it % 2) * 4]
                    vector.tensor_tensor(f4g[it % 2][:, :], gb[:, :],
                                         sxbc[:, f * 512:(f + 1) * 512],
                                         OP.mult)
                    vector.tensor_scalar(f4g[it % 2][:, :], f4g[it % 2][:, :],
                                         sb_wsum[:, 0:1], None, OP.mult)
                    vector.tensor_tensor(f4u[it % 2][:, :], ub[:, :],
                                         sxbc[:, f * 512:(f + 1) * 512],
                                         OP.mult)
                    vector.tensor_scalar(f4u[it % 2][:, :], f4u[it % 2][:, :],
                                         sb_wsum[:, 1:2], None,
                                         OP.mult).then_inc(s4ep, 1)
                    vector.wait_ge(s4si, it + 1)
                    vector.tensor_tensor(f4h[it % 2][:, :], f4g[it % 2][:, :],
                                         f4u[it % 2][:, :],
                                         OP.mult).then_inc(s4hm, 1)
                    if it >= 1:
                        j = it - 1
                        vector.wait_ge(s4tr, j + 1)
                        if j >= 2:
                            vector.wait_ge(dsem["gpsimd"],
                                           g4base + 16 * (j - 1))
                        vector.tensor_copy(f4t[j % 2][:, :],
                                           pb[2 + j % 2][:, 0:512]).then_inc(
                                               s4te, 1)
            j = NI4 - 1
            vector.wait_ge(s4tr, j + 1)
            vector.wait_ge(dsem["gpsimd"], g4base + 16 * (j - 1))
            vector.tensor_copy(f4t[j % 2][:, :],
                               pb[2 + j % 2][:, 0:512]).then_inc(s4te, 1)

        @block.scalar
        def _(scalar):
            for it in range(NI4):
                scalar.wait_ge(s4ep, it + 1)
                scalar.activation(f4g[it % 2][:, :], f4g[it % 2][:, :],
                                  ACT.Silu).then_inc(s4si, 1)

        @block.gpsimd
        def _(gpsimd):
            for it in range(NI4):
                m, f = it // 4, it % 4
                gpsimd.wait_ge(s4te, it + 1)
                gpsimd.dma_start(
                    out=h_scr.ap()[f * 4:(f + 1) * 4, :,
                                   m * 128:(m + 1) * 128]
                    .rearrange("c p q -> p c q"),
                    in_=f4t[it % 2][:, :].rearrange("p (c q) -> p c q", c=4),
                ).then_inc(dsem["gpsimd"], 16)
                dcnt["gpsimd"] += 16
            gpsimd.wait_ge(dsem["gpsimd"], dcnt["gpsimd"])

        prev[0] = None
        f4.close()
        barrier()

        f3b = ExitStack()
        sb_mask = f3b.enter_context(nc.sbuf_tensor("sb_mask", [128, 128],
                                                   F32))
        sb_sel = f3b.enter_context(nc.sbuf_tensor("sb_sel", [8, 8 * 128],
                                                  F32))
        xtmB = [f3b.enter_context(nc.sbuf_tensor(f"xtmB{j}", [128, DIN], F32))
                for j in range(2)]
        bcBB = [f3b.enter_context(nc.sbuf_tensor(f"bcBB{j}", [64, 128], F32))
                for j in range(2)]
        bcCB = [f3b.enter_context(nc.sbuf_tensor(f"bcCB{j}", [64, 128], F32))
                for j in range(2)]
        zch = [f3b.enter_context(nc.sbuf_tensor(f"zch{j}", [128, DIN], F32))
               for j in range(2)]
        spre = [f3b.enter_context(nc.sbuf_tensor(f"spre{j}", [64, DIN], F32))
                for j in range(2)]
        sinit = f3b.enter_context(nc.sbuf_tensor("sinit", [64, DIN], F32))
        sful = f3b.enter_context(nc.sbuf_tensor("sful", [64, DIN], F32))
        cbt = f3b.enter_context(nc.sbuf_tensor("cbt", [128, 128], F32))
        mht = f3b.enter_context(nc.sbuf_tensor("mht", [128, 8 * 128], F32))
        ych = f3b.enter_context(nc.sbuf_tensor("ych", [128, DIN], F32))
        lamr = f3b.enter_context(nc.sbuf_tensor("lamr", [64, 64], F32))
        yst = f3b.enter_context(nc.sbuf_tensor("yst", [128, 512], F32))
        rows8 = f3b.enter_context(nc.sbuf_tensor("rows8", [8, 128], F32))
        f3brow = f3b.enter_context(nc.sbuf_tensor("f3brow", [1, 128], F32))
        f3bld = f3b.enter_context(nc.sbuf_tensor("f3bld", [128, 128], F32))
        sqacc = f3b.enter_context(nc.sbuf_tensor("sqacc", [128, 128], F32))
        yfm = f3b.enter_context(nc.sbuf_tensor("yfm", [128, DIN], F32))

        def ldsel(sync):
            dma(sync, "sync", sb_mask[:], maskneg.ap())
            return dma(sync, "sync",
                       sb_sel[:].rearrange("p (j q) -> p j q", j=8),
                       selmat.ap().rearrange("j p q -> p j q"))
        step("sync", ldsel, dmas=2)

        def getinit(gpsimd):
            gpsimd.wait_ge(sC, 1)
            return dma(gpsimd, "gpsimd", sinit[:, :], cc_out.ap()[0:64, :])
        step("gpsimd", getinit, dmas=1)

        def maskinit(vector):
            return vector.tensor_scalar(sinit[:, :], sinit[:, :],
                                        sb_smask[:, 0:1], None, OP.mult)
        step("vector", maskinit)

        def pfb0(sync):
            dma(sync, "sync", xtmB[0][:, :], xh_scr.ap()[0])
            dma(sync, "sync", bcBB[0][:, :], bc_scr.ap()[0, 0:64])
            dma(sync, "sync", bcCB[0][:, :], bc_scr.ap()[0, 64:128])
            return dma(sync, "sync",
                       zch[0][:, :].rearrange("p (mq q) -> p mq q", mq=32),
                       z_scr.ap()[:, :, 0:128].rearrange("m p q -> p m q"))
        step("sync", pfb0, dmas=4)

        for c in range(NT):
            cur, nxt = c % 2, (c + 1) % 2
            if c + 1 < NT:
                def pfb(sync, c=c, nxt=nxt):
                    dma(sync, "sync", xtmB[nxt][:, :], xh_scr.ap()[c + 1])
                    dma(sync, "sync", bcBB[nxt][:, :],
                        bc_scr.ap()[c + 1, 0:64])
                    dma(sync, "sync", bcCB[nxt][:, :],
                        bc_scr.ap()[c + 1, 64:128])
                    dma(sync, "sync",
                        zch[nxt][:, :].rearrange("p (mq q) -> p mq q", mq=32),
                        z_scr.ap()[:, :, (c + 1) * 128:(c + 2) * 128]
                        .rearrange("m p q -> p m q"))
                    return dma(sync, "sync", spre[nxt][:, :],
                               sloc_scr.ap()[c])
                step("sync", pfb, dmas=5)
            if c == 0:
                def z0(vector):
                    return vector.memset(spre[0][:, :], 0.0)
                step("vector", z0)

            def lam(scalar, c=c):
                return scalar.activation(lamr[:, 0:1],
                                         sb_lcum_pre[:, c:c + 1], ACT.Exp)
            step("scalar", lam)

            def lamt(tensor, c=c):
                tensor.transpose(pb[2][:1, 0:64], lamr[:, 0:1],
                                 sb_ident[:64, :64])
                return None
            step("tensor", lamt)

            def lamev(vector, c=c):
                return vector.tensor_copy(f3brow[:1, 0:64], pb[2][:1, 0:64])
            step("vector", lamev)

            def lamb(tensor, c=c):
                tensor.matmul(pb[2][0:64, 64:128], sb_ones_r[:, 0:64],
                              f3brow[:1, 0:64], start=True, stop=True)
                return None
            step("tensor", lamb)

            def lamev2(vector, c=c):
                return vector.tensor_copy(lamr[:, :], pb[2][0:64, 64:128])
            step("vector", lamev2)

            def sfull(vector, c=c, cur=cur):
                last = None
                for hh in range(H):
                    last = vector.scalar_tensor_tensor(
                        sful[:, hh * 64:(hh + 1) * 64],
                        sinit[:, hh * 64:(hh + 1) * 64],
                        lamr[:, hh:hh + 1],
                        spre[cur][:, hh * 64:(hh + 1) * 64],
                        OP.mult, OP.add)
                return last
            step("vector", sfull)

            def trsb(tensor, c=c, cur=cur):
                tensor.transpose(pb[2][:128, 0:64],
                                 sb_lall[:, c * 128:(c + 1) * 128],
                                 sb_ident[:64, :64])
                tensor.transpose(pb[2][:128, 64:128],
                                 sb_dt[:, c * 128:(c + 1) * 128],
                                 sb_ident[:64, :64])
                tensor.matmul(pb[3][:, 0:128], bcBB[cur][:, :],
                              bcCB[cur][:, :], start=True, stop=True)
                return None
            step("tensor", trsb)

            def evb(vector, c=c):
                vector.tensor_copy(f3bld[:, 0:128], pb[2][:, 0:128])
                return vector.tensor_copy(cbt[:, :], pb[3][:, 0:128])
            step("vector", evb)

            def gamm(scalar, c=c):
                return scalar.activation(sb_gam[:, :], f3bld[:, 0:64],
                                         ACT.Exp)
            step("scalar", gamm)

            for g in range(8):
                def ltr(tensor, c=c, g=g):
                    tensor.transpose(pb[2][:8, 0:128],
                                     f3bld[:, g * 8:g * 8 + 8]
                                     if False else f3bld[:, 0:64][:, g * 8:g * 8 + 8],
                                     sb_ident[:, :])
                    return None
                step("tensor", ltr)

                def lev(vector, c=c, g=g):
                    return vector.tensor_copy(rows8[:, :], pb[2][:8, 0:128])
                step("vector", lev)

                def lb(tensor, c=c, g=g):
                    for j in range(8):
                        bank = pb[2 + j // 4]
                        slot = (j % 4) * 128
                        tensor.matmul(bank[:, slot:slot + 128],
                                      sb_sel[:, j * 128:(j + 1) * 128],
                                      rows8[:, :], start=True, stop=True)
                    return None
                step("tensor", lb)

                def mbuild(vector, c=c, g=g):
                    last = None
                    for j in range(8):
                        hh = g * 8 + j
                        bank = pb[2 + j // 4]
                        slot = (j % 4) * 128
                        last = vector.scalar_tensor_tensor(
                            mht[:, j * 128:(j + 1) * 128],
                            bank[:, slot:slot + 128],
                            f3bld[:, hh:hh + 1], sb_mask[:, :],
                            OP.subtract, OP.add)
                    return last
                step("vector", mbuild)

                def mexp(scalar, c=c, g=g):
                    return scalar.activation(mht[:, :], mht[:, :], ACT.Exp)
                step("scalar", mexp)

                def mfin(vector, c=c, g=g):
                    last = None
                    for j in range(8):
                        hh = g * 8 + j
                        last = vector.scalar_tensor_tensor(
                            mht[:, j * 128:(j + 1) * 128],
                            mht[:, j * 128:(j + 1) * 128],
                            f3bld[:, 64 + hh:64 + hh + 1], cbt[:, :],
                            OP.mult, OP.mult)
                    return last
                step("vector", mfin)

                def ymm(tensor, c=c, g=g, cur=cur):
                    for j in range(8):
                        hh = g * 8 + j
                        tensor.matmul(
                            pb[4][:, j * 64:(j + 1) * 64],
                            mht[:, j * 128:(j + 1) * 128],
                            xtmB[cur][:, hh * 64:(hh + 1) * 64],
                            start=True, stop=True)
                        tensor.matmul(
                            pb[5][:, j * 64:(j + 1) * 64],
                            bcCB[cur][:, :],
                            sful[:, hh * 64:(hh + 1) * 64],
                            start=True, stop=True)
                    return None
                step("tensor", ymm)

                def yev(scalar, c=c, g=g):
                    last = None
                    for j in range(8):
                        hh = g * 8 + j
                        last = scalar.activation(
                            yst[:, j * 64:(j + 1) * 64],
                            pb[5][:, j * 64:(j + 1) * 64], ACT.Copy,
                            scale=sb_gam[:, hh:hh + 1])
                    return last
                step("scalar", yev)

                def yev2(vector, c=c, g=g, cur=cur):
                    last = None
                    for j in range(8):
                        hh = g * 8 + j
                        vector.scalar_tensor_tensor(
                            ych[:, hh * 64:(hh + 1) * 64],
                            xtmB[cur][:, hh * 64:(hh + 1) * 64],
                            sb_dbc[:, hh:hh + 1],
                            pb[4][:, j * 64:(j + 1) * 64],
                            OP.mult, OP.add)
                        last = vector.tensor_tensor(
                            ych[:, hh * 64:(hh + 1) * 64],
                            ych[:, hh * 64:(hh + 1) * 64],
                            yst[:, j * 64:(j + 1) * 64], OP.add)
                    return last
                step("vector", yev2)

            transpose_batch([(ych[:, q * 128:(q + 1) * 128],
                              yfm[:, q * 128:(q + 1) * 128])
                             for q in range(32)])

            def gsil(scalar, c=c, cur=cur):
                return scalar.activation(zch[cur][:, :], zch[cur][:, :],
                                         ACT.Silu)
            step("scalar", gsil)

            def gmul(vector, c=c, cur=cur):
                vector.memset(sqacc[:, :], 0.0)
                last = None
                for q in range(32):
                    sl = slice(q * 128, (q + 1) * 128)
                    vector.tensor_tensor(yfm[:, sl], yfm[:, sl],
                                         zch[cur][:, sl], OP.mult)
                    vector.tensor_tensor(yst[:, 0:128], yfm[:, sl],
                                         yfm[:, sl], OP.mult)
                    last = vector.tensor_tensor(sqacc[:, :], sqacc[:, :],
                                                yst[:, 0:128], OP.add)
                return last
            step("vector", gmul)

            def gred(tensor, c=c):
                tensor.matmul(pb[2][:1, 0:128], sb_ones_c[:, 0:1],
                              sqacc[:, :], start=True, stop=True)
                return None
            step("tensor", gred)

            def grr(vector, c=c):
                return vector.tensor_scalar(f3brow[:1, 0:128],
                                            pb[2][:1, 0:128], 1.0 / DIN,
                                            1e-5, OP.mult, OP.add)
            step("vector", grr)

            def grs(scalar, c=c):
                scalar.activation(f3brow[:1, 0:128], f3brow[:1, 0:128],
                                  ACT.Ln)
                return scalar.activation(
                    f3brow[:1, 0:128], f3brow[:1, 0:128], ACT.Exp,
                    scale=sb_cconst[:1, C_NEGHALF:C_NEGHALF + 1])
            step("scalar", grs)

            def grb(tensor, c=c):
                tensor.matmul(pb[2][:, 128:256], sb_ones_r[:, :],
                              f3brow[:1, 0:128], start=True, stop=True)
                return None
            step("tensor", grb)

            def grbe(vector, c=c):
                return vector.tensor_copy(sqacc[:, :], pb[2][:, 128:256])
            step("vector", grbe)

            def yrm(vector, c=c):
                last = None
                for q in range(32):
                    sl = slice(q * 128, (q + 1) * 128)
                    last = vector.scalar_tensor_tensor(
                        yfm[:, sl], yfm[:, sl], sb_normw[:, q:q + 1],
                        sqacc[:, :], OP.mult, OP.mult)
                return last
            step("vector", yrm)

            def styr(gpsimd, c=c):
                return dma(gpsimd, "gpsimd",
                           yr_scr.ap()[c].rearrange("k p q -> p k q"),
                           yfm[:, :].rearrange("p (k q) -> p k q", k=KWO))
            step("gpsimd", styr, dmas=1)

        f3b.close()
        barrier()

        f3c = ExitStack()
        yrb = f3c.enter_context(nc.sbuf_tensor("yrb", [128, KWO * 512], F32))
        wob = [f3c.enter_context(nc.sbuf_tensor(f"wob{j}", [128, KWO * 128],
                                                F32)) for j in range(2)]
        f3cs = [f3c.enter_context(nc.sbuf_tensor(f"f3cs{j}", [128, 512],
                                                 F32)) for j in range(2)]
        f3ct = [f3c.enter_context(nc.sbuf_tensor(f"f3ct{j}", [128, 512],
                                                 F32)) for j in range(2)]

        @block.sync
        def _(sync):
            for f in range(4):
                for m in range(MWO):
                    it = f * MWO + m
                    if it >= 2:
                        sync.wait_ge(sWC, it - 1)
                    sync.dma_start(
                        out=wob[it % 2][:, :].rearrange("p (k q) -> p k q",
                                                        k=KWO),
                        in_=w_out_t.ap()[:, m].rearrange("k p q -> p k q"),
                    ).then_inc(dsem["sync"], 16)
                    dcnt["sync"] += 16
        wc_base = dcnt["sync"] - 16 * 4 * MWO

        NIC = 4 * MWO
        scmm = e(nc.semaphore("scmm"))
        sctr = e(nc.semaphore("sctr"))
        scte = e(nc.semaphore("scte"))
        scld = e(nc.semaphore("scld"))
        gcbase = dcnt["gpsimd"]
        acbase = dcnt["scalar"]

        @block.scalar
        def _(scalar):
            for f in range(4):
                if f >= 1:
                    scalar.wait_ge(scmm, f * MWO)
                for k in range(KWO):
                    scalar.dma_start(
                        out=yrb[:, k * 512:(k + 1) * 512].rearrange(
                            "p (cs q) -> p cs q", cs=4),
                        in_=yr_scr.ap()[4 * f:4 * f + 4, k].rearrange(
                            "cs p q -> p cs q"),
                    ).then_inc(dsem["scalar"], 16)
                    dcnt["scalar"] += 16

        @block.tensor
        def _(tensor):
            def trc(j):
                tensor.wait_ge(sWC, j + 1)
                if j >= 1:
                    tensor.wait_ge(scte, j)
                for q in range(4):
                    tensor.transpose(pb[2][:, q * 128:(q + 1) * 128],
                                     f3cs[j % 2][:, q * 128:(q + 1) * 128],
                                     sb_ident[:, :])
                tensor.drain()
                tensor.nop().then_inc(sctr, 1)

            for it in range(NIC):
                f, m = it // MWO, it % MWO
                if it >= 1:
                    trc(it - 1)
                if m == 0:
                    tensor.wait_ge(dsem["scalar"],
                                   acbase + 16 * KWO * (f + 1))
                tensor.wait_ge(dsem["sync"], wc_base + 16 * (it + 1))
                if it >= 2:
                    tensor.wait_ge(sWC, it - 1)
                last = None
                for k in range(KWO):
                    last = tensor.matmul(
                        pb[it % 2][:, :],
                        wob[it % 2][:, k * 128:(k + 1) * 128],
                        yrb[:, k * 512:(k + 1) * 512],
                        start=(k == 0), stop=(k == KWO - 1))
                last.then_inc(scmm, 1)
            trc(NIC - 1)

        @block.vector
        def _(vector):
            for it in range(NIC):
                vector.wait_ge(scmm, it + 1)
                ins = vector.tensor_copy(f3cs[it % 2][:, :],
                                         pb[it % 2][:, :])
                ins.then_inc(sWC, 1)
                if it >= 1:
                    j = it - 1
                    vector.wait_ge(sctr, j + 1)
                    if j >= 2:
                        vector.wait_ge(dsem["gpsimd"], gcbase + 16 * (j - 1))
                    vector.tensor_copy(f3ct[j % 2][:, :],
                                       pb[2][:, 0:512]).then_inc(scte, 1)
            j = NIC - 1
            vector.wait_ge(sctr, j + 1)
            vector.wait_ge(dsem["gpsimd"], gcbase + 16 * (j - 1))
            vector.tensor_copy(f3ct[j % 2][:, :],
                               pb[2][:, 0:512]).then_inc(scte, 1)

        @block.gpsimd
        def _(gpsimd):
            for it in range(NIC):
                f, m = it // MWO, it % MWO
                gpsimd.wait_ge(scte, it + 1)
                gpsimd.dma_start(
                    out=s_scr.ap()[f * 4:(f + 1) * 4, :,
                                   m * 128:(m + 1) * 128]
                    .rearrange("c p q -> p c q"),
                    in_=f3ct[it % 2][:, :].rearrange("p (c q) -> p c q", c=4),
                ).then_inc(dsem["gpsimd"], 16)
                dcnt["gpsimd"] += 16
            gpsimd.wait_ge(dsem["gpsimd"], dcnt["gpsimd"])

        prev[0] = None
        f3c.close()
        barrier()

        f5p = ExitStack()
        wdp = [f5p.enter_context(nc.sbuf_tensor(f"wdp{j}", [128, MDN * 128],
                                                F32)) for j in range(2)]
        qwp = f5p.enter_context(nc.sbuf_tensor("qwp", [128, MDN * 128], BF16))
        f5pc = f5p.enter_context(nc.sbuf_tensor("f5pc", [128, 512], F32))
        f5pw = f5p.enter_context(nc.sbuf_tensor("f5pw", [128, 512], F32))

        @block.sync
        def _(sync):
            for k in range(KDN):
                if k >= 2:
                    sync.wait_ge(sWP, k - 1)
                sync.dma_start(
                    out=wdp[k % 2][:, :].rearrange("p (m q) -> p m q", m=MDN),
                    in_=w_down_t.ap()[k].rearrange("m p q -> p m q"),
                ).then_inc(dsem["sync"], 16)
                dcnt["sync"] += 16
        wp_base = dcnt["sync"] - 16 * KDN

        for k in range(KDN):
            def qd(vector, k=k):
                vector.wait_ge(dsem["sync"], wp_base + 16 * (k + 1))
                last = None
                for blk in range(4):
                    sl = slice(blk * 512, (blk + 1) * 512)
                    vector.tensor_scalar(f5pw[:, :], wdp[k % 2][:, sl],
                                         sb_swinv[:, 2:3], None, OP.mult)
                    vector.tensor_scalar(f5pc[:, :], f5pw[:, :], -0.5,
                                         None, OP.is_le)
                    last = vector.scalar_tensor_tensor(
                        qwp[:, sl], f5pw[:, :], 0.5, f5pc[:, :],
                        OP.is_ge, OP.subtract)
                last.then_inc(sWP, 1)
                vector.drain()
                return None
            step("vector", qd)

            def stq(gpsimd, k=k):
                return dma(gpsimd, "gpsimd",
                           qwd_scr.ap()[k].rearrange("m p q -> p m q"),
                           qwp[:, :].rearrange("p (m q) -> p m q", m=MDN))
            step("gpsimd", stq, dmas=1)

        f5p.close()
        barrier()

        f5a = ExitStack()
        hch = [f5a.enter_context(nc.sbuf_tensor("hch0", [128, IFF], F32))]
        sb_lnw = f5a.enter_context(nc.sbuf_tensor("sb_lnw", [128, IFF], F32))
        f5w = f5a.enter_context(nc.sbuf_tensor("f5w", [128, IFF], F32))
        f5v = f5a.enter_context(nc.sbuf_tensor("f5v", [128, IFF], F32))
        f5h = f5a.enter_context(nc.sbuf_tensor("f5h", [128, IFF], F16))
        f5qs = f5a.enter_context(nc.sbuf_tensor("f5qs", [128, IFF], BF16))
        f5c = f5a.enter_context(nc.sbuf_tensor("f5c", [128, 8], F32))

        def pf50(sync):
            return dma(sync, "sync", sb_lnw[:, :], lnw_bc.ap())
        step("sync", pf50, dmas=1)

        for c in range(NT):
            cur = 0
            def pf5(sync, c=c):
                return dma(sync, "sync", hch[0][:, :], h_scr.ap()[c])
            step("sync", pf5, dmas=1)

            def rms(scalar, cur=cur):
                return scalar.activation(f5w[:, :], hch[cur][:, :],
                                         ACT.Square, accum_out=f5c[:, 0:1])
            step("scalar", rms)

            def rr(vector, cur=cur):
                return vector.tensor_scalar(f5c[:, 1:2], f5c[:, 0:1],
                                            1.0 / IFF, 1e-6, OP.mult, OP.add)
            step("vector", rr)

            def rr2(scalar):
                scalar.activation(f5c[:, 2:3], f5c[:, 1:2], ACT.Ln)
                return scalar.activation(
                    f5c[:, 3:4], f5c[:, 2:3], ACT.Exp,
                    scale=sb_cconst[:, C_NEGHALF:C_NEGHALF + 1])
            step("scalar", rr2)

            def h2(vector, cur=cur):
                vector.tensor_scalar(f5w[:, :], hch[cur][:, :], f5c[:, 3:4],
                                     None, OP.mult)
                last = None
                for q in range(4):
                    sl = slice(q * 2048, (q + 1) * 2048)
                    last = vector.tensor_tensor(f5w[:, sl], f5w[:, sl],
                                                sb_lnw[:, sl], OP.mult)
                return last
            step("vector", h2)

            def qs(vector, c=c):
                ins = vector.reduce_max(f5c[:, 4:5], f5w[:, :], AX,
                                        apply_absolute_value=True)
                sfence(vector, "vector", ins)
                vector.tensor_scalar(f5c[:, 4:5], f5c[:, 4:5], 1e-5, None,
                                     OP.max)
                ins = vector.reciprocal(f5c[:, 5:6], f5c[:, 4:5])
                sfence(vector, "vector", ins)
                vector.tensor_scalar(f5c[:, 5:6], f5c[:, 5:6], 127.0, None,
                                     OP.mult)
                ins = vector.tensor_scalar(sb_s2c[:, c:c + 1], f5c[:, 4:5],
                                            1.0 / 127.0, None, OP.mult)
                sfence(vector, "vector", ins)
                vector.tensor_scalar(f5v[:, :], f5w[:, :], f5c[:, 5:6],
                                     None, OP.mult)
                vector.tensor_scalar(f5h[:, :], f5v[:, :], 1536.0, None,
                                     OP.add)
                vector.tensor_scalar(f5w[:, :], f5h[:, :], 1536.0, None,
                                     OP.subtract)
                vector.tensor_tensor(hch[0][:, :], f5v[:, :], f5w[:, :],
                                     OP.subtract)
                vector.scalar_tensor_tensor(f5v[:, :], hch[0][:, :], 0.5,
                                            f5w[:, :], OP.is_gt, OP.add)
                vector.scalar_tensor_tensor(f5w[:, :], hch[0][:, :], -0.5,
                                            f5v[:, :], OP.is_ge, OP.add)
                return vector.tensor_scalar(f5v[:, :], f5w[:, :], 1.0, None,
                                            OP.subtract)
            step("vector", qs)

            transpose_batch([(f5v[:, k * 128:(k + 1) * 128],
                              f5qs[:, k * 128:(k + 1) * 128])
                             for k in range(KDN)])

            def stq5(gpsimd, c=c):
                return dma(
                    gpsimd, "gpsimd",
                    qh_scr.ap()[:, :, c * 128:(c + 1) * 128].rearrange(
                        "k p q -> p k q"),
                    f5qs[:, :].rearrange("p (k q) -> p k q", k=KDN))
            step("gpsimd", stq5, dmas=1)

        f5a.close()
        barrier()

        f5b = ExitStack()
        qhb = f5b.enter_context(nc.sbuf_tensor("qhb", [128, KDN * 512], BF16))
        qwdb = [f5b.enter_context(nc.sbuf_tensor(f"qwdb{j}",
                                                 [128, KDN * 128], BF16))
                for j in range(2)]
        s2bc = f5b.enter_context(nc.sbuf_tensor("s2bc", [128, R], F32))
        f5d = [f5b.enter_context(nc.sbuf_tensor(f"f5d{j}", [128, 512], F32))
               for j in range(2)]
        gtm = f5b.enter_context(nc.sbuf_tensor("gtm", [128, 4 * 2048], F32))
        xr2 = [f5b.enter_context(nc.sbuf_tensor(f"xr2{j}", [128, 2048], F32))
               for j in range(2)]
        sr2 = [f5b.enter_context(nc.sbuf_tensor(f"sr2{j}", [128, 2048], F32))
               for j in range(2)]
        f5row = f5b.enter_context(nc.sbuf_tensor("f5row", [1, 128], F32))

        @block.sync
        def _(sync):
            for f in range(4):
                for m in range(MDN):
                    it = f * MDN + m
                    if it >= 2:
                        sync.wait_ge(sWD, it - 1)
                    sync.dma_start(
                        out=qwdb[it % 2][:, :].rearrange("p (k q) -> p k q",
                                                         k=KDN),
                        in_=qwd_scr.ap()[:, m].rearrange("k p q -> p k q"),
                    ).then_inc(dsem["sync"], 16)
                    dcnt["sync"] += 16
        wd_base = dcnt["sync"] - 16 * 4 * MDN

        for i in range(NT):
            bcast_col(sb_s2c[:, i:i + 1], s2bc[:, i * 128:(i + 1) * 128],
                      f5row)

        NID = 4 * MDN
        sdmm = e(nc.semaphore("sdmm"))
        sdep = e(nc.semaphore("sdep"))
        sdsi = e(nc.semaphore("sdsi"))
        sdtr = e(nc.semaphore("sdtr"))
        sdte = e(nc.semaphore("sdte"))
        sdrs = e(nc.semaphore("sdrs"))
        adbase = dcnt["scalar"]

        @block.scalar
        def _(scalar):
            for f in range(4):
                if f >= 1:
                    scalar.wait_ge(sdmm, f * MDN)
                scalar.dma_start(
                    out=qhb[:, :].rearrange("p (k q) -> p k q", k=KDN),
                    in_=qh_scr.ap()[:, :, f * 512:(f + 1) * 512]
                    .rearrange("k p q -> p k q"),
                ).then_inc(dsem["scalar"], 16)
                dcnt["scalar"] += 16
                for it in range(f * MDN, (f + 1) * MDN):
                    scalar.wait_ge(sdep, it + 1)
                    scalar.activation(f5d[it % 2][:, :], f5d[it % 2][:, :],
                                      ACT.Silu).then_inc(sdsi, 1)
                for q in range(4):
                    r = f * 4 + q
                    cch = f * 4 + q
                    if r >= 2:
                        scalar.wait_ge(sOut, 16 * (r - 1))
                    scalar.dma_start(
                        out=xr2[q % 2][:, :],
                        in_=x_own.ap()[cch * 128:(cch + 1) * 128, :],
                    ).then_inc(dsem["scalar"], 16)
                    scalar.dma_start(
                        out=sr2[q % 2][:, :],
                        in_=s_scr.ap()[cch]).then_inc(dsem["scalar"], 16)
                    dcnt["scalar"] += 32

        @block.tensor
        def _(tensor):
            def trd(j):
                tensor.wait_ge(sdsi, j + 1)
                if j >= 1:
                    tensor.wait_ge(sdte, j)
                for q in range(4):
                    tensor.transpose(pb[2][:, q * 128:(q + 1) * 128],
                                     f5d[j % 2][:, q * 128:(q + 1) * 128],
                                     sb_ident[:, :])
                tensor.drain()
                tensor.nop().then_inc(sdtr, 1)

            for it in range(NID):
                f, m = it // MDN, it % MDN
                if it >= 1:
                    trd(it - 1)
                if m == 0:
                    tensor.wait_ge(dsem["scalar"], adbase + 144 * f + 16)
                tensor.wait_ge(dsem["sync"], wd_base + 16 * (it + 1))
                if it >= 2:
                    tensor.wait_ge(sdep, it - 1)
                last = None
                for k in range(KDN):
                    last = tensor.matmul(
                        pb[it % 2][:, :],
                        qwdb[it % 2][:, k * 128:(k + 1) * 128],
                        qhb[:, k * 512:(k + 1) * 512],
                        start=(k == 0), stop=(k == KDN - 1))
                last.then_inc(sdmm, 1)
            trd(NID - 1)

        @block.vector
        def _(vector):
            def tev(j):
                jm = j % MDN
                vector.wait_ge(sdtr, j + 1)
                last = None
                for q in range(4):
                    last = vector.tensor_copy(
                        gtm[:, q * 2048 + jm * 128:q * 2048 + (jm + 1) * 128],
                        pb[2][:, q * 128:(q + 1) * 128])
                last.then_inc(sdte, 1)
                if (j + 1) % MDN == 0:
                    fdone = j // MDN
                    for q in range(4):
                        r = fdone * 4 + q
                        vector.wait_ge(dsem["scalar"],
                                       adbase + 144 * fdone + 48 + 32 * q)
                        vector.tensor_tensor(
                            sr2[q % 2][:, :], sr2[q % 2][:, :],
                            gtm[:, q * 2048:(q + 1) * 2048], OP.mult)
                        vector.tensor_tensor(
                            sr2[q % 2][:, :], xr2[q % 2][:, :],
                            sr2[q % 2][:, :], OP.add).then_inc(sdrs, 1)

            for it in range(NID):
                f = it // MDN
                if it >= 1:
                    tev(it - 1)
                vector.wait_ge(sdmm, it + 1)
                ins = vector.tensor_tensor(
                    f5d[it % 2][:, :], pb[it % 2][:, :],
                    s2bc[:, f * 512:f * 512 + 512], OP.mult)
                ins.then_inc(sWD, 1)
                vector.tensor_scalar(f5d[it % 2][:, :], f5d[it % 2][:, :],
                                     sb_wsum[:, 2:3], None,
                                     OP.mult).then_inc(sdep, 1)
            tev(NID - 1)

        @block.gpsimd
        def _(gpsimd):
            for r in range(16):
                cch = r
                gpsimd.wait_ge(sdrs, r + 1)
                gpsimd.dma_start(
                    out=out.ap()[cch * 128:(cch + 1) * 128, :],
                    in_=sr2[r % 2][:, :]).then_inc(sOut, 16)

        prev[0] = None
        f5b.close()

        @block.gpsimd
        def _(gpsimd):
            gpsimd.wait_ge(sOut, 16 * NT)

        @block.sync
        def _(sync):
            sync.wait_ge(sOut, 16 * NT)

    return nc


_NC_CACHE = [None]


def kernel(**inputs):
    import ml_dtypes

    def tile4(w):
        K, M = w.shape[0] // 128, w.shape[1] // 128
        return np.ascontiguousarray(
            w.reshape(K, 128, M, 128).transpose(0, 2, 1, 3))

    x = np.asarray(inputs["L0_attn_x"], np.float32)
    w_in = np.asarray(inputs["L0_attn_W_in"], np.float32)
    conv_w = np.asarray(inputs["L0_attn_conv_w"], np.float32)
    conv_b = np.asarray(inputs["L0_attn_conv_b"], np.float32)
    dt_bias = np.asarray(inputs["L0_attn_dt_bias"], np.float32)
    a_log = np.asarray(inputs["L0_attn_A_log"], np.float32)
    d_vec = np.asarray(inputs["L0_attn_D"], np.float32)
    norm_w = np.asarray(inputs["L0_attn_norm_w"], np.float32)
    w_out = np.asarray(inputs["L0_attn_W_out"], np.float32)
    w_gate = np.asarray(inputs["L0_ffn_w_gate"], np.float32)
    w_up = np.asarray(inputs["L0_ffn_w_up"], np.float32)
    w_down = np.asarray(inputs["L0_ffn_w_down"], np.float32)
    ln_w = np.asarray(inputs["L0_ffn_ln_weight"], np.float32)

    w_in_t = np.zeros((DM, M_IN * 128), np.float32)
    w_in_t[:, :8384] = w_in.T
    w_in_t = tile4(w_in_t)
    w_out_t = tile4(np.ascontiguousarray(w_out.T))
    w_gate_t = tile4(np.ascontiguousarray(w_gate.T))
    w_up_t = tile4(np.ascontiguousarray(w_up.T))
    w_down_t = tile4(np.ascontiguousarray(w_down.T))

    cw_cols = np.ascontiguousarray(
        conv_w.reshape(33, 128, 4).transpose(1, 0, 2).reshape(128, 132))
    cb_cols = np.ascontiguousarray(conv_b.reshape(33, 128).T)
    normw_cols = np.ascontiguousarray(norm_w.reshape(KWO, 128).T)
    lnw_bcast = np.ascontiguousarray(np.broadcast_to(ln_w, (128, IFF)))
    d_bcast = np.ascontiguousarray(np.broadcast_to(d_vec, (128, H)))
    identity = np.eye(128, dtype=np.float32)
    identityb = np.eye(128, dtype=ml_dtypes.bfloat16)
    mneg = np.where(np.arange(128)[None, :] >= np.arange(128)[:, None],
                    0.0, -1e9).astype(np.float32)
    ones_row = np.ones((1, 128), np.float32)
    ones_col = np.ones((128, 1), np.float32)
    sel = np.zeros((8, 8, 128), np.float32)
    for j in range(8):
        sel[j, j, :] = 1.0
    cconsts = np.zeros((128, 4), np.float32)
    cconsts[:, C_NEGHALF] = -0.5

    common = dict(
        w_in_t=w_in_t, w_out_t=w_out_t, w_gate_t=w_gate_t, w_up_t=w_up_t,
        w_down_t=w_down_t, cw_cols=cw_cols, cb_cols=cb_cols,
        normw_cols=normw_cols, lnw_bc=lnw_bcast,
        dtb_col=dt_bias.reshape(64, 1), alog_col=a_log.reshape(64, 1),
        d_bc=d_bcast, ident=identity, identb=identityb, maskneg=mneg,
        ones_r=ones_row, ones_c=ones_col, selmat=sel, cconst=cconsts)

    in_maps = []
    for c in range(NCORE):
        b, half = c // 2, c % 2
        x_own_c = np.ascontiguousarray(x[b, half * R:(half + 1) * R])
        if half == 0:
            x_halo_c = np.zeros((3, DM), np.float32)
        else:
            x_halo_c = np.ascontiguousarray(x[b, R - 3:R])
        smask_c = np.full((64, 1), 1.0 if half == 1 else 0.0, np.float32)
        in_maps.append(dict(common, x_own=x_own_c, x_halo=x_halo_c,
                            smask=smask_c))

    if _NC_CACHE[0] is None:
        _NC_CACHE[0] = build_program()
    res = run_bass_kernel_spmd(_NC_CACHE[0], in_maps,
                               core_ids=list(range(NCORE)))

    full = np.empty((B, T, DM), np.float32)
    for c in range(NCORE):
        b, half = c // 2, c % 2
        full[b, half * R:(half + 1) * R] = np.asarray(res.results[c]["out"])
    return full


# version 07
# speedup vs baseline: 1.0285x; 1.0285x over previous
import numpy as np
from contextlib import ExitStack

import concourse.bass as bass
import concourse.mybir as mybir
from concourse.bass_utils import run_bass_kernel_spmd

F32 = mybir.dt.float32
F16 = mybir.dt.float16
BF16 = mybir.dt.bfloat16
AX = mybir.AxisListType.X
OP = mybir.AluOpType
ACT = mybir.ActivationFunctionType

B, T, DM = 4, 4096, 2048
R = 2048
NT = 16
RH = R + 3
DIN = 4096
H = 64
KD = 16
M_IN = 66
IFF = 8192
MFF = 64
KDN = 64
MDN = 16
KWO = 32
MWO = 16
NCORE = 8
FB2 = [(0, 512), (512, 512), (1024, 512), (1536, 512), (2048, 3)]

C_NEGHALF = 0

DBG = False


def build_program():
    nc = bass.Bass("TRN2", target_bir_lowering=False,
                   detect_race_conditions=False)

    def dram_in(name, shape, dtype=F32):
        return nc.dram_tensor(name, list(shape), dtype, kind="ExternalInput")

    def dram_scr(name, shape, dtype=F32):
        if DBG:
            return nc.dram_tensor(name, list(shape), dtype,
                                  kind="ExternalOutput")
        return nc.dram_tensor(name, list(shape), dtype)

    x_own = dram_in("x_own", [R, DM])
    x_halo = dram_in("x_halo", [3, DM])
    w_in_t = dram_in("w_in_t", [KD, M_IN, 128, 128])
    w_out_t = dram_in("w_out_t", [KWO, MWO, 128, 128])
    w_gate_t = dram_in("w_gate_t", [KD, MFF, 128, 128])
    w_up_t = dram_in("w_up_t", [KD, MFF, 128, 128])
    w_down_t = dram_in("w_down_t", [KDN, MDN, 128, 128])
    cw_cols = dram_in("cw_cols", [128, 33 * 4])
    cb_cols = dram_in("cb_cols", [128, 33])
    normw_cols = dram_in("normw_cols", [128, KWO])
    lnw_bc = dram_in("lnw_bc", [128, IFF])
    dtb_col = dram_in("dtb_col", [64, 1])
    alog_col = dram_in("alog_col", [64, 1])
    d_bc = dram_in("d_bc", [128, H])
    smask = dram_in("smask", [64, 1])
    ident = dram_in("ident", [128, 128])
    identb = dram_in("identb", [128, 128], BF16)
    maskneg = dram_in("maskneg", [128, 128])
    ones_r = dram_in("ones_r", [1, 128])
    ones_c = dram_in("ones_c", [128, 1])
    selmat = dram_in("selmat", [8, 8, 128])
    cconst = dram_in("cconst", [128, 4])

    out = nc.dram_tensor("out", [R, DM], F32, kind="ExternalOutput")

    z_scr = dram_scr("z_scr", [32, 128, R])
    xh_scr = dram_scr("xh_scr", [NT, 128, DIN])
    bc_scr = dram_scr("bc_scr", [NT, 128, 128])
    qx_scr = dram_scr("qx_scr", [KD, 128, R], BF16)
    h_scr = dram_scr("h_scr", [NT, 128, IFF])
    qh_scr = dram_scr("qh_scr", [KDN, 128, R], BF16)
    qwd_scr = dram_scr("qwd_scr", [KDN, MDN, 128, 128], BF16)
    yr_scr = dram_scr("yr_scr", [NT, KWO, 128, 128])
    s_scr = dram_scr("s_scr", [NT, 128, DM])
    sloc_scr = nc.dram_tensor("sloc_scr", [NT, 64, DIN], F32)
    sloc_dbg = (nc.dram_tensor("sloc_dbg", [NT, 64, DIN], F32,
                               kind="ExternalOutput") if DBG else None)
    cc_out = nc.dram_tensor("cc_out", [128, DIN], F32)

    ctx = ExitStack()
    with ctx:
        e = ctx.enter_context

        sb_ident = e(nc.sbuf_tensor("sb_ident", [128, 128], F32))
        sb_identb = e(nc.sbuf_tensor("sb_identb", [128, 128], BF16))
        sb_ones_r = e(nc.sbuf_tensor("sb_ones_r", [1, 128], F32))
        sb_ones_c = e(nc.sbuf_tensor("sb_ones_c", [128, 1], F32))
        sb_cconst = e(nc.sbuf_tensor("sb_cconst", [128, 4], F32))
        sb_cw = e(nc.sbuf_tensor("sb_cw", [128, 33 * 4], F32))
        sb_cb = e(nc.sbuf_tensor("sb_cb", [128, 33], F32))
        sb_normw = e(nc.sbuf_tensor("sb_normw", [128, KWO], F32))
        sb_dtb = e(nc.sbuf_tensor("sb_dtb", [64, 1], F32))
        sb_alog = e(nc.sbuf_tensor("sb_alog", [64, 1], F32))
        sb_dbc = e(nc.sbuf_tensor("sb_dbc", [128, H], F32))
        sb_smask = e(nc.sbuf_tensor("sb_smask", [64, 1], F32))
        sb_acol = e(nc.sbuf_tensor("sb_acol", [64, 1], F32))
        sb_dt = e(nc.sbuf_tensor("sb_dt", [64, R], F32))
        sb_lall = e(nc.sbuf_tensor("sb_lall", [64, R], F32))
        sb_lat = e(nc.sbuf_tensor("sb_lat", [64, 128], F32))
        sb_zero64 = e(nc.sbuf_tensor("sb_zero64", [64, 128], F32))
        sb_lcum = e(nc.sbuf_tensor("sb_lcum", [64, 2], F32))
        sb_lcum_pre = e(nc.sbuf_tensor("sb_lcum_pre", [64, NT], F32))
        sb_sxc = e(nc.sbuf_tensor("sb_sxc", [128, NT], F32))
        sb_s2c = e(nc.sbuf_tensor("sb_s2c", [128, NT], F32))
        sb_swinv = e(nc.sbuf_tensor("sb_swinv", [128, 4], F32))
        sb_wsum = e(nc.sbuf_tensor("sb_wsum", [128, 4], F32))
        sb_gam = e(nc.sbuf_tensor("sb_gam", [128, 64], F32))

        pb = [e(nc.psum_tensor(f"pb{i}", [128, 512], F32)) for i in range(6)]
        pbh = e(nc.psum_tensor("pbh", [128, 1024], BF16))

        block = e(nc.Block())

        ENGS = ("sync", "vector", "scalar", "tensor", "gpsimd")
        esem = {n: e(nc.semaphore(f"se_{n}")) for n in ENGS}
        ecnt = {n: 0 for n in ENGS}
        dsem = {n: e(nc.semaphore(f"sd_{n}")) for n in
                ("sync", "scalar", "gpsimd")}
        dcnt = {n: 0 for n in ("sync", "scalar", "gpsimd")}
        sB = e(nc.semaphore("sBar"))
        sC = e(nc.semaphore("sColl"))
        sW2 = e(nc.semaphore("sW2"))
        sW4 = e(nc.semaphore("sW4"))
        sWC = e(nc.semaphore("sWC"))
        sWP = e(nc.semaphore("sWP"))
        sWD = e(nc.semaphore("sWD"))
        sOut = e(nc.semaphore("sOut"))
        nbar = [0]
        prev = [None]

        def step(ename, fn, dmas=0):
            @getattr(block, ename)
            def _(eng):
                if prev[0] is not None:
                    sem, val = prev[0]
                    eng.wait_ge(sem, val)
                last = fn(eng)
                if dmas:
                    dcnt[ename] += 16 * dmas
                    prev[0] = (dsem[ename], dcnt[ename])
                else:
                    if last is None:
                        eng.drain()
                        last = eng.nop()
                    ecnt[ename] += 1
                    last.then_inc(esem[ename], 1)
                    prev[0] = (esem[ename], ecnt[ename])

        def sfence(eng, ename, ins):
            ecnt[ename] += 1
            ins.then_inc(esem[ename], 1)
            eng.wait_ge(esem[ename], ecnt[ename])

        def dma(eng, ename, out_ap, in_ap):
            ins = eng.dma_start(out=out_ap, in_=in_ap)
            ins.then_inc(dsem[ename], 16)
            return ins

        def barrier():
            nbar[0] += 1
            tgt = 5 * nbar[0]
            for name in ENGS:
                @getattr(block, name)
                def _(eng, tgt=tgt):
                    if prev[0] is not None:
                        sem, val = prev[0]
                        eng.wait_ge(sem, val)
                    eng.drain()
                    eng.sem_inc(sB, 1)
                    eng.wait_ge(sB, tgt)
            prev[0] = None

        def transpose_batch(pairs, bf=False):
            W = 8
            for w0 in range(0, len(pairs), W):
                wave = pairs[w0:w0 + W]

                def tr(tensor, wave=wave):
                    for j, (src, dst) in enumerate(wave):
                        P, Q = src.shape
                        if bf:
                            tensor.transpose(pbh[:Q, j * 128:j * 128 + P],
                                             src, sb_identb[:P, :P])
                        else:
                            bank = pb[2 + j // 4]
                            slot = (j % 4) * 128
                            tensor.transpose(bank[:Q, slot:slot + P], src,
                                             sb_ident[:P, :P])
                    return None
                step("tensor", tr)

                def ev(vector, wave=wave):
                    last = None
                    for j, (src, dst) in enumerate(wave):
                        P, Q = src.shape
                        if bf:
                            last = vector.tensor_copy(
                                dst, pbh[:Q, j * 128:j * 128 + P])
                        else:
                            bank = pb[2 + j // 4]
                            slot = (j % 4) * 128
                            last = vector.tensor_copy(
                                dst, bank[:Q, slot:slot + P])
                    return last
                step("vector", ev)

        def bcast_col(col_ap, dst_ap, tmprow, P=128):
            def t1(tensor):
                tensor.transpose(pb[2][:1, 0:128], col_ap, sb_ident[:, :])
                return None
            step("tensor", t1)

            def v1(vector):
                return vector.tensor_copy(tmprow[:1, 0:128], pb[2][:1, 0:128])
            step("vector", v1)

            def t2(tensor, P=P):
                tensor.matmul(pb[3][:P, 0:128], sb_ones_r[:, :P],
                              tmprow[:1, 0:128], start=True, stop=True)
                return None
            step("tensor", t2)

            def v2(vector, P=P):
                return vector.tensor_copy(dst_ap, pb[3][:P, 0:128])
            step("vector", v2)

        def p0(sync):
            last = None
            for dst, src in [
                    (sb_ident[:], ident.ap()), (sb_identb[:], identb.ap()),
                    (sb_ones_r[:], ones_r.ap()),
                    (sb_ones_c[:], ones_c.ap()), (sb_cconst[:], cconst.ap()),
                    (sb_cw[:], cw_cols.ap()), (sb_cb[:], cb_cols.ap()),
                    (sb_normw[:], normw_cols.ap()), (sb_dtb[:], dtb_col.ap()),
                    (sb_alog[:], alog_col.ap()), (sb_dbc[:], d_bc.ap()),
                    (sb_smask[:], smask.ap())]:
                last = dma(sync, "sync", dst, src)
            return last
        step("sync", p0, dmas=12)

        def p0v(vector):
            vector.memset(sb_zero64[:], 0.0)
            vector.memset(sb_lcum[:], 0.0)
            return vector.memset(sb_wsum[:], 0.0)
        step("vector", p0v)

        def p0s(scalar):
            return scalar.activation(sb_acol[:], sb_alog[:], ACT.Exp)
        step("scalar", p0s)

        def p0v2(vector):
            return vector.tensor_scalar(sb_acol[:], sb_acol[:], -1.0, None,
                                        OP.mult)
        step("vector", p0v2)

        f12 = ExitStack()
        xnfm = [f12.enter_context(nc.sbuf_tensor(f"xnfm{k}", [128, RH], F32))
                for k in range(KD)]
        f1 = ExitStack()
        xtile = f1.enter_context(nc.sbuf_tensor("xtile", [128, DM], F32))
        f1w = f1.enter_context(nc.sbuf_tensor("f1w", [128, DM], F32))
        f1q = f1.enter_context(nc.sbuf_tensor("f1q", [128, DM], BF16))
        f1h = f1.enter_context(nc.sbuf_tensor("f1h", [128, DM], F16))
        f1qs = f1.enter_context(nc.sbuf_tensor("f1qs", [128, DM], BF16))
        f1v = f1.enter_context(nc.sbuf_tensor("f1v", [128, DM], F32))
        f1r = f1.enter_context(nc.sbuf_tensor("f1r", [128, DM], F32))
        f1c = f1.enter_context(nc.sbuf_tensor("f1c", [128, 8], F32))

        for i in range(NT + 1):
            P = 128 if i < NT else 3

            def ld(scalar, i=i, P=P):
                src = (x_own.ap()[i * 128:(i + 1) * 128, :] if i < NT
                       else x_halo.ap()[:, :])
                return dma(scalar, "scalar", xtile[:P, :], src)
            step("scalar", ld, dmas=1)

            def v1(vector, P=P):
                ins = vector.reduce_sum(f1c[:P, 0:1], xtile[:P, :], AX,
                                        negate=True)
                sfence(vector, "vector", ins)
                ins = vector.tensor_scalar(f1c[:P, 1:2], f1c[:P, 0:1],
                                           1.0 / DM, None, OP.mult)
                sfence(vector, "vector", ins)
                return vector.tensor_scalar(f1w[:P, :], xtile[:P, :],
                                            f1c[:P, 1:2], None, OP.add)
            step("vector", v1)

            def s1(scalar, P=P):
                return scalar.activation(xtile[:P, :], f1w[:P, :], ACT.Square,
                                         accum_out=f1c[:P, 2:3])
            step("scalar", s1)

            def v2(vector, P=P):
                return vector.tensor_scalar(f1c[:P, 3:4], f1c[:P, 2:3],
                                            1.0 / DM, 1e-5, OP.mult, OP.add)
            step("vector", v2)

            def s2(scalar, P=P):
                scalar.activation(f1c[:P, 4:5], f1c[:P, 3:4], ACT.Ln)
                return scalar.activation(
                    f1c[:P, 5:6], f1c[:P, 4:5], ACT.Exp,
                    scale=sb_cconst[:P, C_NEGHALF:C_NEGHALF + 1])
            step("scalar", s2)

            def v3(vector, i=i, P=P):
                vector.tensor_scalar(f1w[:P, :], f1w[:P, :], f1c[:P, 5:6],
                                     None, OP.mult)
                if P == 128:
                    ins = vector.reduce_max(f1c[:P, 6:7], f1w[:P, :], AX,
                                            apply_absolute_value=True)
                    sfence(vector, "vector", ins)
                    vector.tensor_scalar(f1c[:P, 6:7], f1c[:P, 6:7], 1e-5,
                                         None, OP.max)
                    ins = vector.reciprocal(f1c[:P, 7:8], f1c[:P, 6:7])
                    sfence(vector, "vector", ins)
                    vector.tensor_scalar(f1c[:P, 7:8], f1c[:P, 7:8], 127.0,
                                         None, OP.mult)
                    ins = vector.tensor_scalar(sb_sxc[:P, i:i + 1],
                                               f1c[:P, 6:7], 1.0 / 127.0,
                                               None, OP.mult)
                    sfence(vector, "vector", ins)
                    vector.tensor_scalar(f1v[:P, :], f1w[:P, :], f1c[:P, 7:8],
                                         None, OP.mult)
                    vector.tensor_scalar(f1h[:P, :], f1v[:P, :], 1536.0,
                                         None, OP.add)
                    vector.tensor_scalar(f1r[:P, :], f1h[:P, :], 1536.0,
                                         None, OP.subtract)
                    vector.tensor_tensor(xtile[:P, :], f1v[:P, :],
                                         f1r[:P, :], OP.subtract)
                    vector.scalar_tensor_tensor(f1v[:P, :], xtile[:P, :], 0.5,
                                                f1r[:P, :], OP.is_gt, OP.add)
                    vector.scalar_tensor_tensor(f1r[:P, :], xtile[:P, :],
                                                -0.5, f1v[:P, :], OP.is_ge,
                                                OP.add)
                    return vector.tensor_scalar(f1q[:P, :], f1r[:P, :], 1.0,
                                                None, OP.subtract)
                return None
            step("vector", v3)

            col = 3 + i * 128 if i < NT else 0
            transpose_batch([(f1w[:P, k * 128:(k + 1) * 128],
                              xnfm[k][:, col:col + P]) for k in range(KD)])

            if i < NT:
                transpose_batch([(f1q[:, k * 128:(k + 1) * 128],
                                  f1qs[:, k * 128:(k + 1) * 128])
                                 for k in range(KD)], bf=True)

                def st(gpsimd, i=i):
                    return dma(
                        gpsimd, "gpsimd",
                        qx_scr.ap()[:, :, i * 128:(i + 1) * 128].rearrange(
                            "k p q -> p k q"),
                        f1qs[:, :].rearrange("p (k q) -> p k q", k=KD))
                step("gpsimd", st, dmas=1)

        f1.close()
        barrier()

        f2 = ExitStack()
        wbuf = [f2.enter_context(nc.sbuf_tensor(f"wbuf{j}", [128, KD * 128],
                                                F32)) for j in range(2)]
        f2s0 = f2.enter_context(nc.sbuf_tensor("f2s0", [128, RH], F32))
        f2s = [f2s0, f2s0]
        f2c = [f2.enter_context(nc.sbuf_tensor(f"f2c{j}", [128, R], F32))
               for j in range(2)]
        f2t = f2.enter_context(nc.sbuf_tensor("f2t", [128, 2048], F32))
        NPAW = 4
        paw = f2.enter_context(nc.sbuf_tensor("paw", [128, NPAW * 128], F32))

        pa_srcs = []
        for wi, wt in enumerate((w_gate_t, w_up_t, w_down_t)):
            flat = wt.ap().rearrange("a b p q -> (a b) p q")
            ntile = KD * MFF if wi < 2 else KDN * MDN
            for j in range(0, ntile, NPAW):
                pa_srcs.append((wi, flat[j:j + NPAW]))

        FB2Z = [(3, 512), (515, 512), (1027, 512), (1539, 512)]

        def f2_blocks(m):
            return FB2 if 32 <= m <= 64 else FB2Z
        mmcum = [0]
        for _m in range(M_IN):
            mmcum.append(mmcum[-1] + len(f2_blocks(_m)))
        bank_of = {}
        for _m in range(M_IN):
            _rot = [0, 1, 4, 5] if _m < 32 else [0, 1]
            for _fi in range(len(f2_blocks(_m))):
                bank_of[mmcum[_m] + _fi] = _rot[_fi % len(_rot)]
        s2mm = e(nc.semaphore("s2mm"))
        s2ev = e(nc.semaphore("s2ev"))
        s2cv = e(nc.semaphore("s2cv"))
        s2si = e(nc.semaphore("s2si"))
        s2tr = e(nc.semaphore("s2tr"))
        s2te = e(nc.semaphore("s2te"))
        sPAr = e(nc.semaphore("sPAr"))
        sPAv = e(nc.semaphore("sPAv"))
        NPA = len(pa_srcs)
        g_after = {}
        pa_after = {}
        gc = [dcnt["gpsimd"]]
        _gcc = gc[0]
        _pa = 0
        for _m in range(M_IN):
            if _m < 32:
                _gcc += 16
                g_after[("z", _m)] = _gcc
            elif 33 <= _m <= 64:
                _gcc += 16
                g_after[("xh", _m - 33)] = _gcc
            elif _m == 65:
                _gcc += 16
                g_after[("bc", 0)] = _gcc
            while _pa < min(NPA, (NPA * (_m + 1)) // M_IN):
                _gcc += 16
                pa_after[_pa] = _gcc
                _pa += 1

        @block.sync
        def _(sync):
            for m in range(M_IN):
                if m >= 2:
                    sync.wait_ge(s2mm, mmcum[m - 1])
                sync.dma_start(
                    out=wbuf[m % 2][:, :].rearrange("p (k q) -> p k q", k=KD),
                    in_=w_in_t.ap()[:, m].rearrange("k p q -> p k q"),
                ).then_inc(dsem["sync"], 16)
                dcnt["sync"] += 16
        w2_base = dcnt["sync"] - 16 * M_IN

        last_mm = {}

        @block.tensor
        def _(tensor):
            for m in range(M_IN):
                if 33 <= m <= 64:
                    mc1 = m - 33
                    tensor.wait_ge(s2si, mc1 + 1)
                    if mc1 >= 1:
                        tensor.wait_ge(s2te, mc1)
                    else:
                        need = max(last_mm.get(4, -1), last_mm.get(5, -1))
                        if need >= 0:
                            tensor.wait_ge(s2ev, need + 1)
                    for j in range(16):
                        bank = pb[2 + j // 4]
                        slot = (j % 4) * 128
                        tensor.transpose(
                            bank[:, slot:slot + 128],
                            f2c[(m - 1) % 2][:, j * 128:(j + 1) * 128],
                            sb_ident[:, :])
                    tensor.drain()
                    tensor.nop().then_inc(s2tr, 1)
                for fi, (f0, fw) in enumerate(f2_blocks(m)):
                    gidx = mmcum[m] + fi
                    b = bank_of[gidx]
                    if fi == 0:
                        tensor.wait_ge(dsem["sync"], w2_base + 16 * (m + 1))
                    if b in last_mm:
                        tensor.wait_ge(s2ev, last_mm[b] + 1)
                    if m == 65 and b in (4, 5):
                        tensor.wait_ge(s2te, 32)
                    last = None
                    for k in range(KD):
                        last = tensor.matmul(
                            pb[b][:, 0:fw],
                            wbuf[m % 2][:, k * 128:(k + 1) * 128],
                            xnfm[k][:, f0:f0 + fw],
                            start=(k == 0), stop=(k == KD - 1))
                    last.then_inc(s2mm, 1)
                    last_mm[b] = gidx

        pa_v = [0]

        @block.vector
        def _(vector):
            for m in range(M_IN):
                if 33 <= m <= 64:
                    mc1 = m - 33
                    vector.wait_ge(s2tr, mc1 + 1)
                    if mc1 >= 1:
                        vector.wait_ge(dsem["gpsimd"],
                                       g_after[("xh", mc1 - 1)])
                    for j in range(4):
                        ins = vector.tensor_copy(
                            f2t[:, j * 512:(j + 1) * 512], pb[2 + j][:, 0:512])
                    ins.then_inc(s2te, 1)
                P_EV = 128 if m < 65 else 64
                for fi, (f0, fw) in enumerate(f2_blocks(m)):
                    gidx = mmcum[m] + fi
                    vector.wait_ge(s2mm, gidx + 1)
                    if fi == 0 and m >= 1 and (m - 1) < 32:
                        vector.wait_ge(dsem["gpsimd"], g_after[("z", m - 1)])
                    vector.tensor_copy(f2s0[:P_EV, f0:f0 + fw],
                                       pb[bank_of[gidx]][:P_EV, 0:fw]
                                       ).then_inc(s2ev, 1)
                if 32 <= m < 65:
                    mc = m - 32
                    if mc >= 2 and mc - 2 < 32:
                        vector.wait_ge(s2tr, mc - 1)
                    vector.tensor_scalar(f2c[m % 2][:, :],
                                         f2s[m % 2][:, 0:R],
                                         sb_cw[:, mc * 4:mc * 4 + 1], None,
                                         OP.mult)
                    last = None
                    for k in (1, 2, 3):
                        last = vector.scalar_tensor_tensor(
                            f2c[m % 2][:, :], f2s[m % 2][:, k:k + R],
                            sb_cw[:, mc * 4 + k:mc * 4 + k + 1],
                            f2c[m % 2][:, :], OP.mult, OP.add)
                    last.then_inc(s2cv, 1)
                while pa_v[0] < min(NPA, (NPA * (m + 1)) // M_IN):
                    i = pa_v[0]
                    pa_v[0] += 1
                    wi = pa_srcs[i][0]
                    vector.wait_ge(sPAr, i + 1)
                    vector.tensor_tensor(sb_wsum[:, wi:wi + 1],
                                         sb_wsum[:, wi:wi + 1],
                                         sb_wsum[:, 3:4],
                                         OP.add).then_inc(sPAv, 1)

        pa_s = [0]

        @block.scalar
        def _(scalar):
            def pa_emit_s(upto):
                while pa_s[0] < min(NPA, upto):
                    i = pa_s[0]
                    pa_s[0] += 1
                    scalar.wait_ge(dsem["gpsimd"], pa_after[i])
                    if i >= 1:
                        scalar.wait_ge(sPAv, i)
                    scalar.activation(paw[:, :], paw[:, :], ACT.Abs,
                                      accum_out=sb_wsum[:, 3:4]).then_inc(
                                          sPAr, 1)

            for m in range(M_IN):
                if 32 <= m < 65:
                    mc = m - 32
                    scalar.wait_ge(s2cv, mc + 1)
                    scalar.activation(f2c[m % 2][:, :],
                                      f2c[m % 2][:, :], ACT.Silu,
                                      bias=sb_cb[:, mc:mc + 1]).then_inc(
                                          s2si, 1)
                pa_emit_s((NPA * (m + 1)) // M_IN)
            pa_emit_s(NPA)

        pa_g = [0]

        @block.gpsimd
        def _(gpsimd):
            def pa_emit(upto):
                while pa_g[0] < min(NPA, upto):
                    i = pa_g[0]
                    pa_g[0] += 1
                    if i >= 1:
                        gpsimd.wait_ge(sPAr, i)
                    wi, src = pa_srcs[i]
                    gpsimd.dma_start(
                        out=paw[:, :].rearrange("p (n q) -> p n q", n=NPAW),
                        in_=src.rearrange("n p q -> p n q"),
                    ).then_inc(dsem["gpsimd"], 16)
                    gc[0] += 16
                    assert pa_after[i] == gc[0]

            for m in range(M_IN):
                if m < 32:
                    gpsimd.wait_ge(s2ev, mmcum[m + 1])
                    gpsimd.dma_start(out=z_scr.ap()[m],
                                     in_=f2s[m % 2][:, 3:3 + R]).then_inc(
                                         dsem["gpsimd"], 16)
                    gc[0] += 16
                    assert g_after[("z", m)] == gc[0]
                elif 33 <= m <= 64:
                    mc = m - 33
                    gpsimd.wait_ge(s2te, mc + 1)
                    gpsimd.dma_start(
                        out=xh_scr.ap()[:, :, mc * 128:(mc + 1) * 128]
                        .rearrange("c p q -> p c q"),
                        in_=f2t[:, :].rearrange("p (c q) -> p c q", c=NT),
                    ).then_inc(dsem["gpsimd"], 16)
                    gc[0] += 16
                    assert g_after[("xh", mc)] == gc[0]
                elif m == 65:
                    gpsimd.wait_ge(s2si, 33)
                    gpsimd.dma_start(
                        out=bc_scr.ap()[:, :, :].rearrange("c p q -> p c q"),
                        in_=f2c[0][:, :].rearrange("p (c q) -> p c q", c=NT),
                    ).then_inc(dsem["gpsimd"], 16)
                    gc[0] += 16
                    assert g_after[("bc", 0)] == gc[0]
                pa_emit((NPA * (m + 1)) // M_IN)
            pa_emit(NPA)
            dcnt["gpsimd"] = gc[0]
            gpsimd.wait_ge(dsem["gpsimd"], gc[0])

        prev[0] = None
        barrier()

        def dtact(scalar):
            scalar.activation(f2c[1][:64, 0:R], f2s0[:64, 3:3 + R],
                              ACT.Exp, bias=sb_dtb[:, 0:1])
            return scalar.activation(sb_dt[:, :], f2c[1][:64, 0:R],
                                     ACT.Ln, bias=1.0)
        step("scalar", dtact)

        def wfin_mm(tensor):
            tensor.matmul(pb[4][:1, 0:3], sb_wsum[:, 0:3], sb_ones_c[:, 0:1],
                          start=True, stop=True)
            return None
        def wfin_mm2(tensor):
            tensor.matmul(pb[4][:1, 0:3], sb_ones_c[:, 0:1], sb_wsum[:, 0:3],
                          start=True, stop=True)
            return None
        step("tensor", wfin_mm2)

        def wfin_v(vector):
            return vector.tensor_scalar(f2s[0][:1, 0:3], pb[4][:1, 0:3],
                                        1.0 / (IFF * DM), 1e-5, OP.mult,
                                        OP.max)
        step("vector", wfin_v)

        def wfin_bc(tensor):
            tensor.matmul(pb[4][:, 64:67], sb_ones_r[:, :], f2s[0][:1, 0:3],
                          start=True, stop=True)
            return None
        step("tensor", wfin_bc)

        def wfin_v2(vector):
            ins = vector.tensor_copy(sb_wsum[:, 0:3], pb[4][:, 64:67])
            sfence(vector, "vector", ins)
            return vector.reciprocal(sb_swinv[:, 0:3], sb_wsum[:, 0:3])
        step("vector", wfin_v2)

        f2.close()
        f12.close()
        barrier()

        f3a = ExitStack()
        sb_state = f3a.enter_context(nc.sbuf_tensor("sb_state", [64, DIN],
                                                    F32))
        xtm = [f3a.enter_context(nc.sbuf_tensor(f"xtm{j}", [128, DIN], F32))
               for j in range(2)]
        bcB = [f3a.enter_context(nc.sbuf_tensor(f"bcB{j}", [64, 128], F32))
               for j in range(2)]
        f3ld = f3a.enter_context(nc.sbuf_tensor("f3ld", [128, 192], F32))
        f3fac = f3a.enter_context(nc.sbuf_tensor("f3fac", [128, 64], F32))
        f3btm = f3a.enter_context(nc.sbuf_tensor("f3btm", [128, 64], F32))
        f3bs = f3a.enter_context(nc.sbuf_tensor("f3bs", [128, 512], F32))
        f3row = f3a.enter_context(nc.sbuf_tensor("f3row", [1, 192], F32))
        f3crep = f3a.enter_context(nc.sbuf_tensor("f3crep", [64, 64], F32))

        def st0(vector):
            return vector.memset(sb_state[:, :], 0.0)
        step("vector", st0)

        def pf0(sync):
            dma(sync, "sync", xtm[0][:, :], xh_scr.ap()[0])
            return dma(sync, "sync", bcB[0][:, :], bc_scr.ap()[0, 0:64])
        step("sync", pf0, dmas=2)

        for c in range(NT):
            cur, nxt = c % 2, (c + 1) % 2
            if c + 1 < NT:
                def pf(sync, c=c, nxt=nxt):
                    dma(sync, "sync", xtm[nxt][:, :], xh_scr.ap()[c + 1])
                    return dma(sync, "sync", bcB[nxt][:, :],
                               bc_scr.ap()[c + 1, 0:64])
                step("sync", pf, dmas=2)

            def cum(vector, c=c):
                vector.tensor_scalar(sb_lat[:, :],
                                     sb_dt[:, c * 128:(c + 1) * 128],
                                     sb_acol[:, 0:1], None, OP.mult)
                ins = vector.tensor_tensor_scan(
                    sb_lall[:, c * 128:(c + 1) * 128],
                    sb_lat[:, :], sb_zero64[:, :],
                    0.0, OP.add, OP.add)
                sfence(vector, "vector", ins)
                vector.tensor_copy(sb_lcum_pre[:, c:c + 1], sb_lcum[:, 0:1])
                return vector.tensor_tensor(
                    sb_lcum[:, 0:1], sb_lcum[:, 0:1],
                    sb_lall[:, c * 128 + 127:c * 128 + 128], OP.add)
            step("vector", cum)

            def trs(tensor, c=c, cur=cur):
                tensor.transpose(pb[2][:128, 0:64],
                                 sb_lall[:, c * 128:(c + 1) * 128],
                                 sb_ident[:64, :64])
                tensor.transpose(pb[2][:128, 64:128],
                                 sb_dt[:, c * 128:(c + 1) * 128],
                                 sb_ident[:64, :64])
                tensor.transpose(pb[2][:128, 128:192], bcB[cur][:, :],
                                 sb_ident[:64, :64])
                tensor.transpose(pb[3][:1, 0:64],
                                 sb_lall[:, c * 128 + 127:c * 128 + 128],
                                 sb_ident[:64, :64])
                return None
            step("tensor", trs)

            def ev1(vector, c=c):
                vector.tensor_copy(f3ld[:, 0:128], pb[2][:, 0:128])
                vector.tensor_copy(f3btm[:, :], pb[2][:, 128:192])
                return vector.tensor_copy(f3row[:1, 0:64], pb[3][:1, 0:64])
            step("vector", ev1)

            def crepe(scalar, c=c):
                return scalar.activation(f3row[:1, 64:128], f3row[:1, 0:64],
                                         ACT.Exp)
            step("scalar", crepe)

            def lbc(tensor, c=c):
                tensor.matmul(pb[3][:, 64:128], sb_ones_r[:, :],
                              f3row[:1, 0:64], start=True, stop=True)
                tensor.matmul(pb[3][0:64, 128:192], sb_ones_r[:, 0:64],
                              f3row[:1, 64:128], start=True, stop=True)
                return None
            step("tensor", lbc)

            def fac(vector, c=c):
                vector.tensor_copy(f3crep[:, :], pb[3][0:64, 128:192])
                return vector.tensor_tensor(f3fac[:, :], pb[3][:, 64:128],
                                            f3ld[:, 0:64], OP.subtract)
            step("vector", fac)

            def face(scalar, c=c):
                return scalar.activation(f3fac[:, :], f3fac[:, :], ACT.Exp)
            step("scalar", face)

            def fac2(vector, c=c):
                return vector.tensor_tensor(f3fac[:, :], f3fac[:, :],
                                            f3ld[:, 64:128], OP.mult)
            step("vector", fac2)

            for g in range(8):
                def bs(vector, g=g, cur=cur):
                    last = None
                    for j in range(8):
                        hh = g * 8 + j
                        last = vector.tensor_scalar(
                            f3bs[:, j * 64:(j + 1) * 64], f3btm[:, 0:64],
                            f3fac[:, hh:hh + 1], None, OP.mult)
                    return last
                step("vector", bs)

                def mm3(tensor, g=g, cur=cur):
                    for j in range(8):
                        hh = g * 8 + j
                        tensor.matmul(
                            pb[4][0:64, j * 64:(j + 1) * 64],
                            f3bs[:, j * 64:(j + 1) * 64],
                            xtm[cur][:, hh * 64:(hh + 1) * 64],
                            start=True, stop=True)
                    return None
                step("tensor", mm3)

                def supd(vector, g=g):
                    last = None
                    for j in range(8):
                        hh = g * 8 + j
                        last = vector.scalar_tensor_tensor(
                            sb_state[:, hh * 64:(hh + 1) * 64],
                            sb_state[:, hh * 64:(hh + 1) * 64],
                            f3crep[:, hh:hh + 1],
                            pb[4][0:64, j * 64:(j + 1) * 64],
                            OP.mult, OP.add)
                    return last
                step("vector", supd)

            def stst(gpsimd, c=c):
                ins = dma(gpsimd, "gpsimd", sloc_scr.ap()[c], sb_state[:, :])
                if DBG:
                    ins = dma(gpsimd, "gpsimd", sloc_dbg.ap()[c],
                              sb_state[:, :])
                return ins
            step("gpsimd", stst, dmas=2 if DBG else 1)

        def coll(gpsimd):
            gpsimd.collective_compute(
                "AllGather", OP.bypass,
                replica_groups=[[2 * i, 2 * i + 1] for i in range(4)],
                ins=[sloc_scr.ap()[NT - 1]],
                outs=[cc_out.ap()[:, :]],
            ).then_inc(sC, 1)
            gpsimd.drain()
            return None
        step("gpsimd", coll)

        f3a.close()
        barrier()

        f4 = ExitStack()
        qxr = [f4.enter_context(nc.sbuf_tensor(f"qxr{k}", [128, R], BF16))
               for k in range(KD)]
        wgb = [f4.enter_context(nc.sbuf_tensor(f"wgb{j}", [128, KD * 128],
                                               F32)) for j in range(2)]
        wub = [f4.enter_context(nc.sbuf_tensor(f"wub{j}", [128, KD * 128],
                                               F32)) for j in range(2)]
        qwg = [f4.enter_context(nc.sbuf_tensor(f"qwg{j}", [128, KD * 128],
                                               BF16)) for j in range(2)]
        qwu = [f4.enter_context(nc.sbuf_tensor(f"qwu{j}", [128, KD * 128],
                                               BF16)) for j in range(2)]
        f4g = [f4.enter_context(nc.sbuf_tensor(f"f4g{j}", [128, 512], F32))
               for j in range(2)]
        f4u = [f4.enter_context(nc.sbuf_tensor(f"f4u{j}", [128, 512], F32))
               for j in range(2)]
        f4h = [f4.enter_context(nc.sbuf_tensor(f"f4h{j}", [128, 512], F32))
               for j in range(2)]
        f4t = [f4.enter_context(nc.sbuf_tensor(f"f4t{j}", [128, 512], F32))
               for j in range(2)]
        f4w = f4.enter_context(nc.sbuf_tensor("f4w", [128, 512], F32))
        f4qs = f4.enter_context(nc.sbuf_tensor("f4qs", [128, 512], F32))
        sxbc = f4.enter_context(nc.sbuf_tensor("sxbc", [128, R], F32))
        f4row = f4.enter_context(nc.sbuf_tensor("f4row", [1, 128], F32))

        @block.sync
        def _(sync):
            for k in range(KD):
                sync.dma_start(out=qxr[k][:, :], in_=qx_scr.ap()[k]).then_inc(
                    dsem["sync"], 16)
                dcnt["sync"] += 16
            for m in range(MFF):
                if m >= 2:
                    sync.wait_ge(sW4, m - 1)
                sync.dma_start(
                    out=wgb[m % 2][:, :].rearrange("p (k q) -> p k q", k=KD),
                    in_=w_gate_t.ap()[:, m].rearrange("k p q -> p k q"),
                ).then_inc(dsem["sync"], 16)
                sync.dma_start(
                    out=wub[m % 2][:, :].rearrange("p (k q) -> p k q", k=KD),
                    in_=w_up_t.ap()[:, m].rearrange("k p q -> p k q"),
                ).then_inc(dsem["sync"], 16)
                dcnt["sync"] += 32
        w4_base = dcnt["sync"] - 32 * MFF

        for i in range(NT):
            bcast_col(sb_sxc[:, i:i + 1], sxbc[:, i * 128:(i + 1) * 128],
                      f4row)

        NI4 = MFF * 4
        s4mm = e(nc.semaphore("s4mm"))
        s4ep = e(nc.semaphore("s4ep"))
        s4si = e(nc.semaphore("s4si"))
        s4hm = e(nc.semaphore("s4hm"))
        s4tr = e(nc.semaphore("s4tr"))
        s4te = e(nc.semaphore("s4te"))
        g4base = dcnt["gpsimd"]

        @block.tensor
        def _(tensor):
            def tr4(j):
                tensor.wait_ge(s4hm, j + 1)
                if j >= 2:
                    tensor.wait_ge(s4te, j - 1)
                for q in range(4):
                    tensor.transpose(pb[2 + j % 2][:, q * 128:(q + 1) * 128],
                                     f4h[j % 2][:, q * 128:(q + 1) * 128],
                                     sb_ident[:, :])
                tensor.drain()
                tensor.nop().then_inc(s4tr, 1)

            for m in range(MFF):
                for f in range(4):
                    it = m * 4 + f
                    if it >= 1:
                        tr4(it - 1)
                    if f == 0:
                        tensor.wait_ge(dsem["sync"], w4_base + 32 * (m + 1))
                        tensor.wait_ge(sW4, m + 1)
                    if it >= 2:
                        tensor.wait_ge(s4ep, it - 1)
                    gb = pb[(it % 2) * 4]
                    ub = pb[1 + (it % 2) * 4]
                    for k in range(KD):
                        tensor.matmul(
                            gb[:, :], qwg[m % 2][:, k * 128:(k + 1) * 128],
                            qxr[k][:, f * 512:(f + 1) * 512],
                            start=(k == 0), stop=(k == KD - 1))
                    last = None
                    for k in range(KD):
                        last = tensor.matmul(
                            ub[:, :], qwu[m % 2][:, k * 128:(k + 1) * 128],
                            qxr[k][:, f * 512:(f + 1) * 512],
                            start=(k == 0), stop=(k == KD - 1))
                    last.then_inc(s4mm, 1)
            tr4(NI4 - 1)

        @block.vector
        def _(vector):
            for m in range(MFF):
                vector.wait_ge(dsem["sync"], w4_base + 32 * (m + 1))
                if m >= 2:
                    vector.wait_ge(s4mm, 4 * (m - 1))
                last = None
                for src, qdst, wi in ((wgb[m % 2], qwg[m % 2], 0),
                                      (wub[m % 2], qwu[m % 2], 1)):
                    for blk in range(4):
                        sl = slice(blk * 512, (blk + 1) * 512)
                        vector.tensor_scalar(f4w[:, :], src[:, sl],
                                             sb_swinv[:, wi:wi + 1], None,
                                             OP.mult)
                        vector.tensor_scalar(f4qs[:, :], f4w[:, :], -0.5,
                                             None, OP.is_le)
                        last = vector.scalar_tensor_tensor(
                            qdst[:, sl], f4w[:, :], 0.5, f4qs[:, :],
                            OP.is_ge, OP.subtract)
                last.then_inc(sW4, 1)
                for f in range(4):
                    it = m * 4 + f
                    vector.wait_ge(s4mm, it + 1)
                    gb = pb[(it % 2) * 4]
                    ub = pb[1 + (it % 2) * 4]
                    vector.tensor_tensor(f4g[it % 2][:, :], gb[:, :],
                                         sxbc[:, f * 512:(f + 1) * 512],
                                         OP.mult)
                    vector.tensor_scalar(f4g[it % 2][:, :], f4g[it % 2][:, :],
                                         sb_wsum[:, 0:1], None, OP.mult)
                    vector.tensor_tensor(f4u[it % 2][:, :], ub[:, :],
                                         sxbc[:, f * 512:(f + 1) * 512],
                                         OP.mult)
                    vector.tensor_scalar(f4u[it % 2][:, :], f4u[it % 2][:, :],
                                         sb_wsum[:, 1:2], None,
                                         OP.mult).then_inc(s4ep, 1)
                    vector.wait_ge(s4si, it + 1)
                    vector.tensor_tensor(f4h[it % 2][:, :], f4g[it % 2][:, :],
                                         f4u[it % 2][:, :],
                                         OP.mult).then_inc(s4hm, 1)
                    if it >= 1:
                        j = it - 1
                        vector.wait_ge(s4tr, j + 1)
                        if j >= 2:
                            vector.wait_ge(dsem["gpsimd"],
                                           g4base + 16 * (j - 1))
                        vector.tensor_copy(f4t[j % 2][:, :],
                                           pb[2 + j % 2][:, 0:512]).then_inc(
                                               s4te, 1)
            j = NI4 - 1
            vector.wait_ge(s4tr, j + 1)
            vector.wait_ge(dsem["gpsimd"], g4base + 16 * (j - 1))
            vector.tensor_copy(f4t[j % 2][:, :],
                               pb[2 + j % 2][:, 0:512]).then_inc(s4te, 1)

        @block.scalar
        def _(scalar):
            for it in range(NI4):
                scalar.wait_ge(s4ep, it + 1)
                scalar.activation(f4g[it % 2][:, :], f4g[it % 2][:, :],
                                  ACT.Silu).then_inc(s4si, 1)

        @block.gpsimd
        def _(gpsimd):
            for it in range(NI4):
                m, f = it // 4, it % 4
                gpsimd.wait_ge(s4te, it + 1)
                gpsimd.dma_start(
                    out=h_scr.ap()[f * 4:(f + 1) * 4, :,
                                   m * 128:(m + 1) * 128]
                    .rearrange("c p q -> p c q"),
                    in_=f4t[it % 2][:, :].rearrange("p (c q) -> p c q", c=4),
                ).then_inc(dsem["gpsimd"], 16)
                dcnt["gpsimd"] += 16
            gpsimd.wait_ge(dsem["gpsimd"], dcnt["gpsimd"])

        prev[0] = None
        f4.close()
        barrier()

        f3b = ExitStack()
        sb_mask = f3b.enter_context(nc.sbuf_tensor("sb_mask", [128, 128],
                                                   F32))
        sb_sel = f3b.enter_context(nc.sbuf_tensor("sb_sel", [8, 8 * 128],
                                                  F32))
        xtmB = [f3b.enter_context(nc.sbuf_tensor(f"xtmB{j}", [128, DIN], F32))
                for j in range(2)]
        bcBB = [f3b.enter_context(nc.sbuf_tensor(f"bcBB{j}", [64, 128], F32))
                for j in range(2)]
        bcCB = [f3b.enter_context(nc.sbuf_tensor(f"bcCB{j}", [64, 128], F32))
                for j in range(2)]
        zch = [f3b.enter_context(nc.sbuf_tensor(f"zch{j}", [128, DIN], F32))
               for j in range(2)]
        spre = [f3b.enter_context(nc.sbuf_tensor(f"spre{j}", [64, DIN], F32))
                for j in range(2)]
        sinit = f3b.enter_context(nc.sbuf_tensor("sinit", [64, DIN], F32))
        sful = f3b.enter_context(nc.sbuf_tensor("sful", [64, DIN], F32))
        cbt = f3b.enter_context(nc.sbuf_tensor("cbt", [128, 128], F32))
        mht = f3b.enter_context(nc.sbuf_tensor("mht", [128, 8 * 128], F32))
        ych = f3b.enter_context(nc.sbuf_tensor("ych", [128, DIN], F32))
        lamr = f3b.enter_context(nc.sbuf_tensor("lamr", [64, 64], F32))
        yst = f3b.enter_context(nc.sbuf_tensor("yst", [128, 512], F32))
        rows8 = f3b.enter_context(nc.sbuf_tensor("rows8", [8, 128], F32))
        f3brow = f3b.enter_context(nc.sbuf_tensor("f3brow", [1, 128], F32))
        f3bld = f3b.enter_context(nc.sbuf_tensor("f3bld", [128, 128], F32))
        sqacc = f3b.enter_context(nc.sbuf_tensor("sqacc", [128, 128], F32))
        yfm = f3b.enter_context(nc.sbuf_tensor("yfm", [128, DIN], F32))

        def ldsel(sync):
            dma(sync, "sync", sb_mask[:], maskneg.ap())
            return dma(sync, "sync",
                       sb_sel[:].rearrange("p (j q) -> p j q", j=8),
                       selmat.ap().rearrange("j p q -> p j q"))
        step("sync", ldsel, dmas=2)

        def getinit(gpsimd):
            gpsimd.wait_ge(sC, 1)
            return dma(gpsimd, "gpsimd", sinit[:, :], cc_out.ap()[0:64, :])
        step("gpsimd", getinit, dmas=1)

        def maskinit(vector):
            return vector.tensor_scalar(sinit[:, :], sinit[:, :],
                                        sb_smask[:, 0:1], None, OP.mult)
        step("vector", maskinit)

        def pfb0(sync):
            dma(sync, "sync", xtmB[0][:, :], xh_scr.ap()[0])
            dma(sync, "sync", bcBB[0][:, :], bc_scr.ap()[0, 0:64])
            dma(sync, "sync", bcCB[0][:, :], bc_scr.ap()[0, 64:128])
            return dma(sync, "sync",
                       zch[0][:, :].rearrange("p (mq q) -> p mq q", mq=32),
                       z_scr.ap()[:, :, 0:128].rearrange("m p q -> p m q"))
        step("sync", pfb0, dmas=4)

        for c in range(NT):
            cur, nxt = c % 2, (c + 1) % 2
            if c + 1 < NT:
                def pfb(sync, c=c, nxt=nxt):
                    dma(sync, "sync", xtmB[nxt][:, :], xh_scr.ap()[c + 1])
                    dma(sync, "sync", bcBB[nxt][:, :],
                        bc_scr.ap()[c + 1, 0:64])
                    dma(sync, "sync", bcCB[nxt][:, :],
                        bc_scr.ap()[c + 1, 64:128])
                    dma(sync, "sync",
                        zch[nxt][:, :].rearrange("p (mq q) -> p mq q", mq=32),
                        z_scr.ap()[:, :, (c + 1) * 128:(c + 2) * 128]
                        .rearrange("m p q -> p m q"))
                    return dma(sync, "sync", spre[nxt][:, :],
                               sloc_scr.ap()[c])
                step("sync", pfb, dmas=5)
            if c == 0:
                def z0(vector):
                    return vector.memset(spre[0][:, :], 0.0)
                step("vector", z0)

            def lam(scalar, c=c):
                return scalar.activation(lamr[:, 0:1],
                                         sb_lcum_pre[:, c:c + 1], ACT.Exp)
            step("scalar", lam)

            def lamt(tensor, c=c):
                tensor.transpose(pb[2][:1, 0:64], lamr[:, 0:1],
                                 sb_ident[:64, :64])
                return None
            step("tensor", lamt)

            def lamev(vector, c=c):
                return vector.tensor_copy(f3brow[:1, 0:64], pb[2][:1, 0:64])
            step("vector", lamev)

            def lamb(tensor, c=c):
                tensor.matmul(pb[2][0:64, 64:128], sb_ones_r[:, 0:64],
                              f3brow[:1, 0:64], start=True, stop=True)
                return None
            step("tensor", lamb)

            def lamev2(vector, c=c):
                return vector.tensor_copy(lamr[:, :], pb[2][0:64, 64:128])
            step("vector", lamev2)

            def sfull(vector, c=c, cur=cur):
                last = None
                for hh in range(H):
                    last = vector.scalar_tensor_tensor(
                        sful[:, hh * 64:(hh + 1) * 64],
                        sinit[:, hh * 64:(hh + 1) * 64],
                        lamr[:, hh:hh + 1],
                        spre[cur][:, hh * 64:(hh + 1) * 64],
                        OP.mult, OP.add)
                return last
            step("vector", sfull)

            def trsb(tensor, c=c, cur=cur):
                tensor.transpose(pb[2][:128, 0:64],
                                 sb_lall[:, c * 128:(c + 1) * 128],
                                 sb_ident[:64, :64])
                tensor.transpose(pb[2][:128, 64:128],
                                 sb_dt[:, c * 128:(c + 1) * 128],
                                 sb_ident[:64, :64])
                tensor.matmul(pb[3][:, 0:128], bcBB[cur][:, :],
                              bcCB[cur][:, :], start=True, stop=True)
                return None
            step("tensor", trsb)

            def evb(vector, c=c):
                vector.tensor_copy(f3bld[:, 0:128], pb[2][:, 0:128])
                return vector.tensor_copy(cbt[:, :], pb[3][:, 0:128])
            step("vector", evb)

            def gamm(scalar, c=c):
                return scalar.activation(sb_gam[:, :], f3bld[:, 0:64],
                                         ACT.Exp)
            step("scalar", gamm)

            for g in range(8):
                def ltr(tensor, c=c, g=g):
                    tensor.transpose(pb[2][:8, 0:128],
                                     f3bld[:, g * 8:g * 8 + 8]
                                     if False else f3bld[:, 0:64][:, g * 8:g * 8 + 8],
                                     sb_ident[:, :])
                    return None
                step("tensor", ltr)

                def lev(vector, c=c, g=g):
                    return vector.tensor_copy(rows8[:, :], pb[2][:8, 0:128])
                step("vector", lev)

                def lb(tensor, c=c, g=g):
                    for j in range(8):
                        bank = pb[2 + j // 4]
                        slot = (j % 4) * 128
                        tensor.matmul(bank[:, slot:slot + 128],
                                      sb_sel[:, j * 128:(j + 1) * 128],
                                      rows8[:, :], start=True, stop=True)
                    return None
                step("tensor", lb)

                def mbuild(vector, c=c, g=g):
                    last = None
                    for j in range(8):
                        hh = g * 8 + j
                        bank = pb[2 + j // 4]
                        slot = (j % 4) * 128
                        last = vector.scalar_tensor_tensor(
                            mht[:, j * 128:(j + 1) * 128],
                            bank[:, slot:slot + 128],
                            f3bld[:, hh:hh + 1], sb_mask[:, :],
                            OP.subtract, OP.add)
                    return last
                step("vector", mbuild)

                def mexp(scalar, c=c, g=g):
                    return scalar.activation(mht[:, :], mht[:, :], ACT.Exp)
                step("scalar", mexp)

                def mfin(vector, c=c, g=g):
                    last = None
                    for j in range(8):
                        hh = g * 8 + j
                        last = vector.scalar_tensor_tensor(
                            mht[:, j * 128:(j + 1) * 128],
                            mht[:, j * 128:(j + 1) * 128],
                            f3bld[:, 64 + hh:64 + hh + 1], cbt[:, :],
                            OP.mult, OP.mult)
                    return last
                step("vector", mfin)

                def ymm(tensor, c=c, g=g, cur=cur):
                    for j in range(8):
                        hh = g * 8 + j
                        tensor.matmul(
                            pb[4][:, j * 64:(j + 1) * 64],
                            mht[:, j * 128:(j + 1) * 128],
                            xtmB[cur][:, hh * 64:(hh + 1) * 64],
                            start=True, stop=True)
                        tensor.matmul(
                            pb[5][:, j * 64:(j + 1) * 64],
                            bcCB[cur][:, :],
                            sful[:, hh * 64:(hh + 1) * 64],
                            start=True, stop=True)
                    return None
                step("tensor", ymm)

                def yev(scalar, c=c, g=g):
                    last = None
                    for j in range(8):
                        hh = g * 8 + j
                        last = scalar.activation(
                            yst[:, j * 64:(j + 1) * 64],
                            pb[5][:, j * 64:(j + 1) * 64], ACT.Copy,
                            scale=sb_gam[:, hh:hh + 1])
                    return last
                step("scalar", yev)

                def yev2(vector, c=c, g=g, cur=cur):
                    last = None
                    for j in range(8):
                        hh = g * 8 + j
                        vector.scalar_tensor_tensor(
                            ych[:, hh * 64:(hh + 1) * 64],
                            xtmB[cur][:, hh * 64:(hh + 1) * 64],
                            sb_dbc[:, hh:hh + 1],
                            pb[4][:, j * 64:(j + 1) * 64],
                            OP.mult, OP.add)
                        last = vector.tensor_tensor(
                            ych[:, hh * 64:(hh + 1) * 64],
                            ych[:, hh * 64:(hh + 1) * 64],
                            yst[:, j * 64:(j + 1) * 64], OP.add)
                    return last
                step("vector", yev2)

            transpose_batch([(ych[:, q * 128:(q + 1) * 128],
                              yfm[:, q * 128:(q + 1) * 128])
                             for q in range(32)])

            def gsil(scalar, c=c, cur=cur):
                return scalar.activation(zch[cur][:, :], zch[cur][:, :],
                                         ACT.Silu)
            step("scalar", gsil)

            def gmul(vector, c=c, cur=cur):
                vector.memset(sqacc[:, :], 0.0)
                last = None
                for q in range(32):
                    sl = slice(q * 128, (q + 1) * 128)
                    vector.tensor_tensor(yfm[:, sl], yfm[:, sl],
                                         zch[cur][:, sl], OP.mult)
                    vector.tensor_tensor(yst[:, 0:128], yfm[:, sl],
                                         yfm[:, sl], OP.mult)
                    last = vector.tensor_tensor(sqacc[:, :], sqacc[:, :],
                                                yst[:, 0:128], OP.add)
                return last
            step("vector", gmul)

            def gred(tensor, c=c):
                tensor.matmul(pb[2][:1, 0:128], sb_ones_c[:, 0:1],
                              sqacc[:, :], start=True, stop=True)
                return None
            step("tensor", gred)

            def grr(vector, c=c):
                return vector.tensor_scalar(f3brow[:1, 0:128],
                                            pb[2][:1, 0:128], 1.0 / DIN,
                                            1e-5, OP.mult, OP.add)
            step("vector", grr)

            def grs(scalar, c=c):
                scalar.activation(f3brow[:1, 0:128], f3brow[:1, 0:128],
                                  ACT.Ln)
                return scalar.activation(
                    f3brow[:1, 0:128], f3brow[:1, 0:128], ACT.Exp,
                    scale=sb_cconst[:1, C_NEGHALF:C_NEGHALF + 1])
            step("scalar", grs)

            def grb(tensor, c=c):
                tensor.matmul(pb[2][:, 128:256], sb_ones_r[:, :],
                              f3brow[:1, 0:128], start=True, stop=True)
                return None
            step("tensor", grb)

            def grbe(vector, c=c):
                return vector.tensor_copy(sqacc[:, :], pb[2][:, 128:256])
            step("vector", grbe)

            def yrm(vector, c=c):
                last = None
                for q in range(32):
                    sl = slice(q * 128, (q + 1) * 128)
                    last = vector.scalar_tensor_tensor(
                        yfm[:, sl], yfm[:, sl], sb_normw[:, q:q + 1],
                        sqacc[:, :], OP.mult, OP.mult)
                return last
            step("vector", yrm)

            def styr(gpsimd, c=c):
                return dma(gpsimd, "gpsimd",
                           yr_scr.ap()[c].rearrange("k p q -> p k q"),
                           yfm[:, :].rearrange("p (k q) -> p k q", k=KWO))
            step("gpsimd", styr, dmas=1)

        f3b.close()
        barrier()

        f3c = ExitStack()
        yrb = f3c.enter_context(nc.sbuf_tensor("yrb", [128, KWO * 512], F32))
        wob = [f3c.enter_context(nc.sbuf_tensor(f"wob{j}", [128, KWO * 128],
                                                F32)) for j in range(2)]
        f3cs = [f3c.enter_context(nc.sbuf_tensor(f"f3cs{j}", [128, 512],
                                                 F32)) for j in range(2)]
        f3ct = [f3c.enter_context(nc.sbuf_tensor(f"f3ct{j}", [128, 512],
                                                 F32)) for j in range(2)]
        wdp = [f3c.enter_context(nc.sbuf_tensor(f"wdp{j}", [128, MDN * 128],
                                                F32)) for j in range(2)]
        qwp = [f3c.enter_context(nc.sbuf_tensor(f"qwp{j}", [128, MDN * 128],
                                                BF16)) for j in range(2)]
        f5pc = f3c.enter_context(nc.sbuf_tensor("f5pc", [128, 512], F32))
        f5pw = f3c.enter_context(nc.sbuf_tensor("f5pw", [128, 512], F32))

        wob_after = {}
        wd_after = {}
        store_after = {}

        @block.sync
        def _(sync):
            for f in range(4):
                for m in range(MWO):
                    it = f * MWO + m
                    if it >= 2:
                        sync.wait_ge(sWC, it - 1)
                    sync.dma_start(
                        out=wob[it % 2][:, :].rearrange("p (k q) -> p k q",
                                                        k=KWO),
                        in_=w_out_t.ap()[:, m].rearrange("k p q -> p k q"),
                    ).then_inc(dsem["sync"], 16)
                    dcnt["sync"] += 16
                    wob_after[it] = dcnt["sync"]
                    k = it
                    if k < KDN:
                        if k >= 2:
                            sync.wait_ge(sWP, k - 1)
                        sync.dma_start(
                            out=wdp[k % 2][:, :].rearrange("p (m q) -> p m q",
                                                           m=MDN),
                            in_=w_down_t.ap()[k].rearrange("m p q -> p m q"),
                        ).then_inc(dsem["sync"], 16)
                        dcnt["sync"] += 16
                        wd_after[k] = dcnt["sync"]
                    if 2 <= it < KDN + 2:
                        ks = it - 2
                        sync.wait_ge(sWP, ks + 1)
                        sync.dma_start(
                            out=qwd_scr.ap()[ks].rearrange("m p q -> p m q"),
                            in_=qwp[ks % 2][:, :].rearrange(
                                "p (m q) -> p m q", m=MDN),
                        ).then_inc(dsem["sync"], 16)
                        dcnt["sync"] += 16
                        store_after[ks] = dcnt["sync"]

        NIC = 4 * MWO
        scmm = e(nc.semaphore("scmm"))
        sctr = e(nc.semaphore("sctr"))
        scte = e(nc.semaphore("scte"))
        scld = e(nc.semaphore("scld"))
        gcbase = dcnt["gpsimd"]
        acbase = dcnt["scalar"]

        @block.scalar
        def _(scalar):
            for f in range(4):
                if f >= 1:
                    scalar.wait_ge(scmm, f * MWO)
                for k in range(KWO):
                    scalar.dma_start(
                        out=yrb[:, k * 512:(k + 1) * 512].rearrange(
                            "p (cs q) -> p cs q", cs=4),
                        in_=yr_scr.ap()[4 * f:4 * f + 4, k].rearrange(
                            "cs p q -> p cs q"),
                    ).then_inc(dsem["scalar"], 16)
                    dcnt["scalar"] += 16

        @block.tensor
        def _(tensor):
            def trc(j):
                tensor.wait_ge(sWC, j + 1)
                if j >= 1:
                    tensor.wait_ge(scte, j)
                for q in range(4):
                    tensor.transpose(pb[2][:, q * 128:(q + 1) * 128],
                                     f3cs[j % 2][:, q * 128:(q + 1) * 128],
                                     sb_ident[:, :])
                tensor.drain()
                tensor.nop().then_inc(sctr, 1)

            for it in range(NIC):
                f, m = it // MWO, it % MWO
                if it >= 1:
                    trc(it - 1)
                if m == 0:
                    tensor.wait_ge(dsem["scalar"],
                                   acbase + 16 * KWO * (f + 1))
                tensor.wait_ge(dsem["sync"], wob_after[it])
                if it >= 2:
                    tensor.wait_ge(sWC, it - 1)
                last = None
                for k in range(KWO):
                    last = tensor.matmul(
                        pb[it % 2][:, :],
                        wob[it % 2][:, k * 128:(k + 1) * 128],
                        yrb[:, k * 512:(k + 1) * 512],
                        start=(k == 0), stop=(k == KWO - 1))
                last.then_inc(scmm, 1)
            trc(NIC - 1)

        @block.sync
        def _(sync):
            for ks in (KDN - 2, KDN - 1):
                sync.wait_ge(sWP, ks + 1)
                sync.dma_start(
                    out=qwd_scr.ap()[ks].rearrange("m p q -> p m q"),
                    in_=qwp[ks % 2][:, :].rearrange("p (m q) -> p m q",
                                                    m=MDN),
                ).then_inc(dsem["sync"], 16)
                dcnt["sync"] += 16
                store_after[ks] = dcnt["sync"]
            sync.wait_ge(dsem["sync"], dcnt["sync"])

        @block.vector
        def _(vector):
            for it in range(NIC):
                vector.wait_ge(scmm, it + 1)
                ins = vector.tensor_copy(f3cs[it % 2][:, :],
                                         pb[it % 2][:, :])
                ins.then_inc(sWC, 1)
                k = it
                if k < KDN:
                    vector.wait_ge(dsem["sync"], wd_after[k])
                    if k >= 2:
                        vector.wait_ge(dsem["sync"], store_after[k - 2])
                    last = None
                    for blk in range(4):
                        sl = slice(blk * 512, (blk + 1) * 512)
                        vector.tensor_scalar(f5pw[:, :], wdp[k % 2][:, sl],
                                             sb_swinv[:, 2:3], None, OP.mult)
                        vector.tensor_scalar(f5pc[:, :], f5pw[:, :], -0.5,
                                             None, OP.is_le)
                        last = vector.scalar_tensor_tensor(
                            qwp[k % 2][:, sl], f5pw[:, :], 0.5, f5pc[:, :],
                            OP.is_ge, OP.subtract)
                    last.then_inc(sWP, 1)
                if it >= 1:
                    j = it - 1
                    vector.wait_ge(sctr, j + 1)
                    if j >= 2:
                        vector.wait_ge(dsem["gpsimd"], gcbase + 16 * (j - 1))
                    vector.tensor_copy(f3ct[j % 2][:, :],
                                       pb[2][:, 0:512]).then_inc(scte, 1)
            j = NIC - 1
            vector.wait_ge(sctr, j + 1)
            vector.wait_ge(dsem["gpsimd"], gcbase + 16 * (j - 1))
            vector.tensor_copy(f3ct[j % 2][:, :],
                               pb[2][:, 0:512]).then_inc(scte, 1)

        @block.gpsimd
        def _(gpsimd):
            for it in range(NIC):
                f, m = it // MWO, it % MWO
                gpsimd.wait_ge(scte, it + 1)
                gpsimd.dma_start(
                    out=s_scr.ap()[f * 4:(f + 1) * 4, :,
                                   m * 128:(m + 1) * 128]
                    .rearrange("c p q -> p c q"),
                    in_=f3ct[it % 2][:, :].rearrange("p (c q) -> p c q", c=4),
                ).then_inc(dsem["gpsimd"], 16)
                dcnt["gpsimd"] += 16
            gpsimd.wait_ge(dsem["gpsimd"], dcnt["gpsimd"])

        prev[0] = None
        f3c.close()
        barrier()

        f5a = ExitStack()
        hch = [f5a.enter_context(nc.sbuf_tensor("hch0", [128, IFF], F32))]
        sb_lnw = f5a.enter_context(nc.sbuf_tensor("sb_lnw", [128, IFF], F32))
        f5w = f5a.enter_context(nc.sbuf_tensor("f5w", [128, IFF], F32))
        f5v = f5a.enter_context(nc.sbuf_tensor("f5v", [128, IFF], F32))
        f5h = f5a.enter_context(nc.sbuf_tensor("f5h", [128, IFF], F16))
        f5qs = f5a.enter_context(nc.sbuf_tensor("f5qs", [128, IFF], BF16))
        f5c = f5a.enter_context(nc.sbuf_tensor("f5c", [128, 8], F32))

        def pf50(sync):
            return dma(sync, "sync", sb_lnw[:, :], lnw_bc.ap())
        step("sync", pf50, dmas=1)

        for c in range(NT):
            cur = 0
            def pf5(sync, c=c):
                return dma(sync, "sync", hch[0][:, :], h_scr.ap()[c])
            step("sync", pf5, dmas=1)

            def rms(scalar, cur=cur):
                return scalar.activation(f5w[:, :], hch[cur][:, :],
                                         ACT.Square, accum_out=f5c[:, 0:1])
            step("scalar", rms)

            def rr(vector, cur=cur):
                return vector.tensor_scalar(f5c[:, 1:2], f5c[:, 0:1],
                                            1.0 / IFF, 1e-6, OP.mult, OP.add)
            step("vector", rr)

            def rr2(scalar):
                scalar.activation(f5c[:, 2:3], f5c[:, 1:2], ACT.Ln)
                return scalar.activation(
                    f5c[:, 3:4], f5c[:, 2:3], ACT.Exp,
                    scale=sb_cconst[:, C_NEGHALF:C_NEGHALF + 1])
            step("scalar", rr2)

            def h2(vector, cur=cur):
                vector.tensor_scalar(f5w[:, :], hch[cur][:, :], f5c[:, 3:4],
                                     None, OP.mult)
                last = None
                for q in range(4):
                    sl = slice(q * 2048, (q + 1) * 2048)
                    last = vector.tensor_tensor(f5w[:, sl], f5w[:, sl],
                                                sb_lnw[:, sl], OP.mult)
                return last
            step("vector", h2)

            def qs(vector, c=c):
                ins = vector.reduce_max(f5c[:, 4:5], f5w[:, :], AX,
                                        apply_absolute_value=True)
                sfence(vector, "vector", ins)
                vector.tensor_scalar(f5c[:, 4:5], f5c[:, 4:5], 1e-5, None,
                                     OP.max)
                ins = vector.reciprocal(f5c[:, 5:6], f5c[:, 4:5])
                sfence(vector, "vector", ins)
                vector.tensor_scalar(f5c[:, 5:6], f5c[:, 5:6], 127.0, None,
                                     OP.mult)
                ins = vector.tensor_scalar(sb_s2c[:, c:c + 1], f5c[:, 4:5],
                                            1.0 / 127.0, None, OP.mult)
                sfence(vector, "vector", ins)
                vector.tensor_scalar(f5v[:, :], f5w[:, :], f5c[:, 5:6],
                                     None, OP.mult)
                vector.tensor_scalar(f5h[:, :], f5v[:, :], 1536.0, None,
                                     OP.add)
                vector.tensor_scalar(f5w[:, :], f5h[:, :], 1536.0, None,
                                     OP.subtract)
                vector.tensor_tensor(hch[0][:, :], f5v[:, :], f5w[:, :],
                                     OP.subtract)
                vector.scalar_tensor_tensor(f5v[:, :], hch[0][:, :], 0.5,
                                            f5w[:, :], OP.is_gt, OP.add)
                vector.scalar_tensor_tensor(f5w[:, :], hch[0][:, :], -0.5,
                                            f5v[:, :], OP.is_ge, OP.add)
                return vector.tensor_scalar(f5v[:, :], f5w[:, :], 1.0, None,
                                            OP.subtract)
            step("vector", qs)

            transpose_batch([(f5v[:, k * 128:(k + 1) * 128],
                              f5qs[:, k * 128:(k + 1) * 128])
                             for k in range(KDN)])

            def stq5(gpsimd, c=c):
                return dma(
                    gpsimd, "gpsimd",
                    qh_scr.ap()[:, :, c * 128:(c + 1) * 128].rearrange(
                        "k p q -> p k q"),
                    f5qs[:, :].rearrange("p (k q) -> p k q", k=KDN))
            step("gpsimd", stq5, dmas=1)

        f5a.close()
        barrier()

        f5b = ExitStack()
        qhb = f5b.enter_context(nc.sbuf_tensor("qhb", [128, KDN * 512], BF16))
        qwdb = [f5b.enter_context(nc.sbuf_tensor(f"qwdb{j}",
                                                 [128, KDN * 128], BF16))
                for j in range(2)]
        s2bc = f5b.enter_context(nc.sbuf_tensor("s2bc", [128, R], F32))
        f5d = [f5b.enter_context(nc.sbuf_tensor(f"f5d{j}", [128, 512], F32))
               for j in range(2)]
        gtm = f5b.enter_context(nc.sbuf_tensor("gtm", [128, 4 * 2048], F32))
        xr2 = [f5b.enter_context(nc.sbuf_tensor(f"xr2{j}", [128, 2048], F32))
               for j in range(2)]
        sr2 = [f5b.enter_context(nc.sbuf_tensor(f"sr2{j}", [128, 2048], F32))
               for j in range(2)]
        f5row = f5b.enter_context(nc.sbuf_tensor("f5row", [1, 128], F32))

        @block.sync
        def _(sync):
            for f in range(4):
                for m in range(MDN):
                    it = f * MDN + m
                    if it >= 2:
                        sync.wait_ge(sWD, it - 1)
                    sync.dma_start(
                        out=qwdb[it % 2][:, :].rearrange("p (k q) -> p k q",
                                                         k=KDN),
                        in_=qwd_scr.ap()[:, m].rearrange("k p q -> p k q"),
                    ).then_inc(dsem["sync"], 16)
                    dcnt["sync"] += 16
        wd_base = dcnt["sync"] - 16 * 4 * MDN

        for i in range(NT):
            bcast_col(sb_s2c[:, i:i + 1], s2bc[:, i * 128:(i + 1) * 128],
                      f5row)

        NID = 4 * MDN
        sdmm = e(nc.semaphore("sdmm"))
        sdep = e(nc.semaphore("sdep"))
        sdsi = e(nc.semaphore("sdsi"))
        sdtr = e(nc.semaphore("sdtr"))
        sdte = e(nc.semaphore("sdte"))
        sdrs = e(nc.semaphore("sdrs"))
        adbase = dcnt["scalar"]

        @block.scalar
        def _(scalar):
            for f in range(4):
                if f >= 1:
                    scalar.wait_ge(sdmm, f * MDN)
                scalar.dma_start(
                    out=qhb[:, :].rearrange("p (k q) -> p k q", k=KDN),
                    in_=qh_scr.ap()[:, :, f * 512:(f + 1) * 512]
                    .rearrange("k p q -> p k q"),
                ).then_inc(dsem["scalar"], 16)
                dcnt["scalar"] += 16
                for it in range(f * MDN, (f + 1) * MDN):
                    scalar.wait_ge(sdep, it + 1)
                    scalar.activation(f5d[it % 2][:, :], f5d[it % 2][:, :],
                                      ACT.Silu).then_inc(sdsi, 1)
                for q in range(4):
                    r = f * 4 + q
                    cch = f * 4 + q
                    if r >= 2:
                        scalar.wait_ge(sOut, 16 * (r - 1))
                    scalar.dma_start(
                        out=xr2[q % 2][:, :],
                        in_=x_own.ap()[cch * 128:(cch + 1) * 128, :],
                    ).then_inc(dsem["scalar"], 16)
                    scalar.dma_start(
                        out=sr2[q % 2][:, :],
                        in_=s_scr.ap()[cch]).then_inc(dsem["scalar"], 16)
                    dcnt["scalar"] += 32

        @block.tensor
        def _(tensor):
            def trd(j):
                tensor.wait_ge(sdsi, j + 1)
                if j >= 1:
                    tensor.wait_ge(sdte, j)
                for q in range(4):
                    tensor.transpose(pb[2][:, q * 128:(q + 1) * 128],
                                     f5d[j % 2][:, q * 128:(q + 1) * 128],
                                     sb_ident[:, :])
                tensor.drain()
                tensor.nop().then_inc(sdtr, 1)

            for it in range(NID):
                f, m = it // MDN, it % MDN
                if it >= 1:
                    trd(it - 1)
                if m == 0:
                    tensor.wait_ge(dsem["scalar"], adbase + 144 * f + 16)
                tensor.wait_ge(dsem["sync"], wd_base + 16 * (it + 1))
                if it >= 2:
                    tensor.wait_ge(sdep, it - 1)
                last = None
                for k in range(KDN):
                    last = tensor.matmul(
                        pb[it % 2][:, :],
                        qwdb[it % 2][:, k * 128:(k + 1) * 128],
                        qhb[:, k * 512:(k + 1) * 512],
                        start=(k == 0), stop=(k == KDN - 1))
                last.then_inc(sdmm, 1)
            trd(NID - 1)

        @block.vector
        def _(vector):
            def tev(j):
                jm = j % MDN
                vector.wait_ge(sdtr, j + 1)
                last = None
                for q in range(4):
                    last = vector.tensor_copy(
                        gtm[:, q * 2048 + jm * 128:q * 2048 + (jm + 1) * 128],
                        pb[2][:, q * 128:(q + 1) * 128])
                last.then_inc(sdte, 1)
                if (j + 1) % MDN == 0:
                    fdone = j // MDN
                    for q in range(4):
                        r = fdone * 4 + q
                        vector.wait_ge(dsem["scalar"],
                                       adbase + 144 * fdone + 48 + 32 * q)
                        vector.tensor_tensor(
                            sr2[q % 2][:, :], sr2[q % 2][:, :],
                            gtm[:, q * 2048:(q + 1) * 2048], OP.mult)
                        vector.tensor_tensor(
                            sr2[q % 2][:, :], xr2[q % 2][:, :],
                            sr2[q % 2][:, :], OP.add).then_inc(sdrs, 1)

            for it in range(NID):
                f = it // MDN
                if it >= 1:
                    tev(it - 1)
                vector.wait_ge(sdmm, it + 1)
                ins = vector.tensor_tensor(
                    f5d[it % 2][:, :], pb[it % 2][:, :],
                    s2bc[:, f * 512:f * 512 + 512], OP.mult)
                ins.then_inc(sWD, 1)
                vector.tensor_scalar(f5d[it % 2][:, :], f5d[it % 2][:, :],
                                     sb_wsum[:, 2:3], None,
                                     OP.mult).then_inc(sdep, 1)
            tev(NID - 1)

        @block.gpsimd
        def _(gpsimd):
            for r in range(16):
                cch = r
                gpsimd.wait_ge(sdrs, r + 1)
                gpsimd.dma_start(
                    out=out.ap()[cch * 128:(cch + 1) * 128, :],
                    in_=sr2[r % 2][:, :]).then_inc(sOut, 16)

        prev[0] = None
        f5b.close()

        @block.gpsimd
        def _(gpsimd):
            gpsimd.wait_ge(sOut, 16 * NT)

        @block.sync
        def _(sync):
            sync.wait_ge(sOut, 16 * NT)

    return nc


_NC_CACHE = [None]


def kernel(**inputs):
    import ml_dtypes

    def tile4(w):
        K, M = w.shape[0] // 128, w.shape[1] // 128
        return np.ascontiguousarray(
            w.reshape(K, 128, M, 128).transpose(0, 2, 1, 3))

    x = np.asarray(inputs["L0_attn_x"], np.float32)
    w_in = np.asarray(inputs["L0_attn_W_in"], np.float32)
    conv_w = np.asarray(inputs["L0_attn_conv_w"], np.float32)
    conv_b = np.asarray(inputs["L0_attn_conv_b"], np.float32)
    dt_bias = np.asarray(inputs["L0_attn_dt_bias"], np.float32)
    a_log = np.asarray(inputs["L0_attn_A_log"], np.float32)
    d_vec = np.asarray(inputs["L0_attn_D"], np.float32)
    norm_w = np.asarray(inputs["L0_attn_norm_w"], np.float32)
    w_out = np.asarray(inputs["L0_attn_W_out"], np.float32)
    w_gate = np.asarray(inputs["L0_ffn_w_gate"], np.float32)
    w_up = np.asarray(inputs["L0_ffn_w_up"], np.float32)
    w_down = np.asarray(inputs["L0_ffn_w_down"], np.float32)
    ln_w = np.asarray(inputs["L0_ffn_ln_weight"], np.float32)

    w_in_t = np.zeros((DM, M_IN * 128), np.float32)
    w_in_t[:, :8384] = w_in.T
    w_in_t = tile4(w_in_t)
    w_out_t = tile4(np.ascontiguousarray(w_out.T))
    w_gate_t = tile4(np.ascontiguousarray(w_gate.T))
    w_up_t = tile4(np.ascontiguousarray(w_up.T))
    w_down_t = tile4(np.ascontiguousarray(w_down.T))

    cw_cols = np.ascontiguousarray(
        conv_w.reshape(33, 128, 4).transpose(1, 0, 2).reshape(128, 132))
    cb_cols = np.ascontiguousarray(conv_b.reshape(33, 128).T)
    normw_cols = np.ascontiguousarray(norm_w.reshape(KWO, 128).T)
    lnw_bcast = np.ascontiguousarray(np.broadcast_to(ln_w, (128, IFF)))
    d_bcast = np.ascontiguousarray(np.broadcast_to(d_vec, (128, H)))
    identity = np.eye(128, dtype=np.float32)
    identityb = np.eye(128, dtype=ml_dtypes.bfloat16)
    mneg = np.where(np.arange(128)[None, :] >= np.arange(128)[:, None],
                    0.0, -1e9).astype(np.float32)
    ones_row = np.ones((1, 128), np.float32)
    ones_col = np.ones((128, 1), np.float32)
    sel = np.zeros((8, 8, 128), np.float32)
    for j in range(8):
        sel[j, j, :] = 1.0
    cconsts = np.zeros((128, 4), np.float32)
    cconsts[:, C_NEGHALF] = -0.5

    common = dict(
        w_in_t=w_in_t, w_out_t=w_out_t, w_gate_t=w_gate_t, w_up_t=w_up_t,
        w_down_t=w_down_t, cw_cols=cw_cols, cb_cols=cb_cols,
        normw_cols=normw_cols, lnw_bc=lnw_bcast,
        dtb_col=dt_bias.reshape(64, 1), alog_col=a_log.reshape(64, 1),
        d_bc=d_bcast, ident=identity, identb=identityb, maskneg=mneg,
        ones_r=ones_row, ones_c=ones_col, selmat=sel, cconst=cconsts)

    in_maps = []
    for c in range(NCORE):
        b, half = c // 2, c % 2
        x_own_c = np.ascontiguousarray(x[b, half * R:(half + 1) * R])
        if half == 0:
            x_halo_c = np.zeros((3, DM), np.float32)
        else:
            x_halo_c = np.ascontiguousarray(x[b, R - 3:R])
        smask_c = np.full((64, 1), 1.0 if half == 1 else 0.0, np.float32)
        in_maps.append(dict(common, x_own=x_own_c, x_halo=x_halo_c,
                            smask=smask_c))

    if _NC_CACHE[0] is None:
        _NC_CACHE[0] = build_program()
    res = run_bass_kernel_spmd(_NC_CACHE[0], in_maps,
                               core_ids=list(range(NCORE)))

    full = np.empty((B, T, DM), np.float32)
    for c in range(NCORE):
        b, half = c // 2, c % 2
        full[b, half * R:(half + 1) * R] = np.asarray(res.results[c]["out"])
    return full


# version 08
# speedup vs baseline: 1.0495x; 1.0205x over previous
import numpy as np
from contextlib import ExitStack

import concourse.bass as bass
import concourse.mybir as mybir
from concourse.bass_utils import run_bass_kernel_spmd

F32 = mybir.dt.float32
F16 = mybir.dt.float16
BF16 = mybir.dt.bfloat16
AX = mybir.AxisListType.X
OP = mybir.AluOpType
ACT = mybir.ActivationFunctionType

B, T, DM = 4, 4096, 2048
R = 2048
NT = 16
RH = R + 3
DIN = 4096
H = 64
KD = 16
M_IN = 66
IFF = 8192
MFF = 64
KDN = 64
MDN = 16
KWO = 32
MWO = 16
NCORE = 8
FB2 = [(0, 512), (512, 512), (1024, 512), (1536, 512), (2048, 3)]

C_NEGHALF = 0

DBG = False


def build_program():
    nc = bass.Bass("TRN2", target_bir_lowering=False,
                   detect_race_conditions=False)

    def dram_in(name, shape, dtype=F32):
        return nc.dram_tensor(name, list(shape), dtype, kind="ExternalInput")

    def dram_scr(name, shape, dtype=F32):
        if DBG:
            return nc.dram_tensor(name, list(shape), dtype,
                                  kind="ExternalOutput")
        return nc.dram_tensor(name, list(shape), dtype)

    x_own = dram_in("x_own", [R, DM])
    x_halo = dram_in("x_halo", [3, DM])
    w_in_t = dram_in("w_in_t", [KD, M_IN, 128, 128])
    w_out_t = dram_in("w_out_t", [KWO, MWO, 128, 128])
    w_gate_t = dram_in("w_gate_t", [KD, MFF, 128, 128])
    w_up_t = dram_in("w_up_t", [KD, MFF, 128, 128])
    w_down_t = dram_in("w_down_t", [KDN, MDN, 128, 128])
    cw_cols = dram_in("cw_cols", [128, 33 * 4])
    cb_cols = dram_in("cb_cols", [128, 33])
    normw_cols = dram_in("normw_cols", [128, KWO])
    lnw_bc = dram_in("lnw_bc", [128, IFF])
    dtb_col = dram_in("dtb_col", [64, 1])
    alog_col = dram_in("alog_col", [64, 1])
    d_bc = dram_in("d_bc", [128, H])
    smask = dram_in("smask", [64, 1])
    ident = dram_in("ident", [128, 128])
    identb = dram_in("identb", [128, 128], BF16)
    maskneg = dram_in("maskneg", [128, 128])
    ones_r = dram_in("ones_r", [1, 128])
    ones_c = dram_in("ones_c", [128, 1])
    selmat = dram_in("selmat", [8, 8, 128])
    sel64 = dram_in("sel64", [64, 64, 128])
    cconst = dram_in("cconst", [128, 4])

    out = nc.dram_tensor("out", [R, DM], F32, kind="ExternalOutput")

    z_scr = dram_scr("z_scr", [32, 128, R])
    xh_scr = dram_scr("xh_scr", [NT, 128, DIN])
    bc_scr = dram_scr("bc_scr", [NT, 128, 128])
    qx_scr = dram_scr("qx_scr", [KD, 128, R], BF16)
    h_scr = dram_scr("h_scr", [NT, 128, IFF])
    qh_scr = dram_scr("qh_scr", [KDN, 128, R], BF16)
    qwd_scr = dram_scr("qwd_scr", [KDN, MDN, 128, 128], BF16)
    yr_scr = dram_scr("yr_scr", [NT, KWO, 128, 128])
    s_scr = dram_scr("s_scr", [NT, 128, DM])
    sloc_scr = nc.dram_tensor("sloc_scr", [NT, 64, DIN], F32)
    sloc_dbg = (nc.dram_tensor("sloc_dbg", [NT, 64, DIN], F32,
                               kind="ExternalOutput") if DBG else None)
    cc_out = nc.dram_tensor("cc_out", [128, DIN], F32)

    ctx = ExitStack()
    with ctx:
        e = ctx.enter_context

        sb_ident = e(nc.sbuf_tensor("sb_ident", [128, 128], F32))
        sb_identb = e(nc.sbuf_tensor("sb_identb", [128, 128], BF16))
        sb_ones_r = e(nc.sbuf_tensor("sb_ones_r", [1, 128], F32))
        sb_ones_c = e(nc.sbuf_tensor("sb_ones_c", [128, 1], F32))
        sb_cconst = e(nc.sbuf_tensor("sb_cconst", [128, 4], F32))
        sb_cw = e(nc.sbuf_tensor("sb_cw", [128, 33 * 4], F32))
        sb_cb = e(nc.sbuf_tensor("sb_cb", [128, 33], F32))
        sb_normw = e(nc.sbuf_tensor("sb_normw", [128, KWO], F32))
        sb_dtb = e(nc.sbuf_tensor("sb_dtb", [64, 1], F32))
        sb_alog = e(nc.sbuf_tensor("sb_alog", [64, 1], F32))
        sb_dbc = e(nc.sbuf_tensor("sb_dbc", [128, H], F32))
        sb_smask = e(nc.sbuf_tensor("sb_smask", [64, 1], F32))
        sb_acol = e(nc.sbuf_tensor("sb_acol", [64, 1], F32))
        sb_dt = e(nc.sbuf_tensor("sb_dt", [64, R], F32))
        sb_lall = e(nc.sbuf_tensor("sb_lall", [64, R], F32))
        sb_lat = e(nc.sbuf_tensor("sb_lat", [64, 128], F32))
        sb_zero64 = e(nc.sbuf_tensor("sb_zero64", [64, 128], F32))
        sb_lcum = e(nc.sbuf_tensor("sb_lcum", [64, 2], F32))
        sb_lcum_pre = e(nc.sbuf_tensor("sb_lcum_pre", [64, NT], F32))
        sb_sxc = e(nc.sbuf_tensor("sb_sxc", [128, NT], F32))
        sb_s2c = e(nc.sbuf_tensor("sb_s2c", [128, NT], F32))
        sb_swinv = e(nc.sbuf_tensor("sb_swinv", [128, 4], F32))
        sb_wsum = e(nc.sbuf_tensor("sb_wsum", [128, 4], F32))
        sb_gam = e(nc.sbuf_tensor("sb_gam", [128, 64], F32))

        pb = [e(nc.psum_tensor(f"pb{i}", [128, 512], F32)) for i in range(6)]
        pbh = e(nc.psum_tensor("pbh", [128, 1024], BF16))

        block = e(nc.Block())

        ENGS = ("sync", "vector", "scalar", "tensor", "gpsimd")
        esem = {n: e(nc.semaphore(f"se_{n}")) for n in ENGS}
        ecnt = {n: 0 for n in ENGS}
        dsem = {n: e(nc.semaphore(f"sd_{n}")) for n in
                ("sync", "scalar", "gpsimd")}
        dcnt = {n: 0 for n in ("sync", "scalar", "gpsimd")}
        sB = e(nc.semaphore("sBar"))
        sC = e(nc.semaphore("sColl"))
        sW2 = e(nc.semaphore("sW2"))
        sW4 = e(nc.semaphore("sW4"))
        sWC = e(nc.semaphore("sWC"))
        sWP = e(nc.semaphore("sWP"))
        sWD = e(nc.semaphore("sWD"))
        sOut = e(nc.semaphore("sOut"))
        nbar = [0]
        prev = [None]

        def step(ename, fn, dmas=0):
            @getattr(block, ename)
            def _(eng):
                if prev[0] is not None:
                    sem, val = prev[0]
                    eng.wait_ge(sem, val)
                last = fn(eng)
                if dmas:
                    dcnt[ename] += 16 * dmas
                    prev[0] = (dsem[ename], dcnt[ename])
                else:
                    if last is None:
                        eng.drain()
                        last = eng.nop()
                    ecnt[ename] += 1
                    last.then_inc(esem[ename], 1)
                    prev[0] = (esem[ename], ecnt[ename])

        def sfence(eng, ename, ins):
            ecnt[ename] += 1
            ins.then_inc(esem[ename], 1)
            eng.wait_ge(esem[ename], ecnt[ename])

        def dma(eng, ename, out_ap, in_ap):
            ins = eng.dma_start(out=out_ap, in_=in_ap)
            ins.then_inc(dsem[ename], 16)
            return ins

        def barrier():
            nbar[0] += 1
            tgt = 5 * nbar[0]
            for name in ENGS:
                @getattr(block, name)
                def _(eng, tgt=tgt):
                    if prev[0] is not None:
                        sem, val = prev[0]
                        eng.wait_ge(sem, val)
                    eng.drain()
                    eng.sem_inc(sB, 1)
                    eng.wait_ge(sB, tgt)
            prev[0] = None

        def transpose_batch(pairs, bf=False):
            W = 8
            for w0 in range(0, len(pairs), W):
                wave = pairs[w0:w0 + W]

                def tr(tensor, wave=wave):
                    for j, (src, dst) in enumerate(wave):
                        P, Q = src.shape
                        if bf:
                            tensor.transpose(pbh[:Q, j * 128:j * 128 + P],
                                             src, sb_identb[:P, :P])
                        else:
                            bank = pb[2 + j // 4]
                            slot = (j % 4) * 128
                            tensor.transpose(bank[:Q, slot:slot + P], src,
                                             sb_ident[:P, :P])
                    return None
                step("tensor", tr)

                def ev(vector, wave=wave):
                    last = None
                    for j, (src, dst) in enumerate(wave):
                        P, Q = src.shape
                        if bf:
                            last = vector.tensor_copy(
                                dst, pbh[:Q, j * 128:j * 128 + P])
                        else:
                            bank = pb[2 + j // 4]
                            slot = (j % 4) * 128
                            last = vector.tensor_copy(
                                dst, bank[:Q, slot:slot + P])
                    return last
                step("vector", ev)

        def bcast_col(col_ap, dst_ap, tmprow, P=128):
            def t1(tensor):
                tensor.transpose(pb[2][:1, 0:128], col_ap, sb_ident[:, :])
                return None
            step("tensor", t1)

            def v1(vector):
                return vector.tensor_copy(tmprow[:1, 0:128], pb[2][:1, 0:128])
            step("vector", v1)

            def t2(tensor, P=P):
                tensor.matmul(pb[3][:P, 0:128], sb_ones_r[:, :P],
                              tmprow[:1, 0:128], start=True, stop=True)
                return None
            step("tensor", t2)

            def v2(vector, P=P):
                return vector.tensor_copy(dst_ap, pb[3][:P, 0:128])
            step("vector", v2)

        def p0(sync):
            last = None
            for dst, src in [
                    (sb_ident[:], ident.ap()), (sb_identb[:], identb.ap()),
                    (sb_ones_r[:], ones_r.ap()),
                    (sb_ones_c[:], ones_c.ap()), (sb_cconst[:], cconst.ap()),
                    (sb_cw[:], cw_cols.ap()), (sb_cb[:], cb_cols.ap()),
                    (sb_normw[:], normw_cols.ap()), (sb_dtb[:], dtb_col.ap()),
                    (sb_alog[:], alog_col.ap()), (sb_dbc[:], d_bc.ap()),
                    (sb_smask[:], smask.ap())]:
                last = dma(sync, "sync", dst, src)
            return last
        step("sync", p0, dmas=12)

        def p0v(vector):
            vector.memset(sb_zero64[:], 0.0)
            vector.memset(sb_lcum[:], 0.0)
            return vector.memset(sb_wsum[:], 0.0)
        step("vector", p0v)

        def p0s(scalar):
            return scalar.activation(sb_acol[:], sb_alog[:], ACT.Exp)
        step("scalar", p0s)

        def p0v2(vector):
            return vector.tensor_scalar(sb_acol[:], sb_acol[:], -1.0, None,
                                        OP.mult)
        step("vector", p0v2)

        f12 = ExitStack()
        xnfm = [f12.enter_context(nc.sbuf_tensor(f"xnfm{k}", [128, RH], F32))
                for k in range(KD)]
        f1 = ExitStack()
        xtile = f1.enter_context(nc.sbuf_tensor("xtile", [128, DM], F32))
        f1w = f1.enter_context(nc.sbuf_tensor("f1w", [128, DM], F32))
        f1q = f1.enter_context(nc.sbuf_tensor("f1q", [128, DM], BF16))
        f1h = f1.enter_context(nc.sbuf_tensor("f1h", [128, DM], F16))
        f1qs = f1.enter_context(nc.sbuf_tensor("f1qs", [128, DM], BF16))
        f1v = f1.enter_context(nc.sbuf_tensor("f1v", [128, DM], F32))
        f1r = f1.enter_context(nc.sbuf_tensor("f1r", [128, DM], F32))
        f1c = f1.enter_context(nc.sbuf_tensor("f1c", [128, 8], F32))

        for i in range(NT + 1):
            P = 128 if i < NT else 3

            def ld(scalar, i=i, P=P):
                src = (x_own.ap()[i * 128:(i + 1) * 128, :] if i < NT
                       else x_halo.ap()[:, :])
                return dma(scalar, "scalar", xtile[:P, :], src)
            step("scalar", ld, dmas=1)

            def v1(vector, P=P):
                ins = vector.reduce_sum(f1c[:P, 0:1], xtile[:P, :], AX,
                                        negate=True)
                sfence(vector, "vector", ins)
                ins = vector.tensor_scalar(f1c[:P, 1:2], f1c[:P, 0:1],
                                           1.0 / DM, None, OP.mult)
                sfence(vector, "vector", ins)
                return vector.tensor_scalar(f1w[:P, :], xtile[:P, :],
                                            f1c[:P, 1:2], None, OP.add)
            step("vector", v1)

            def s1(scalar, P=P):
                return scalar.activation(xtile[:P, :], f1w[:P, :], ACT.Square,
                                         accum_out=f1c[:P, 2:3])
            step("scalar", s1)

            def v2(vector, P=P):
                return vector.tensor_scalar(f1c[:P, 3:4], f1c[:P, 2:3],
                                            1.0 / DM, 1e-5, OP.mult, OP.add)
            step("vector", v2)

            def s2(scalar, P=P):
                scalar.activation(f1c[:P, 4:5], f1c[:P, 3:4], ACT.Ln)
                return scalar.activation(
                    f1c[:P, 5:6], f1c[:P, 4:5], ACT.Exp,
                    scale=sb_cconst[:P, C_NEGHALF:C_NEGHALF + 1])
            step("scalar", s2)

            def v3(vector, i=i, P=P):
                vector.tensor_scalar(f1w[:P, :], f1w[:P, :], f1c[:P, 5:6],
                                     None, OP.mult)
                if P == 128:
                    ins = vector.reduce_max(f1c[:P, 6:7], f1w[:P, :], AX,
                                            apply_absolute_value=True)
                    sfence(vector, "vector", ins)
                    vector.tensor_scalar(f1c[:P, 6:7], f1c[:P, 6:7], 1e-5,
                                         None, OP.max)
                    ins = vector.reciprocal(f1c[:P, 7:8], f1c[:P, 6:7])
                    sfence(vector, "vector", ins)
                    vector.tensor_scalar(f1c[:P, 7:8], f1c[:P, 7:8], 127.0,
                                         None, OP.mult)
                    ins = vector.tensor_scalar(sb_sxc[:P, i:i + 1],
                                               f1c[:P, 6:7], 1.0 / 127.0,
                                               None, OP.mult)
                    sfence(vector, "vector", ins)
                    vector.tensor_scalar(f1v[:P, :], f1w[:P, :], f1c[:P, 7:8],
                                         None, OP.mult)
                    vector.tensor_scalar(f1h[:P, :], f1v[:P, :], 1536.0,
                                         None, OP.add)
                    vector.tensor_scalar(f1r[:P, :], f1h[:P, :], 1536.0,
                                         None, OP.subtract)
                    vector.tensor_tensor(xtile[:P, :], f1v[:P, :],
                                         f1r[:P, :], OP.subtract)
                    vector.scalar_tensor_tensor(f1v[:P, :], xtile[:P, :], 0.5,
                                                f1r[:P, :], OP.is_gt, OP.add)
                    vector.scalar_tensor_tensor(f1r[:P, :], xtile[:P, :],
                                                -0.5, f1v[:P, :], OP.is_ge,
                                                OP.add)
                    return vector.tensor_scalar(f1q[:P, :], f1r[:P, :], 1.0,
                                                None, OP.subtract)
                return None
            step("vector", v3)

            col = 3 + i * 128 if i < NT else 0
            transpose_batch([(f1w[:P, k * 128:(k + 1) * 128],
                              xnfm[k][:, col:col + P]) for k in range(KD)])

            if i < NT:
                transpose_batch([(f1q[:, k * 128:(k + 1) * 128],
                                  f1qs[:, k * 128:(k + 1) * 128])
                                 for k in range(KD)], bf=True)

                def st(gpsimd, i=i):
                    return dma(
                        gpsimd, "gpsimd",
                        qx_scr.ap()[:, :, i * 128:(i + 1) * 128].rearrange(
                            "k p q -> p k q"),
                        f1qs[:, :].rearrange("p (k q) -> p k q", k=KD))
                step("gpsimd", st, dmas=1)

        f1.close()
        barrier()

        f2 = ExitStack()
        wbuf = [f2.enter_context(nc.sbuf_tensor(f"wbuf{j}", [128, KD * 128],
                                                F32)) for j in range(2)]
        f2s0 = f2.enter_context(nc.sbuf_tensor("f2s0", [128, RH], F32))
        f2s = [f2s0, f2s0]
        f2c = [f2.enter_context(nc.sbuf_tensor(f"f2c{j}", [128, R], F32))
               for j in range(2)]
        f2t = f2.enter_context(nc.sbuf_tensor("f2t", [128, 2048], F32))
        NPAW = 4
        paw = f2.enter_context(nc.sbuf_tensor("paw", [128, NPAW * 128], F32))

        pa_srcs = []
        for wi, wt in enumerate((w_gate_t, w_up_t, w_down_t)):
            flat = wt.ap().rearrange("a b p q -> (a b) p q")
            ntile = KD * MFF if wi < 2 else KDN * MDN
            for j in range(0, ntile, NPAW):
                pa_srcs.append((wi, flat[j:j + NPAW]))

        FB2Z = [(3, 512), (515, 512), (1027, 512), (1539, 512)]

        def f2_blocks(m):
            return FB2 if 32 <= m <= 64 else FB2Z
        mmcum = [0]
        for _m in range(M_IN):
            mmcum.append(mmcum[-1] + len(f2_blocks(_m)))
        bank_of = {}
        for _m in range(M_IN):
            _rot = [0, 1, 4, 5] if _m < 32 else [0, 1]
            for _fi in range(len(f2_blocks(_m))):
                bank_of[mmcum[_m] + _fi] = _rot[_fi % len(_rot)]
        s2mm = e(nc.semaphore("s2mm"))
        s2ev = e(nc.semaphore("s2ev"))
        s2cv = e(nc.semaphore("s2cv"))
        s2si = e(nc.semaphore("s2si"))
        s2tr = e(nc.semaphore("s2tr"))
        s2te = e(nc.semaphore("s2te"))
        sPAr = e(nc.semaphore("sPAr"))
        sPAv = e(nc.semaphore("sPAv"))
        NPA = len(pa_srcs)
        g_after = {}
        pa_after = {}
        gc = [dcnt["gpsimd"]]
        _gcc = gc[0]
        _pa = 0
        for _m in range(M_IN):
            if _m < 32:
                _gcc += 16
                g_after[("z", _m)] = _gcc
            elif 33 <= _m <= 64:
                _gcc += 16
                g_after[("xh", _m - 33)] = _gcc
            elif _m == 65:
                _gcc += 16
                g_after[("bc", 0)] = _gcc
            while _pa < min(NPA, (NPA * (_m + 1)) // M_IN):
                _gcc += 16
                pa_after[_pa] = _gcc
                _pa += 1

        @block.sync
        def _(sync):
            for m in range(M_IN):
                if m >= 2:
                    sync.wait_ge(s2mm, mmcum[m - 1])
                sync.dma_start(
                    out=wbuf[m % 2][:, :].rearrange("p (k q) -> p k q", k=KD),
                    in_=w_in_t.ap()[:, m].rearrange("k p q -> p k q"),
                ).then_inc(dsem["sync"], 16)
                dcnt["sync"] += 16
        w2_base = dcnt["sync"] - 16 * M_IN

        last_mm = {}

        @block.tensor
        def _(tensor):
            for m in range(M_IN):
                if 33 <= m <= 64:
                    mc1 = m - 33
                    tensor.wait_ge(s2si, mc1 + 1)
                    if mc1 >= 1:
                        tensor.wait_ge(s2te, mc1)
                    else:
                        need = max(last_mm.get(4, -1), last_mm.get(5, -1))
                        if need >= 0:
                            tensor.wait_ge(s2ev, need + 1)
                    for j in range(16):
                        bank = pb[2 + j // 4]
                        slot = (j % 4) * 128
                        tensor.transpose(
                            bank[:, slot:slot + 128],
                            f2c[(m - 1) % 2][:, j * 128:(j + 1) * 128],
                            sb_ident[:, :])
                    tensor.drain()
                    tensor.nop().then_inc(s2tr, 1)
                for fi, (f0, fw) in enumerate(f2_blocks(m)):
                    gidx = mmcum[m] + fi
                    b = bank_of[gidx]
                    if fi == 0:
                        tensor.wait_ge(dsem["sync"], w2_base + 16 * (m + 1))
                    if b in last_mm:
                        tensor.wait_ge(s2ev, last_mm[b] + 1)
                    if m == 65 and b in (4, 5):
                        tensor.wait_ge(s2te, 32)
                    last = None
                    for k in range(KD):
                        last = tensor.matmul(
                            pb[b][:, 0:fw],
                            wbuf[m % 2][:, k * 128:(k + 1) * 128],
                            xnfm[k][:, f0:f0 + fw],
                            start=(k == 0), stop=(k == KD - 1))
                    last.then_inc(s2mm, 1)
                    last_mm[b] = gidx

        pa_v = [0]

        @block.vector
        def _(vector):
            for m in range(M_IN):
                if 33 <= m <= 64:
                    mc1 = m - 33
                    vector.wait_ge(s2tr, mc1 + 1)
                    if mc1 >= 1:
                        vector.wait_ge(dsem["gpsimd"],
                                       g_after[("xh", mc1 - 1)])
                    for j in range(4):
                        ins = vector.tensor_copy(
                            f2t[:, j * 512:(j + 1) * 512], pb[2 + j][:, 0:512])
                    ins.then_inc(s2te, 1)
                P_EV = 128 if m < 65 else 64
                for fi, (f0, fw) in enumerate(f2_blocks(m)):
                    gidx = mmcum[m] + fi
                    vector.wait_ge(s2mm, gidx + 1)
                    if fi == 0 and m >= 1 and (m - 1) < 32:
                        vector.wait_ge(dsem["gpsimd"], g_after[("z", m - 1)])
                    vector.tensor_copy(f2s0[:P_EV, f0:f0 + fw],
                                       pb[bank_of[gidx]][:P_EV, 0:fw]
                                       ).then_inc(s2ev, 1)
                if 32 <= m < 65:
                    mc = m - 32
                    if mc >= 2 and mc - 2 < 32:
                        vector.wait_ge(s2tr, mc - 1)
                    vector.tensor_scalar(f2c[m % 2][:, :],
                                         f2s[m % 2][:, 0:R],
                                         sb_cw[:, mc * 4:mc * 4 + 1], None,
                                         OP.mult)
                    last = None
                    for k in (1, 2, 3):
                        last = vector.scalar_tensor_tensor(
                            f2c[m % 2][:, :], f2s[m % 2][:, k:k + R],
                            sb_cw[:, mc * 4 + k:mc * 4 + k + 1],
                            f2c[m % 2][:, :], OP.mult, OP.add)
                    last.then_inc(s2cv, 1)
                while pa_v[0] < min(NPA, (NPA * (m + 1)) // M_IN):
                    i = pa_v[0]
                    pa_v[0] += 1
                    wi = pa_srcs[i][0]
                    vector.wait_ge(sPAr, i + 1)
                    vector.tensor_tensor(sb_wsum[:, wi:wi + 1],
                                         sb_wsum[:, wi:wi + 1],
                                         sb_wsum[:, 3:4],
                                         OP.add).then_inc(sPAv, 1)

        pa_s = [0]

        @block.scalar
        def _(scalar):
            def pa_emit_s(upto):
                while pa_s[0] < min(NPA, upto):
                    i = pa_s[0]
                    pa_s[0] += 1
                    scalar.wait_ge(dsem["gpsimd"], pa_after[i])
                    if i >= 1:
                        scalar.wait_ge(sPAv, i)
                    scalar.activation(paw[:, :], paw[:, :], ACT.Abs,
                                      accum_out=sb_wsum[:, 3:4]).then_inc(
                                          sPAr, 1)

            for m in range(M_IN):
                if 32 <= m < 65:
                    mc = m - 32
                    scalar.wait_ge(s2cv, mc + 1)
                    scalar.activation(f2c[m % 2][:, :],
                                      f2c[m % 2][:, :], ACT.Silu,
                                      bias=sb_cb[:, mc:mc + 1]).then_inc(
                                          s2si, 1)
                pa_emit_s((NPA * (m + 1)) // M_IN)
            pa_emit_s(NPA)

        pa_g = [0]

        @block.gpsimd
        def _(gpsimd):
            def pa_emit(upto):
                while pa_g[0] < min(NPA, upto):
                    i = pa_g[0]
                    pa_g[0] += 1
                    if i >= 1:
                        gpsimd.wait_ge(sPAr, i)
                    wi, src = pa_srcs[i]
                    gpsimd.dma_start(
                        out=paw[:, :].rearrange("p (n q) -> p n q", n=NPAW),
                        in_=src.rearrange("n p q -> p n q"),
                    ).then_inc(dsem["gpsimd"], 16)
                    gc[0] += 16
                    assert pa_after[i] == gc[0]

            for m in range(M_IN):
                if m < 32:
                    gpsimd.wait_ge(s2ev, mmcum[m + 1])
                    gpsimd.dma_start(out=z_scr.ap()[m],
                                     in_=f2s[m % 2][:, 3:3 + R]).then_inc(
                                         dsem["gpsimd"], 16)
                    gc[0] += 16
                    assert g_after[("z", m)] == gc[0]
                elif 33 <= m <= 64:
                    mc = m - 33
                    gpsimd.wait_ge(s2te, mc + 1)
                    gpsimd.dma_start(
                        out=xh_scr.ap()[:, :, mc * 128:(mc + 1) * 128]
                        .rearrange("c p q -> p c q"),
                        in_=f2t[:, :].rearrange("p (c q) -> p c q", c=NT),
                    ).then_inc(dsem["gpsimd"], 16)
                    gc[0] += 16
                    assert g_after[("xh", mc)] == gc[0]
                elif m == 65:
                    gpsimd.wait_ge(s2si, 33)
                    gpsimd.dma_start(
                        out=bc_scr.ap()[:, :, :].rearrange("c p q -> p c q"),
                        in_=f2c[0][:, :].rearrange("p (c q) -> p c q", c=NT),
                    ).then_inc(dsem["gpsimd"], 16)
                    gc[0] += 16
                    assert g_after[("bc", 0)] == gc[0]
                pa_emit((NPA * (m + 1)) // M_IN)
            pa_emit(NPA)
            dcnt["gpsimd"] = gc[0]
            gpsimd.wait_ge(dsem["gpsimd"], gc[0])

        prev[0] = None
        barrier()

        def dtact(scalar):
            scalar.activation(f2c[1][:64, 0:R], f2s0[:64, 3:3 + R],
                              ACT.Exp, bias=sb_dtb[:, 0:1])
            return scalar.activation(sb_dt[:, :], f2c[1][:64, 0:R],
                                     ACT.Ln, bias=1.0)
        step("scalar", dtact)

        def wfin_mm(tensor):
            tensor.matmul(pb[4][:1, 0:3], sb_wsum[:, 0:3], sb_ones_c[:, 0:1],
                          start=True, stop=True)
            return None
        def wfin_mm2(tensor):
            tensor.matmul(pb[4][:1, 0:3], sb_ones_c[:, 0:1], sb_wsum[:, 0:3],
                          start=True, stop=True)
            return None
        step("tensor", wfin_mm2)

        def wfin_v(vector):
            return vector.tensor_scalar(f2s[0][:1, 0:3], pb[4][:1, 0:3],
                                        1.0 / (IFF * DM), 1e-5, OP.mult,
                                        OP.max)
        step("vector", wfin_v)

        def wfin_bc(tensor):
            tensor.matmul(pb[4][:, 64:67], sb_ones_r[:, :], f2s[0][:1, 0:3],
                          start=True, stop=True)
            return None
        step("tensor", wfin_bc)

        def wfin_v2(vector):
            ins = vector.tensor_copy(sb_wsum[:, 0:3], pb[4][:, 64:67])
            sfence(vector, "vector", ins)
            return vector.reciprocal(sb_swinv[:, 0:3], sb_wsum[:, 0:3])
        step("vector", wfin_v2)

        f2.close()
        f12.close()
        barrier()

        f3a = ExitStack()
        sb_state = f3a.enter_context(nc.sbuf_tensor("sb_state", [64, DIN],
                                                    F32))
        xtm = [f3a.enter_context(nc.sbuf_tensor(f"xtm{j}", [128, DIN], F32))
               for j in range(2)]
        bcB = [f3a.enter_context(nc.sbuf_tensor(f"bcB{j}", [64, 128], F32))
               for j in range(2)]
        f3ld = f3a.enter_context(nc.sbuf_tensor("f3ld", [128, 192], F32))
        f3fac = f3a.enter_context(nc.sbuf_tensor("f3fac", [128, 64], F32))
        f3btm = f3a.enter_context(nc.sbuf_tensor("f3btm", [128, 64], F32))
        f3bs = f3a.enter_context(nc.sbuf_tensor("f3bs", [128, 512], F32))
        f3row = f3a.enter_context(nc.sbuf_tensor("f3row", [1, 192], F32))
        f3crep = f3a.enter_context(nc.sbuf_tensor("f3crep", [64, 64], F32))

        def st0(vector):
            return vector.memset(sb_state[:, :], 0.0)
        step("vector", st0)

        def pf0(sync):
            dma(sync, "sync", xtm[0][:, :], xh_scr.ap()[0])
            return dma(sync, "sync", bcB[0][:, :], bc_scr.ap()[0, 0:64])
        step("sync", pf0, dmas=2)

        for c in range(NT):
            cur, nxt = c % 2, (c + 1) % 2
            if c + 1 < NT:
                def pf(sync, c=c, nxt=nxt):
                    dma(sync, "sync", xtm[nxt][:, :], xh_scr.ap()[c + 1])
                    return dma(sync, "sync", bcB[nxt][:, :],
                               bc_scr.ap()[c + 1, 0:64])
                step("sync", pf, dmas=2)

            def cum(vector, c=c):
                vector.tensor_scalar(sb_lat[:, :],
                                     sb_dt[:, c * 128:(c + 1) * 128],
                                     sb_acol[:, 0:1], None, OP.mult)
                ins = vector.tensor_tensor_scan(
                    sb_lall[:, c * 128:(c + 1) * 128],
                    sb_lat[:, :], sb_zero64[:, :],
                    0.0, OP.add, OP.add)
                sfence(vector, "vector", ins)
                vector.tensor_copy(sb_lcum_pre[:, c:c + 1], sb_lcum[:, 0:1])
                return vector.tensor_tensor(
                    sb_lcum[:, 0:1], sb_lcum[:, 0:1],
                    sb_lall[:, c * 128 + 127:c * 128 + 128], OP.add)
            step("vector", cum)

            def trs(tensor, c=c, cur=cur):
                tensor.transpose(pb[2][:128, 0:64],
                                 sb_lall[:, c * 128:(c + 1) * 128],
                                 sb_ident[:64, :64])
                tensor.transpose(pb[2][:128, 64:128],
                                 sb_dt[:, c * 128:(c + 1) * 128],
                                 sb_ident[:64, :64])
                tensor.transpose(pb[2][:128, 128:192], bcB[cur][:, :],
                                 sb_ident[:64, :64])
                tensor.transpose(pb[3][:1, 0:64],
                                 sb_lall[:, c * 128 + 127:c * 128 + 128],
                                 sb_ident[:64, :64])
                return None
            step("tensor", trs)

            def ev1(vector, c=c):
                vector.tensor_copy(f3ld[:, 0:128], pb[2][:, 0:128])
                vector.tensor_copy(f3btm[:, :], pb[2][:, 128:192])
                return vector.tensor_copy(f3row[:1, 0:64], pb[3][:1, 0:64])
            step("vector", ev1)

            def crepe(scalar, c=c):
                return scalar.activation(f3row[:1, 64:128], f3row[:1, 0:64],
                                         ACT.Exp)
            step("scalar", crepe)

            def lbc(tensor, c=c):
                tensor.matmul(pb[3][:, 64:128], sb_ones_r[:, :],
                              f3row[:1, 0:64], start=True, stop=True)
                tensor.matmul(pb[3][0:64, 128:192], sb_ones_r[:, 0:64],
                              f3row[:1, 64:128], start=True, stop=True)
                return None
            step("tensor", lbc)

            def fac(vector, c=c):
                vector.tensor_copy(f3crep[:, :], pb[3][0:64, 128:192])
                return vector.tensor_tensor(f3fac[:, :], pb[3][:, 64:128],
                                            f3ld[:, 0:64], OP.subtract)
            step("vector", fac)

            def face(scalar, c=c):
                return scalar.activation(f3fac[:, :], f3fac[:, :], ACT.Exp)
            step("scalar", face)

            def fac2(vector, c=c):
                return vector.tensor_tensor(f3fac[:, :], f3fac[:, :],
                                            f3ld[:, 64:128], OP.mult)
            step("vector", fac2)

            for g in range(8):
                def bs(vector, g=g, cur=cur):
                    last = None
                    for j in range(8):
                        hh = g * 8 + j
                        last = vector.tensor_scalar(
                            f3bs[:, j * 64:(j + 1) * 64], f3btm[:, 0:64],
                            f3fac[:, hh:hh + 1], None, OP.mult)
                    return last
                step("vector", bs)

                def mm3(tensor, g=g, cur=cur):
                    for j in range(8):
                        hh = g * 8 + j
                        tensor.matmul(
                            pb[4][0:64, j * 64:(j + 1) * 64],
                            f3bs[:, j * 64:(j + 1) * 64],
                            xtm[cur][:, hh * 64:(hh + 1) * 64],
                            start=True, stop=True)
                    return None
                step("tensor", mm3)

                def supd(vector, g=g):
                    last = None
                    for j in range(8):
                        hh = g * 8 + j
                        last = vector.scalar_tensor_tensor(
                            sb_state[:, hh * 64:(hh + 1) * 64],
                            sb_state[:, hh * 64:(hh + 1) * 64],
                            f3crep[:, hh:hh + 1],
                            pb[4][0:64, j * 64:(j + 1) * 64],
                            OP.mult, OP.add)
                    return last
                step("vector", supd)

            def stst(gpsimd, c=c):
                ins = dma(gpsimd, "gpsimd", sloc_scr.ap()[c], sb_state[:, :])
                if DBG:
                    ins = dma(gpsimd, "gpsimd", sloc_dbg.ap()[c],
                              sb_state[:, :])
                return ins
            step("gpsimd", stst, dmas=2 if DBG else 1)

        def coll(gpsimd):
            gpsimd.collective_compute(
                "AllGather", OP.bypass,
                replica_groups=[[2 * i, 2 * i + 1] for i in range(4)],
                ins=[sloc_scr.ap()[NT - 1]],
                outs=[cc_out.ap()[:, :]],
            ).then_inc(sC, 1)
            gpsimd.drain()
            return None
        step("gpsimd", coll)

        f3a.close()
        barrier()

        f4 = ExitStack()
        qxr = [f4.enter_context(nc.sbuf_tensor(f"qxr{k}", [128, R], BF16))
               for k in range(KD)]
        wgb = [f4.enter_context(nc.sbuf_tensor(f"wgb{j}", [128, KD * 128],
                                               F32)) for j in range(2)]
        wub = [f4.enter_context(nc.sbuf_tensor(f"wub{j}", [128, KD * 128],
                                               F32)) for j in range(2)]
        qwg = [f4.enter_context(nc.sbuf_tensor(f"qwg{j}", [128, KD * 128],
                                               BF16)) for j in range(2)]
        qwu = [f4.enter_context(nc.sbuf_tensor(f"qwu{j}", [128, KD * 128],
                                               BF16)) for j in range(2)]
        f4g = [f4.enter_context(nc.sbuf_tensor(f"f4g{j}", [128, 512], F32))
               for j in range(2)]
        f4u = [f4.enter_context(nc.sbuf_tensor(f"f4u{j}", [128, 512], F32))
               for j in range(2)]
        f4h = [f4.enter_context(nc.sbuf_tensor(f"f4h{j}", [128, 512], F32))
               for j in range(2)]
        f4t = [f4.enter_context(nc.sbuf_tensor(f"f4t{j}", [128, 512], F32))
               for j in range(2)]
        f4w = f4.enter_context(nc.sbuf_tensor("f4w", [128, 512], F32))
        f4qs = f4.enter_context(nc.sbuf_tensor("f4qs", [128, 512], F32))
        sxbc = f4.enter_context(nc.sbuf_tensor("sxbc", [128, R], F32))
        f4row = f4.enter_context(nc.sbuf_tensor("f4row", [1, 128], F32))

        @block.sync
        def _(sync):
            for k in range(KD):
                sync.dma_start(out=qxr[k][:, :], in_=qx_scr.ap()[k]).then_inc(
                    dsem["sync"], 16)
                dcnt["sync"] += 16
            for m in range(MFF):
                if m >= 2:
                    sync.wait_ge(sW4, m - 1)
                sync.dma_start(
                    out=wgb[m % 2][:, :].rearrange("p (k q) -> p k q", k=KD),
                    in_=w_gate_t.ap()[:, m].rearrange("k p q -> p k q"),
                ).then_inc(dsem["sync"], 16)
                sync.dma_start(
                    out=wub[m % 2][:, :].rearrange("p (k q) -> p k q", k=KD),
                    in_=w_up_t.ap()[:, m].rearrange("k p q -> p k q"),
                ).then_inc(dsem["sync"], 16)
                dcnt["sync"] += 32
        w4_base = dcnt["sync"] - 32 * MFF

        for i in range(NT):
            bcast_col(sb_sxc[:, i:i + 1], sxbc[:, i * 128:(i + 1) * 128],
                      f4row)

        NI4 = MFF * 4
        s4mm = e(nc.semaphore("s4mm"))
        s4ep = e(nc.semaphore("s4ep"))
        s4si = e(nc.semaphore("s4si"))
        s4hm = e(nc.semaphore("s4hm"))
        s4tr = e(nc.semaphore("s4tr"))
        s4te = e(nc.semaphore("s4te"))
        g4base = dcnt["gpsimd"]

        @block.tensor
        def _(tensor):
            def tr4(j):
                tensor.wait_ge(s4hm, j + 1)
                if j >= 2:
                    tensor.wait_ge(s4te, j - 1)
                for q in range(4):
                    tensor.transpose(pb[2 + j % 2][:, q * 128:(q + 1) * 128],
                                     f4h[j % 2][:, q * 128:(q + 1) * 128],
                                     sb_ident[:, :])
                tensor.drain()
                tensor.nop().then_inc(s4tr, 1)

            for m in range(MFF):
                for f in range(4):
                    it = m * 4 + f
                    if it >= 1:
                        tr4(it - 1)
                    if f == 0:
                        tensor.wait_ge(dsem["sync"], w4_base + 32 * (m + 1))
                        tensor.wait_ge(sW4, m + 1)
                    if it >= 2:
                        tensor.wait_ge(s4ep, it - 1)
                    gb = pb[(it % 2) * 4]
                    ub = pb[1 + (it % 2) * 4]
                    for k in range(KD):
                        tensor.matmul(
                            gb[:, :], qwg[m % 2][:, k * 128:(k + 1) * 128],
                            qxr[k][:, f * 512:(f + 1) * 512],
                            start=(k == 0), stop=(k == KD - 1))
                    last = None
                    for k in range(KD):
                        last = tensor.matmul(
                            ub[:, :], qwu[m % 2][:, k * 128:(k + 1) * 128],
                            qxr[k][:, f * 512:(f + 1) * 512],
                            start=(k == 0), stop=(k == KD - 1))
                    last.then_inc(s4mm, 1)
            tr4(NI4 - 1)

        @block.vector
        def _(vector):
            for m in range(MFF):
                vector.wait_ge(dsem["sync"], w4_base + 32 * (m + 1))
                if m >= 2:
                    vector.wait_ge(s4mm, 4 * (m - 1))
                last = None
                for src, qdst, wi in ((wgb[m % 2], qwg[m % 2], 0),
                                      (wub[m % 2], qwu[m % 2], 1)):
                    for blk in range(4):
                        sl = slice(blk * 512, (blk + 1) * 512)
                        vector.tensor_scalar(f4w[:, :], src[:, sl],
                                             sb_swinv[:, wi:wi + 1], None,
                                             OP.mult)
                        vector.tensor_scalar(f4qs[:, :], f4w[:, :], -0.5,
                                             None, OP.is_le)
                        last = vector.scalar_tensor_tensor(
                            qdst[:, sl], f4w[:, :], 0.5, f4qs[:, :],
                            OP.is_ge, OP.subtract)
                last.then_inc(sW4, 1)
                for f in range(4):
                    it = m * 4 + f
                    vector.wait_ge(s4mm, it + 1)
                    gb = pb[(it % 2) * 4]
                    ub = pb[1 + (it % 2) * 4]
                    vector.tensor_tensor(f4g[it % 2][:, :], gb[:, :],
                                         sxbc[:, f * 512:(f + 1) * 512],
                                         OP.mult)
                    vector.tensor_scalar(f4g[it % 2][:, :], f4g[it % 2][:, :],
                                         sb_wsum[:, 0:1], None, OP.mult)
                    vector.tensor_tensor(f4u[it % 2][:, :], ub[:, :],
                                         sxbc[:, f * 512:(f + 1) * 512],
                                         OP.mult)
                    vector.tensor_scalar(f4u[it % 2][:, :], f4u[it % 2][:, :],
                                         sb_wsum[:, 1:2], None,
                                         OP.mult).then_inc(s4ep, 1)
                    vector.wait_ge(s4si, it + 1)
                    vector.tensor_tensor(f4h[it % 2][:, :], f4g[it % 2][:, :],
                                         f4u[it % 2][:, :],
                                         OP.mult).then_inc(s4hm, 1)
                    if it >= 1:
                        j = it - 1
                        vector.wait_ge(s4tr, j + 1)
                        if j >= 2:
                            vector.wait_ge(dsem["gpsimd"],
                                           g4base + 16 * (j - 1))
                        vector.tensor_copy(f4t[j % 2][:, :],
                                           pb[2 + j % 2][:, 0:512]).then_inc(
                                               s4te, 1)
            j = NI4 - 1
            vector.wait_ge(s4tr, j + 1)
            vector.wait_ge(dsem["gpsimd"], g4base + 16 * (j - 1))
            vector.tensor_copy(f4t[j % 2][:, :],
                               pb[2 + j % 2][:, 0:512]).then_inc(s4te, 1)

        @block.scalar
        def _(scalar):
            for it in range(NI4):
                scalar.wait_ge(s4ep, it + 1)
                scalar.activation(f4g[it % 2][:, :], f4g[it % 2][:, :],
                                  ACT.Silu).then_inc(s4si, 1)

        @block.gpsimd
        def _(gpsimd):
            for it in range(NI4):
                m, f = it // 4, it % 4
                gpsimd.wait_ge(s4te, it + 1)
                gpsimd.dma_start(
                    out=h_scr.ap()[f * 4:(f + 1) * 4, :,
                                   m * 128:(m + 1) * 128]
                    .rearrange("c p q -> p c q"),
                    in_=f4t[it % 2][:, :].rearrange("p (c q) -> p c q", c=4),
                ).then_inc(dsem["gpsimd"], 16)
                dcnt["gpsimd"] += 16
            gpsimd.wait_ge(dsem["gpsimd"], dcnt["gpsimd"])

        prev[0] = None
        f4.close()
        barrier()

        f3b = ExitStack()
        sb_mask = f3b.enter_context(nc.sbuf_tensor("sb_mask", [128, 128],
                                                   F32))
        sb_sel64 = f3b.enter_context(nc.sbuf_tensor("sb_sel64",
                                                     [64, 64 * 128], F32))
        xtmB = [f3b.enter_context(nc.sbuf_tensor(f"xtmB{j}", [128, DIN], F32))
                for j in range(2)]
        bcBB = [f3b.enter_context(nc.sbuf_tensor(f"bcBB{j}", [64, 128], F32))
                for j in range(2)]
        bcCB = [f3b.enter_context(nc.sbuf_tensor(f"bcCB{j}", [64, 128], F32))
                for j in range(2)]
        zch = [f3b.enter_context(nc.sbuf_tensor(f"zch{j}", [128, DIN], F32))
               for j in range(2)]
        spre = [f3b.enter_context(nc.sbuf_tensor(f"spre{j}", [64, DIN], F32))
                for j in range(2)]
        sinit = f3b.enter_context(nc.sbuf_tensor("sinit", [64, DIN], F32))
        cbt = f3b.enter_context(nc.sbuf_tensor("cbt", [128, 128], F32))
        mht = f3b.enter_context(nc.sbuf_tensor("mht", [128, 8 * 128], F32))
        ych = f3b.enter_context(nc.sbuf_tensor("ych", [128, DIN], F32))
        lamr = f3b.enter_context(nc.sbuf_tensor("lamr", [64, 64], F32))
        yst = f3b.enter_context(nc.sbuf_tensor("yst", [128, 512], F32))
        rows64 = f3b.enter_context(nc.sbuf_tensor("rows64", [64, 128],
                                                   F32))
        f3brow = f3b.enter_context(nc.sbuf_tensor("f3brow", [1, 128], F32))
        f3bld = f3b.enter_context(nc.sbuf_tensor("f3bld", [128, 128], F32))
        sqacc = f3b.enter_context(nc.sbuf_tensor("sqacc", [128, 128], F32))
        yfm = f3b.enter_context(nc.sbuf_tensor("yfm", [128, DIN], F32))

        def ldsel(sync):
            dma(sync, "sync", sb_mask[:], maskneg.ap())
            return dma(sync, "sync",
                       sb_sel64[:].rearrange("p (j q) -> p j q", j=64),
                       sel64.ap().rearrange("j p q -> p j q"))
        step("sync", ldsel, dmas=2)

        def getinit(gpsimd):
            gpsimd.wait_ge(sC, 1)
            return dma(gpsimd, "gpsimd", sinit[:, :], cc_out.ap()[0:64, :])
        step("gpsimd", getinit, dmas=1)

        def maskinit(vector):
            return vector.tensor_scalar(sinit[:, :], sinit[:, :],
                                        sb_smask[:, 0:1], None, OP.mult)
        step("vector", maskinit)

        def pfb0(sync):
            dma(sync, "sync", xtmB[0][:, :], xh_scr.ap()[0])
            dma(sync, "sync", bcBB[0][:, :], bc_scr.ap()[0, 0:64])
            dma(sync, "sync", bcCB[0][:, :], bc_scr.ap()[0, 64:128])
            return dma(sync, "sync",
                       zch[0][:, :].rearrange("p (mq q) -> p mq q", mq=32),
                       z_scr.ap()[:, :, 0:128].rearrange("m p q -> p m q"))
        step("sync", pfb0, dmas=4)

        for c in range(NT):
            cur, nxt = c % 2, (c + 1) % 2
            if c + 1 < NT:
                def pfb(sync, c=c, nxt=nxt):
                    dma(sync, "sync", xtmB[nxt][:, :], xh_scr.ap()[c + 1])
                    dma(sync, "sync", bcBB[nxt][:, :],
                        bc_scr.ap()[c + 1, 0:64])
                    dma(sync, "sync", bcCB[nxt][:, :],
                        bc_scr.ap()[c + 1, 64:128])
                    dma(sync, "sync",
                        zch[nxt][:, :].rearrange("p (mq q) -> p mq q", mq=32),
                        z_scr.ap()[:, :, (c + 1) * 128:(c + 2) * 128]
                        .rearrange("m p q -> p m q"))
                    return dma(sync, "sync", spre[nxt][:, :],
                               sloc_scr.ap()[c])
                step("sync", pfb, dmas=5)
            if c == 0:
                def z0(vector):
                    return vector.memset(spre[0][:, :], 0.0)
                step("vector", z0)

            def lam(scalar, c=c):
                return scalar.activation(lamr[:, 0:1],
                                         sb_lcum_pre[:, c:c + 1], ACT.Exp)
            step("scalar", lam)

            def lamt(tensor, c=c):
                tensor.transpose(pb[2][:1, 0:64], lamr[:, 0:1],
                                 sb_ident[:64, :64])
                return None
            step("tensor", lamt)

            def lamev(vector, c=c):
                return vector.tensor_copy(f3brow[:1, 0:64], pb[2][:1, 0:64])
            step("vector", lamev)

            def lamb(tensor, c=c):
                tensor.matmul(pb[2][0:64, 64:128], sb_ones_r[:, 0:64],
                              f3brow[:1, 0:64], start=True, stop=True)
                return None
            step("tensor", lamb)

            def lamev2(vector, c=c):
                return vector.tensor_copy(lamr[:, :], pb[2][0:64, 64:128])
            step("vector", lamev2)

            def sfull(vector, c=c, cur=cur):
                last = None
                for hh in range(H):
                    last = vector.scalar_tensor_tensor(
                        spre[cur][:, hh * 64:(hh + 1) * 64],
                        sinit[:, hh * 64:(hh + 1) * 64],
                        lamr[:, hh:hh + 1],
                        spre[cur][:, hh * 64:(hh + 1) * 64],
                        OP.mult, OP.add)
                return last
            step("vector", sfull)

            def trsb(tensor, c=c, cur=cur):
                tensor.transpose(pb[2][:128, 0:64],
                                 sb_lall[:, c * 128:(c + 1) * 128],
                                 sb_ident[:64, :64])
                tensor.transpose(pb[2][:128, 64:128],
                                 sb_dt[:, c * 128:(c + 1) * 128],
                                 sb_ident[:64, :64])
                tensor.matmul(pb[3][:, 0:128], bcBB[cur][:, :],
                              bcCB[cur][:, :], start=True, stop=True)
                return None
            step("tensor", trsb)

            def evb(vector, c=c):
                vector.tensor_copy(f3bld[:, 0:128], pb[2][:, 0:128])
                return vector.tensor_copy(cbt[:, :], pb[3][:, 0:128])
            step("vector", evb)

            def gamm(scalar, c=c):
                return scalar.activation(sb_gam[:, :], f3bld[:, 0:64],
                                         ACT.Exp)
            step("scalar", gamm)

            def ltr64(tensor, c=c):
                tensor.transpose(pb[2][:64, 0:128], f3bld[:, 0:64],
                                 sb_ident[:, :])
                return None
            step("tensor", ltr64)

            def lev64(vector, c=c):
                return vector.tensor_copy(rows64[:, :], pb[2][:64, 0:128])
            step("vector", lev64)

            for g in range(8):
                def lb(tensor, c=c, g=g):
                    for j in range(8):
                        bank = pb[2 + j // 4]
                        slot = (j % 4) * 128
                        hh = g * 8 + j
                        tensor.matmul(bank[:, slot:slot + 128],
                                      sb_sel64[:, hh * 128:(hh + 1) * 128],
                                      rows64[:, :], start=True, stop=True)
                    return None
                step("tensor", lb)

                def mbuild(vector, c=c, g=g):
                    last = None
                    for j in range(8):
                        hh = g * 8 + j
                        bank = pb[2 + j // 4]
                        slot = (j % 4) * 128
                        last = vector.scalar_tensor_tensor(
                            mht[:, j * 128:(j + 1) * 128],
                            bank[:, slot:slot + 128],
                            f3bld[:, hh:hh + 1], sb_mask[:, :],
                            OP.subtract, OP.add)
                    return last
                step("vector", mbuild)

                def mexp(scalar, c=c, g=g):
                    return scalar.activation(mht[:, :], mht[:, :], ACT.Exp)
                step("scalar", mexp)

                def mfin(vector, c=c, g=g):
                    last = None
                    for j in range(8):
                        hh = g * 8 + j
                        last = vector.scalar_tensor_tensor(
                            mht[:, j * 128:(j + 1) * 128],
                            mht[:, j * 128:(j + 1) * 128],
                            f3bld[:, 64 + hh:64 + hh + 1], cbt[:, :],
                            OP.mult, OP.mult)
                    return last
                step("vector", mfin)

                def ymm(tensor, c=c, g=g, cur=cur):
                    for j in range(8):
                        hh = g * 8 + j
                        tensor.matmul(
                            pb[4][:, j * 64:(j + 1) * 64],
                            mht[:, j * 128:(j + 1) * 128],
                            xtmB[cur][:, hh * 64:(hh + 1) * 64],
                            start=True, stop=True)
                    tensor.matmul(
                        pb[5][:, 0:512], bcCB[cur][:, :],
                        spre[cur][:, g * 512:(g + 1) * 512],
                        start=True, stop=True)
                    return None
                step("tensor", ymm)

                def yev(scalar, c=c, g=g):
                    last = None
                    for j in range(8):
                        hh = g * 8 + j
                        last = scalar.activation(
                            yst[:, j * 64:(j + 1) * 64],
                            pb[5][:, j * 64:(j + 1) * 64], ACT.Copy,
                            scale=sb_gam[:, hh:hh + 1])
                    return last
                step("scalar", yev)

                def yev2(vector, c=c, g=g, cur=cur):
                    last = None
                    for j in range(8):
                        hh = g * 8 + j
                        vector.scalar_tensor_tensor(
                            ych[:, hh * 64:(hh + 1) * 64],
                            xtmB[cur][:, hh * 64:(hh + 1) * 64],
                            sb_dbc[:, hh:hh + 1],
                            pb[4][:, j * 64:(j + 1) * 64],
                            OP.mult, OP.add)
                        last = vector.tensor_tensor(
                            ych[:, hh * 64:(hh + 1) * 64],
                            ych[:, hh * 64:(hh + 1) * 64],
                            yst[:, j * 64:(j + 1) * 64], OP.add)
                    return last
                step("vector", yev2)

            transpose_batch([(ych[:, q * 128:(q + 1) * 128],
                              yfm[:, q * 128:(q + 1) * 128])
                             for q in range(32)])

            def gsil(scalar, c=c, cur=cur):
                return scalar.activation(zch[cur][:, :], zch[cur][:, :],
                                         ACT.Silu)
            step("scalar", gsil)

            def gmul(vector, c=c, cur=cur):
                vector.memset(sqacc[:, :], 0.0)
                last = None
                for q in range(32):
                    sl = slice(q * 128, (q + 1) * 128)
                    vector.tensor_tensor(yfm[:, sl], yfm[:, sl],
                                         zch[cur][:, sl], OP.mult)
                    vector.tensor_tensor(yst[:, 0:128], yfm[:, sl],
                                         yfm[:, sl], OP.mult)
                    last = vector.tensor_tensor(sqacc[:, :], sqacc[:, :],
                                                yst[:, 0:128], OP.add)
                return last
            step("vector", gmul)

            def gred(tensor, c=c):
                tensor.matmul(pb[2][:1, 0:128], sb_ones_c[:, 0:1],
                              sqacc[:, :], start=True, stop=True)
                return None
            step("tensor", gred)

            def grr(vector, c=c):
                return vector.tensor_scalar(f3brow[:1, 0:128],
                                            pb[2][:1, 0:128], 1.0 / DIN,
                                            1e-5, OP.mult, OP.add)
            step("vector", grr)

            def grs(scalar, c=c):
                scalar.activation(f3brow[:1, 0:128], f3brow[:1, 0:128],
                                  ACT.Ln)
                return scalar.activation(
                    f3brow[:1, 0:128], f3brow[:1, 0:128], ACT.Exp,
                    scale=sb_cconst[:1, C_NEGHALF:C_NEGHALF + 1])
            step("scalar", grs)

            def grb(tensor, c=c):
                tensor.matmul(pb[2][:, 128:256], sb_ones_r[:, :],
                              f3brow[:1, 0:128], start=True, stop=True)
                return None
            step("tensor", grb)

            def grbe(vector, c=c):
                return vector.tensor_copy(sqacc[:, :], pb[2][:, 128:256])
            step("vector", grbe)

            def yrm(vector, c=c):
                last = None
                for q in range(32):
                    sl = slice(q * 128, (q + 1) * 128)
                    last = vector.scalar_tensor_tensor(
                        yfm[:, sl], yfm[:, sl], sb_normw[:, q:q + 1],
                        sqacc[:, :], OP.mult, OP.mult)
                return last
            step("vector", yrm)

            def styr(gpsimd, c=c):
                return dma(gpsimd, "gpsimd",
                           yr_scr.ap()[c].rearrange("k p q -> p k q"),
                           yfm[:, :].rearrange("p (k q) -> p k q", k=KWO))
            step("gpsimd", styr, dmas=1)

        f3b.close()
        barrier()

        f3c = ExitStack()
        yrb = f3c.enter_context(nc.sbuf_tensor("yrb", [128, KWO * 512], F32))
        wob = [f3c.enter_context(nc.sbuf_tensor(f"wob{j}", [128, KWO * 128],
                                                F32)) for j in range(2)]
        f3cs = [f3c.enter_context(nc.sbuf_tensor(f"f3cs{j}", [128, 512],
                                                 F32)) for j in range(2)]
        f3ct = [f3c.enter_context(nc.sbuf_tensor(f"f3ct{j}", [128, 512],
                                                 F32)) for j in range(2)]
        wdp = [f3c.enter_context(nc.sbuf_tensor(f"wdp{j}", [128, MDN * 128],
                                                F32)) for j in range(2)]
        qwp = [f3c.enter_context(nc.sbuf_tensor(f"qwp{j}", [128, MDN * 128],
                                                BF16)) for j in range(2)]
        f5pc = f3c.enter_context(nc.sbuf_tensor("f5pc", [128, 512], F32))
        f5pw = f3c.enter_context(nc.sbuf_tensor("f5pw", [128, 512], F32))

        wob_after = {}
        wd_after = {}
        store_after = {}

        @block.sync
        def _(sync):
            for f in range(4):
                for m in range(MWO):
                    it = f * MWO + m
                    if it >= 2:
                        sync.wait_ge(sWC, it - 1)
                    sync.dma_start(
                        out=wob[it % 2][:, :].rearrange("p (k q) -> p k q",
                                                        k=KWO),
                        in_=w_out_t.ap()[:, m].rearrange("k p q -> p k q"),
                    ).then_inc(dsem["sync"], 16)
                    dcnt["sync"] += 16
                    wob_after[it] = dcnt["sync"]
                    k = it
                    if k < KDN:
                        if k >= 2:
                            sync.wait_ge(sWP, k - 1)
                        sync.dma_start(
                            out=wdp[k % 2][:, :].rearrange("p (m q) -> p m q",
                                                           m=MDN),
                            in_=w_down_t.ap()[k].rearrange("m p q -> p m q"),
                        ).then_inc(dsem["sync"], 16)
                        dcnt["sync"] += 16
                        wd_after[k] = dcnt["sync"]
                    if 2 <= it < KDN + 2:
                        ks = it - 2
                        sync.wait_ge(sWP, ks + 1)
                        sync.dma_start(
                            out=qwd_scr.ap()[ks].rearrange("m p q -> p m q"),
                            in_=qwp[ks % 2][:, :].rearrange(
                                "p (m q) -> p m q", m=MDN),
                        ).then_inc(dsem["sync"], 16)
                        dcnt["sync"] += 16
                        store_after[ks] = dcnt["sync"]

        NIC = 4 * MWO
        scmm = e(nc.semaphore("scmm"))
        sctr = e(nc.semaphore("sctr"))
        scte = e(nc.semaphore("scte"))
        scld = e(nc.semaphore("scld"))
        gcbase = dcnt["gpsimd"]
        acbase = dcnt["scalar"]

        @block.scalar
        def _(scalar):
            for f in range(4):
                if f >= 1:
                    scalar.wait_ge(scmm, f * MWO)
                for k in range(KWO):
                    scalar.dma_start(
                        out=yrb[:, k * 512:(k + 1) * 512].rearrange(
                            "p (cs q) -> p cs q", cs=4),
                        in_=yr_scr.ap()[4 * f:4 * f + 4, k].rearrange(
                            "cs p q -> p cs q"),
                    ).then_inc(dsem["scalar"], 16)
                    dcnt["scalar"] += 16

        @block.tensor
        def _(tensor):
            def trc(j):
                tensor.wait_ge(sWC, j + 1)
                if j >= 1:
                    tensor.wait_ge(scte, j)
                for q in range(4):
                    tensor.transpose(pb[2][:, q * 128:(q + 1) * 128],
                                     f3cs[j % 2][:, q * 128:(q + 1) * 128],
                                     sb_ident[:, :])
                tensor.drain()
                tensor.nop().then_inc(sctr, 1)

            for it in range(NIC):
                f, m = it // MWO, it % MWO
                if it >= 1:
                    trc(it - 1)
                if m == 0:
                    tensor.wait_ge(dsem["scalar"],
                                   acbase + 16 * KWO * (f + 1))
                tensor.wait_ge(dsem["sync"], wob_after[it])
                if it >= 2:
                    tensor.wait_ge(sWC, it - 1)
                last = None
                for k in range(KWO):
                    last = tensor.matmul(
                        pb[it % 2][:, :],
                        wob[it % 2][:, k * 128:(k + 1) * 128],
                        yrb[:, k * 512:(k + 1) * 512],
                        start=(k == 0), stop=(k == KWO - 1))
                last.then_inc(scmm, 1)
            trc(NIC - 1)

        @block.sync
        def _(sync):
            for ks in (KDN - 2, KDN - 1):
                sync.wait_ge(sWP, ks + 1)
                sync.dma_start(
                    out=qwd_scr.ap()[ks].rearrange("m p q -> p m q"),
                    in_=qwp[ks % 2][:, :].rearrange("p (m q) -> p m q",
                                                    m=MDN),
                ).then_inc(dsem["sync"], 16)
                dcnt["sync"] += 16
                store_after[ks] = dcnt["sync"]
            sync.wait_ge(dsem["sync"], dcnt["sync"])

        @block.vector
        def _(vector):
            for it in range(NIC):
                vector.wait_ge(scmm, it + 1)
                ins = vector.tensor_copy(f3cs[it % 2][:, :],
                                         pb[it % 2][:, :])
                ins.then_inc(sWC, 1)
                k = it
                if k < KDN:
                    vector.wait_ge(dsem["sync"], wd_after[k])
                    if k >= 2:
                        vector.wait_ge(dsem["sync"], store_after[k - 2])
                    last = None
                    for blk in range(4):
                        sl = slice(blk * 512, (blk + 1) * 512)
                        vector.tensor_scalar(f5pw[:, :], wdp[k % 2][:, sl],
                                             sb_swinv[:, 2:3], None, OP.mult)
                        vector.tensor_scalar(f5pc[:, :], f5pw[:, :], -0.5,
                                             None, OP.is_le)
                        last = vector.scalar_tensor_tensor(
                            qwp[k % 2][:, sl], f5pw[:, :], 0.5, f5pc[:, :],
                            OP.is_ge, OP.subtract)
                    last.then_inc(sWP, 1)
                if it >= 1:
                    j = it - 1
                    vector.wait_ge(sctr, j + 1)
                    if j >= 2:
                        vector.wait_ge(dsem["gpsimd"], gcbase + 16 * (j - 1))
                    vector.tensor_copy(f3ct[j % 2][:, :],
                                       pb[2][:, 0:512]).then_inc(scte, 1)
            j = NIC - 1
            vector.wait_ge(sctr, j + 1)
            vector.wait_ge(dsem["gpsimd"], gcbase + 16 * (j - 1))
            vector.tensor_copy(f3ct[j % 2][:, :],
                               pb[2][:, 0:512]).then_inc(scte, 1)

        @block.gpsimd
        def _(gpsimd):
            for it in range(NIC):
                f, m = it // MWO, it % MWO
                gpsimd.wait_ge(scte, it + 1)
                gpsimd.dma_start(
                    out=s_scr.ap()[f * 4:(f + 1) * 4, :,
                                   m * 128:(m + 1) * 128]
                    .rearrange("c p q -> p c q"),
                    in_=f3ct[it % 2][:, :].rearrange("p (c q) -> p c q", c=4),
                ).then_inc(dsem["gpsimd"], 16)
                dcnt["gpsimd"] += 16
            gpsimd.wait_ge(dsem["gpsimd"], dcnt["gpsimd"])

        prev[0] = None
        f3c.close()
        barrier()

        f5a = ExitStack()
        hch = [f5a.enter_context(nc.sbuf_tensor("hch0", [128, IFF], F32))]
        sb_lnw = f5a.enter_context(nc.sbuf_tensor("sb_lnw", [128, IFF], F32))
        f5w = f5a.enter_context(nc.sbuf_tensor("f5w", [128, IFF], F32))
        f5v = f5a.enter_context(nc.sbuf_tensor("f5v", [128, IFF], F32))
        f5h = f5a.enter_context(nc.sbuf_tensor("f5h", [128, IFF], F16))
        f5qs = f5a.enter_context(nc.sbuf_tensor("f5qs", [128, IFF], BF16))
        f5c = f5a.enter_context(nc.sbuf_tensor("f5c", [128, 8], F32))

        def pf50(sync):
            return dma(sync, "sync", sb_lnw[:, :], lnw_bc.ap())
        step("sync", pf50, dmas=1)

        for c in range(NT):
            cur = 0
            def pf5(sync, c=c):
                return dma(sync, "sync", hch[0][:, :], h_scr.ap()[c])
            step("sync", pf5, dmas=1)

            def rms(scalar, cur=cur):
                return scalar.activation(f5w[:, :], hch[cur][:, :],
                                         ACT.Square, accum_out=f5c[:, 0:1])
            step("scalar", rms)

            def rr(vector, cur=cur):
                return vector.tensor_scalar(f5c[:, 1:2], f5c[:, 0:1],
                                            1.0 / IFF, 1e-6, OP.mult, OP.add)
            step("vector", rr)

            def rr2(scalar):
                scalar.activation(f5c[:, 2:3], f5c[:, 1:2], ACT.Ln)
                return scalar.activation(
                    f5c[:, 3:4], f5c[:, 2:3], ACT.Exp,
                    scale=sb_cconst[:, C_NEGHALF:C_NEGHALF + 1])
            step("scalar", rr2)

            def h2(vector, cur=cur):
                vector.tensor_scalar(f5w[:, :], hch[cur][:, :], f5c[:, 3:4],
                                     None, OP.mult)
                last = None
                for q in range(4):
                    sl = slice(q * 2048, (q + 1) * 2048)
                    last = vector.tensor_tensor(f5w[:, sl], f5w[:, sl],
                                                sb_lnw[:, sl], OP.mult)
                return last
            step("vector", h2)

            def qs(vector, c=c):
                ins = vector.reduce_max(f5c[:, 4:5], f5w[:, :], AX,
                                        apply_absolute_value=True)
                sfence(vector, "vector", ins)
                vector.tensor_scalar(f5c[:, 4:5], f5c[:, 4:5], 1e-5, None,
                                     OP.max)
                ins = vector.reciprocal(f5c[:, 5:6], f5c[:, 4:5])
                sfence(vector, "vector", ins)
                vector.tensor_scalar(f5c[:, 5:6], f5c[:, 5:6], 127.0, None,
                                     OP.mult)
                ins = vector.tensor_scalar(sb_s2c[:, c:c + 1], f5c[:, 4:5],
                                            1.0 / 127.0, None, OP.mult)
                sfence(vector, "vector", ins)
                vector.tensor_scalar(f5v[:, :], f5w[:, :], f5c[:, 5:6],
                                     None, OP.mult)
                vector.tensor_scalar(f5h[:, :], f5v[:, :], 1536.0, None,
                                     OP.add)
                vector.tensor_scalar(f5w[:, :], f5h[:, :], 1536.0, None,
                                     OP.subtract)
                vector.tensor_tensor(hch[0][:, :], f5v[:, :], f5w[:, :],
                                     OP.subtract)
                vector.scalar_tensor_tensor(f5v[:, :], hch[0][:, :], 0.5,
                                            f5w[:, :], OP.is_gt, OP.add)
                vector.scalar_tensor_tensor(f5w[:, :], hch[0][:, :], -0.5,
                                            f5v[:, :], OP.is_ge, OP.add)
                return vector.tensor_scalar(f5v[:, :], f5w[:, :], 1.0, None,
                                            OP.subtract)
            step("vector", qs)

            transpose_batch([(f5v[:, k * 128:(k + 1) * 128],
                              f5qs[:, k * 128:(k + 1) * 128])
                             for k in range(KDN)])

            def stq5(gpsimd, c=c):
                return dma(
                    gpsimd, "gpsimd",
                    qh_scr.ap()[:, :, c * 128:(c + 1) * 128].rearrange(
                        "k p q -> p k q"),
                    f5qs[:, :].rearrange("p (k q) -> p k q", k=KDN))
            step("gpsimd", stq5, dmas=1)

        f5a.close()
        barrier()

        f5b = ExitStack()
        qhb = f5b.enter_context(nc.sbuf_tensor("qhb", [128, KDN * 512], BF16))
        qwdb = [f5b.enter_context(nc.sbuf_tensor(f"qwdb{j}",
                                                 [128, KDN * 128], BF16))
                for j in range(2)]
        s2bc = f5b.enter_context(nc.sbuf_tensor("s2bc", [128, R], F32))
        f5d = [f5b.enter_context(nc.sbuf_tensor(f"f5d{j}", [128, 512], F32))
               for j in range(2)]
        gtm = f5b.enter_context(nc.sbuf_tensor("gtm", [128, 4 * 2048], F32))
        xr2 = [f5b.enter_context(nc.sbuf_tensor(f"xr2{j}", [128, 2048], F32))
               for j in range(2)]
        sr2 = [f5b.enter_context(nc.sbuf_tensor(f"sr2{j}", [128, 2048], F32))
               for j in range(2)]
        f5row = f5b.enter_context(nc.sbuf_tensor("f5row", [1, 128], F32))

        @block.sync
        def _(sync):
            for f in range(4):
                for m in range(MDN):
                    it = f * MDN + m
                    if it >= 2:
                        sync.wait_ge(sWD, it - 1)
                    sync.dma_start(
                        out=qwdb[it % 2][:, :].rearrange("p (k q) -> p k q",
                                                         k=KDN),
                        in_=qwd_scr.ap()[:, m].rearrange("k p q -> p k q"),
                    ).then_inc(dsem["sync"], 16)
                    dcnt["sync"] += 16
        wd_base = dcnt["sync"] - 16 * 4 * MDN

        for i in range(NT):
            bcast_col(sb_s2c[:, i:i + 1], s2bc[:, i * 128:(i + 1) * 128],
                      f5row)

        NID = 4 * MDN
        sdmm = e(nc.semaphore("sdmm"))
        sdep = e(nc.semaphore("sdep"))
        sdsi = e(nc.semaphore("sdsi"))
        sdtr = e(nc.semaphore("sdtr"))
        sdte = e(nc.semaphore("sdte"))
        sdrs = e(nc.semaphore("sdrs"))
        adbase = dcnt["scalar"]

        @block.scalar
        def _(scalar):
            for f in range(4):
                if f >= 1:
                    scalar.wait_ge(sdmm, f * MDN)
                scalar.dma_start(
                    out=qhb[:, :].rearrange("p (k q) -> p k q", k=KDN),
                    in_=qh_scr.ap()[:, :, f * 512:(f + 1) * 512]
                    .rearrange("k p q -> p k q"),
                ).then_inc(dsem["scalar"], 16)
                dcnt["scalar"] += 16
                for it in range(f * MDN, (f + 1) * MDN):
                    scalar.wait_ge(sdep, it + 1)
                    scalar.activation(f5d[it % 2][:, :], f5d[it % 2][:, :],
                                      ACT.Silu).then_inc(sdsi, 1)
                for q in range(4):
                    r = f * 4 + q
                    cch = f * 4 + q
                    if r >= 2:
                        scalar.wait_ge(sOut, 16 * (r - 1))
                    scalar.dma_start(
                        out=xr2[q % 2][:, :],
                        in_=x_own.ap()[cch * 128:(cch + 1) * 128, :],
                    ).then_inc(dsem["scalar"], 16)
                    scalar.dma_start(
                        out=sr2[q % 2][:, :],
                        in_=s_scr.ap()[cch]).then_inc(dsem["scalar"], 16)
                    dcnt["scalar"] += 32

        @block.tensor
        def _(tensor):
            def trd(j):
                tensor.wait_ge(sdsi, j + 1)
                if j >= 1:
                    tensor.wait_ge(sdte, j)
                for q in range(4):
                    tensor.transpose(pb[2][:, q * 128:(q + 1) * 128],
                                     f5d[j % 2][:, q * 128:(q + 1) * 128],
                                     sb_ident[:, :])
                tensor.drain()
                tensor.nop().then_inc(sdtr, 1)

            for it in range(NID):
                f, m = it // MDN, it % MDN
                if it >= 1:
                    trd(it - 1)
                if m == 0:
                    tensor.wait_ge(dsem["scalar"], adbase + 144 * f + 16)
                tensor.wait_ge(dsem["sync"], wd_base + 16 * (it + 1))
                if it >= 2:
                    tensor.wait_ge(sdep, it - 1)
                last = None
                for k in range(KDN):
                    last = tensor.matmul(
                        pb[it % 2][:, :],
                        qwdb[it % 2][:, k * 128:(k + 1) * 128],
                        qhb[:, k * 512:(k + 1) * 512],
                        start=(k == 0), stop=(k == KDN - 1))
                last.then_inc(sdmm, 1)
            trd(NID - 1)

        @block.vector
        def _(vector):
            def tev(j):
                jm = j % MDN
                vector.wait_ge(sdtr, j + 1)
                last = None
                for q in range(4):
                    last = vector.tensor_copy(
                        gtm[:, q * 2048 + jm * 128:q * 2048 + (jm + 1) * 128],
                        pb[2][:, q * 128:(q + 1) * 128])
                last.then_inc(sdte, 1)
                if (j + 1) % MDN == 0:
                    fdone = j // MDN
                    for q in range(4):
                        r = fdone * 4 + q
                        vector.wait_ge(dsem["scalar"],
                                       adbase + 144 * fdone + 48 + 32 * q)
                        vector.tensor_tensor(
                            sr2[q % 2][:, :], sr2[q % 2][:, :],
                            gtm[:, q * 2048:(q + 1) * 2048], OP.mult)
                        vector.tensor_tensor(
                            sr2[q % 2][:, :], xr2[q % 2][:, :],
                            sr2[q % 2][:, :], OP.add).then_inc(sdrs, 1)

            for it in range(NID):
                f = it // MDN
                if it >= 1:
                    tev(it - 1)
                vector.wait_ge(sdmm, it + 1)
                ins = vector.tensor_tensor(
                    f5d[it % 2][:, :], pb[it % 2][:, :],
                    s2bc[:, f * 512:f * 512 + 512], OP.mult)
                ins.then_inc(sWD, 1)
                vector.tensor_scalar(f5d[it % 2][:, :], f5d[it % 2][:, :],
                                     sb_wsum[:, 2:3], None,
                                     OP.mult).then_inc(sdep, 1)
            tev(NID - 1)

        @block.gpsimd
        def _(gpsimd):
            for r in range(16):
                cch = r
                gpsimd.wait_ge(sdrs, r + 1)
                gpsimd.dma_start(
                    out=out.ap()[cch * 128:(cch + 1) * 128, :],
                    in_=sr2[r % 2][:, :]).then_inc(sOut, 16)

        prev[0] = None
        f5b.close()

        @block.gpsimd
        def _(gpsimd):
            gpsimd.wait_ge(sOut, 16 * NT)

        @block.sync
        def _(sync):
            sync.wait_ge(sOut, 16 * NT)

    return nc


_NC_CACHE = [None]


def kernel(**inputs):
    import ml_dtypes

    def tile4(w):
        K, M = w.shape[0] // 128, w.shape[1] // 128
        return np.ascontiguousarray(
            w.reshape(K, 128, M, 128).transpose(0, 2, 1, 3))

    x = np.asarray(inputs["L0_attn_x"], np.float32)
    w_in = np.asarray(inputs["L0_attn_W_in"], np.float32)
    conv_w = np.asarray(inputs["L0_attn_conv_w"], np.float32)
    conv_b = np.asarray(inputs["L0_attn_conv_b"], np.float32)
    dt_bias = np.asarray(inputs["L0_attn_dt_bias"], np.float32)
    a_log = np.asarray(inputs["L0_attn_A_log"], np.float32)
    d_vec = np.asarray(inputs["L0_attn_D"], np.float32)
    norm_w = np.asarray(inputs["L0_attn_norm_w"], np.float32)
    w_out = np.asarray(inputs["L0_attn_W_out"], np.float32)
    w_gate = np.asarray(inputs["L0_ffn_w_gate"], np.float32)
    w_up = np.asarray(inputs["L0_ffn_w_up"], np.float32)
    w_down = np.asarray(inputs["L0_ffn_w_down"], np.float32)
    ln_w = np.asarray(inputs["L0_ffn_ln_weight"], np.float32)

    w_in_t = np.zeros((DM, M_IN * 128), np.float32)
    w_in_t[:, :8384] = w_in.T
    w_in_t = tile4(w_in_t)
    w_out_t = tile4(np.ascontiguousarray(w_out.T))
    w_gate_t = tile4(np.ascontiguousarray(w_gate.T))
    w_up_t = tile4(np.ascontiguousarray(w_up.T))
    w_down_t = tile4(np.ascontiguousarray(w_down.T))

    cw_cols = np.ascontiguousarray(
        conv_w.reshape(33, 128, 4).transpose(1, 0, 2).reshape(128, 132))
    cb_cols = np.ascontiguousarray(conv_b.reshape(33, 128).T)
    normw_cols = np.ascontiguousarray(norm_w.reshape(KWO, 128).T)
    lnw_bcast = np.ascontiguousarray(np.broadcast_to(ln_w, (128, IFF)))
    d_bcast = np.ascontiguousarray(np.broadcast_to(d_vec, (128, H)))
    identity = np.eye(128, dtype=np.float32)
    identityb = np.eye(128, dtype=ml_dtypes.bfloat16)
    mneg = np.where(np.arange(128)[None, :] >= np.arange(128)[:, None],
                    0.0, -1e9).astype(np.float32)
    ones_row = np.ones((1, 128), np.float32)
    ones_col = np.ones((128, 1), np.float32)
    sel64np = np.zeros((64, 64, 128), np.float32)
    for hh in range(64):
        sel64np[hh, hh, :] = 1.0
    sel = np.zeros((8, 8, 128), np.float32)
    for j in range(8):
        sel[j, j, :] = 1.0
    cconsts = np.zeros((128, 4), np.float32)
    cconsts[:, C_NEGHALF] = -0.5

    common = dict(
        w_in_t=w_in_t, w_out_t=w_out_t, w_gate_t=w_gate_t, w_up_t=w_up_t,
        w_down_t=w_down_t, cw_cols=cw_cols, cb_cols=cb_cols,
        normw_cols=normw_cols, lnw_bc=lnw_bcast,
        dtb_col=dt_bias.reshape(64, 1), alog_col=a_log.reshape(64, 1),
        d_bc=d_bcast, ident=identity, identb=identityb, maskneg=mneg,
        ones_r=ones_row, ones_c=ones_col, selmat=sel, sel64=sel64np,
        cconst=cconsts)

    in_maps = []
    for c in range(NCORE):
        b, half = c // 2, c % 2
        x_own_c = np.ascontiguousarray(x[b, half * R:(half + 1) * R])
        if half == 0:
            x_halo_c = np.zeros((3, DM), np.float32)
        else:
            x_halo_c = np.ascontiguousarray(x[b, R - 3:R])
        smask_c = np.full((64, 1), 1.0 if half == 1 else 0.0, np.float32)
        in_maps.append(dict(common, x_own=x_own_c, x_halo=x_halo_c,
                            smask=smask_c))

    if _NC_CACHE[0] is None:
        _NC_CACHE[0] = build_program()
    res = run_bass_kernel_spmd(_NC_CACHE[0], in_maps,
                               core_ids=list(range(NCORE)))

    full = np.empty((B, T, DM), np.float32)
    for c in range(NCORE):
        b, half = c // 2, c % 2
        full[b, half * R:(half + 1) * R] = np.asarray(res.results[c]["out"])
    return full


# version 09 (final)
# speedup vs baseline: 1.0806x; 1.0296x over previous
import numpy as np
from contextlib import ExitStack

import concourse.bass as bass
import concourse.mybir as mybir
from concourse.bass_utils import run_bass_kernel_spmd

F32 = mybir.dt.float32
F16 = mybir.dt.float16
BF16 = mybir.dt.bfloat16
AX = mybir.AxisListType.X
OP = mybir.AluOpType
ACT = mybir.ActivationFunctionType

B, T, DM = 4, 4096, 2048
R = 2048
NT = 16
RH = R + 3
DIN = 4096
H = 64
KD = 16
M_IN = 66
IFF = 8192
MFF = 64
KDN = 64
MDN = 16
KWO = 32
MWO = 16
NCORE = 8
FB2 = [(0, 512), (512, 512), (1024, 512), (1536, 512), (2048, 3)]

C_NEGHALF = 0

DBG = False


def build_program():
    nc = bass.Bass("TRN2", target_bir_lowering=False,
                   detect_race_conditions=False)

    def dram_in(name, shape, dtype=F32):
        return nc.dram_tensor(name, list(shape), dtype, kind="ExternalInput")

    def dram_scr(name, shape, dtype=F32):
        if DBG:
            return nc.dram_tensor(name, list(shape), dtype,
                                  kind="ExternalOutput")
        return nc.dram_tensor(name, list(shape), dtype)

    x_own = dram_in("x_own", [R, DM])
    x_halo = dram_in("x_halo", [3, DM])
    w_in_t = dram_in("w_in_t", [KD, M_IN, 128, 128])
    w_out_t = dram_in("w_out_t", [KWO, MWO, 128, 128])
    w_gate_t = dram_in("w_gate_t", [KD, MFF, 128, 128])
    w_up_t = dram_in("w_up_t", [KD, MFF, 128, 128])
    w_down_t = dram_in("w_down_t", [KDN, MDN, 128, 128])
    cw_cols = dram_in("cw_cols", [128, 33 * 4])
    cb_cols = dram_in("cb_cols", [128, 33])
    normw_cols = dram_in("normw_cols", [128, KWO])
    lnw_bc = dram_in("lnw_bc", [128, IFF])
    dtb_col = dram_in("dtb_col", [64, 1])
    alog_col = dram_in("alog_col", [64, 1])
    d_bc = dram_in("d_bc", [128, H])
    smask = dram_in("smask", [64, 1])
    ident = dram_in("ident", [128, 128])
    identb = dram_in("identb", [128, 128], BF16)
    maskneg = dram_in("maskneg", [128, 128])
    ones_r = dram_in("ones_r", [1, 128])
    ones_c = dram_in("ones_c", [128, 1])
    selmat = dram_in("selmat", [8, 8, 128])
    sel64 = dram_in("sel64", [64, 64, 128])
    cconst = dram_in("cconst", [128, 4])

    out = nc.dram_tensor("out", [R, DM], F32, kind="ExternalOutput")

    z_scr = dram_scr("z_scr", [32, 128, R])
    xh_scr = dram_scr("xh_scr", [NT, 128, DIN])
    bc_scr = dram_scr("bc_scr", [NT, 128, 128])
    qx_scr = dram_scr("qx_scr", [KD, 128, R], BF16)
    h_scr = dram_scr("h_scr", [NT, 128, IFF])
    qh_scr = dram_scr("qh_scr", [KDN, 128, R], BF16)
    qwd_scr = dram_scr("qwd_scr", [KDN, MDN, 128, 128], BF16)
    yr_scr = dram_scr("yr_scr", [NT, KWO, 128, 128])
    s_scr = dram_scr("s_scr", [NT, 128, DM])
    sloc_scr = nc.dram_tensor("sloc_scr", [NT, 64, DIN], F32)
    sloc_dbg = (nc.dram_tensor("sloc_dbg", [NT, 64, DIN], F32,
                               kind="ExternalOutput") if DBG else None)
    cc_out = nc.dram_tensor("cc_out", [128, DIN], F32)

    ctx = ExitStack()
    with ctx:
        e = ctx.enter_context

        sb_ident = e(nc.sbuf_tensor("sb_ident", [128, 128], F32))
        sb_identb = e(nc.sbuf_tensor("sb_identb", [128, 128], BF16))
        sb_ones_r = e(nc.sbuf_tensor("sb_ones_r", [1, 128], F32))
        sb_ones_c = e(nc.sbuf_tensor("sb_ones_c", [128, 1], F32))
        sb_cconst = e(nc.sbuf_tensor("sb_cconst", [128, 4], F32))
        sb_cw = e(nc.sbuf_tensor("sb_cw", [128, 33 * 4], F32))
        sb_cb = e(nc.sbuf_tensor("sb_cb", [128, 33], F32))
        sb_normw = e(nc.sbuf_tensor("sb_normw", [128, KWO], F32))
        sb_dtb = e(nc.sbuf_tensor("sb_dtb", [64, 1], F32))
        sb_alog = e(nc.sbuf_tensor("sb_alog", [64, 1], F32))
        sb_dbc = e(nc.sbuf_tensor("sb_dbc", [128, H], F32))
        sb_smask = e(nc.sbuf_tensor("sb_smask", [64, 1], F32))
        sb_acol = e(nc.sbuf_tensor("sb_acol", [64, 1], F32))
        sb_dt = e(nc.sbuf_tensor("sb_dt", [64, R], F32))
        sb_lall = e(nc.sbuf_tensor("sb_lall", [64, R], F32))
        sb_lat = e(nc.sbuf_tensor("sb_lat", [64, 128], F32))
        sb_zero64 = e(nc.sbuf_tensor("sb_zero64", [64, 128], F32))
        sb_lcum = e(nc.sbuf_tensor("sb_lcum", [64, 2], F32))
        sb_lcum_pre = e(nc.sbuf_tensor("sb_lcum_pre", [64, NT], F32))
        sb_sxc = e(nc.sbuf_tensor("sb_sxc", [128, NT], F32))
        sb_s2c = e(nc.sbuf_tensor("sb_s2c", [128, NT], F32))
        sb_swinv = e(nc.sbuf_tensor("sb_swinv", [128, 4], F32))
        sb_wsum = e(nc.sbuf_tensor("sb_wsum", [128, 4], F32))
        sb_gam = e(nc.sbuf_tensor("sb_gam", [128, 64], F32))

        pb = [e(nc.psum_tensor(f"pb{i}", [128, 512], F32)) for i in range(6)]
        pbh = e(nc.psum_tensor("pbh", [128, 1024], BF16))

        block = e(nc.Block())

        ENGS = ("sync", "vector", "scalar", "tensor", "gpsimd")
        esem = {n: e(nc.semaphore(f"se_{n}")) for n in ENGS}
        ecnt = {n: 0 for n in ENGS}
        dsem = {n: e(nc.semaphore(f"sd_{n}")) for n in
                ("sync", "scalar", "gpsimd")}
        dcnt = {n: 0 for n in ("sync", "scalar", "gpsimd")}
        sB = e(nc.semaphore("sBar"))
        sC = e(nc.semaphore("sColl"))
        sW2 = e(nc.semaphore("sW2"))
        sW4 = e(nc.semaphore("sW4"))
        sWC = e(nc.semaphore("sWC"))
        sWP = e(nc.semaphore("sWP"))
        sWD = e(nc.semaphore("sWD"))
        sOut = e(nc.semaphore("sOut"))
        nbar = [0]
        prev = [None]

        def step(ename, fn, dmas=0):
            @getattr(block, ename)
            def _(eng):
                if prev[0] is not None:
                    sem, val = prev[0]
                    eng.wait_ge(sem, val)
                last = fn(eng)
                if dmas:
                    dcnt[ename] += 16 * dmas
                    prev[0] = (dsem[ename], dcnt[ename])
                else:
                    if last is None:
                        eng.drain()
                        last = eng.nop()
                    ecnt[ename] += 1
                    last.then_inc(esem[ename], 1)
                    prev[0] = (esem[ename], ecnt[ename])

        def sfence(eng, ename, ins):
            ecnt[ename] += 1
            ins.then_inc(esem[ename], 1)
            eng.wait_ge(esem[ename], ecnt[ename])

        def dma(eng, ename, out_ap, in_ap):
            ins = eng.dma_start(out=out_ap, in_=in_ap)
            ins.then_inc(dsem[ename], 16)
            return ins

        def barrier():
            nbar[0] += 1
            tgt = 5 * nbar[0]
            for name in ENGS:
                @getattr(block, name)
                def _(eng, tgt=tgt):
                    if prev[0] is not None:
                        sem, val = prev[0]
                        eng.wait_ge(sem, val)
                    eng.drain()
                    eng.sem_inc(sB, 1)
                    eng.wait_ge(sB, tgt)
            prev[0] = None

        def transpose_batch(pairs, bf=False):
            W = 8
            for w0 in range(0, len(pairs), W):
                wave = pairs[w0:w0 + W]

                def tr(tensor, wave=wave):
                    for j, (src, dst) in enumerate(wave):
                        P, Q = src.shape
                        if bf:
                            tensor.transpose(pbh[:Q, j * 128:j * 128 + P],
                                             src, sb_identb[:P, :P])
                        else:
                            bank = pb[2 + j // 4]
                            slot = (j % 4) * 128
                            tensor.transpose(bank[:Q, slot:slot + P], src,
                                             sb_ident[:P, :P])
                    return None
                step("tensor", tr)

                def ev(vector, wave=wave):
                    last = None
                    for j, (src, dst) in enumerate(wave):
                        P, Q = src.shape
                        if bf:
                            last = vector.tensor_copy(
                                dst, pbh[:Q, j * 128:j * 128 + P])
                        else:
                            bank = pb[2 + j // 4]
                            slot = (j % 4) * 128
                            last = vector.tensor_copy(
                                dst, bank[:Q, slot:slot + P])
                    return last
                step("vector", ev)

        def bcast_col(col_ap, dst_ap, tmprow, P=128):
            def t1(tensor):
                tensor.transpose(pb[2][:1, 0:128], col_ap, sb_ident[:, :])
                return None
            step("tensor", t1)

            def v1(vector):
                return vector.tensor_copy(tmprow[:1, 0:128], pb[2][:1, 0:128])
            step("vector", v1)

            def t2(tensor, P=P):
                tensor.matmul(pb[3][:P, 0:128], sb_ones_r[:, :P],
                              tmprow[:1, 0:128], start=True, stop=True)
                return None
            step("tensor", t2)

            def v2(vector, P=P):
                return vector.tensor_copy(dst_ap, pb[3][:P, 0:128])
            step("vector", v2)

        def p0(sync):
            last = None
            for dst, src in [
                    (sb_ident[:], ident.ap()), (sb_identb[:], identb.ap()),
                    (sb_ones_r[:], ones_r.ap()),
                    (sb_ones_c[:], ones_c.ap()), (sb_cconst[:], cconst.ap()),
                    (sb_cw[:], cw_cols.ap()), (sb_cb[:], cb_cols.ap()),
                    (sb_normw[:], normw_cols.ap()), (sb_dtb[:], dtb_col.ap()),
                    (sb_alog[:], alog_col.ap()), (sb_dbc[:], d_bc.ap()),
                    (sb_smask[:], smask.ap())]:
                last = dma(sync, "sync", dst, src)
            return last
        step("sync", p0, dmas=12)

        def p0v(vector):
            vector.memset(sb_zero64[:], 0.0)
            vector.memset(sb_lcum[:], 0.0)
            return vector.memset(sb_wsum[:], 0.0)
        step("vector", p0v)

        def p0s(scalar):
            return scalar.activation(sb_acol[:], sb_alog[:], ACT.Exp)
        step("scalar", p0s)

        def p0v2(vector):
            return vector.tensor_scalar(sb_acol[:], sb_acol[:], -1.0, None,
                                        OP.mult)
        step("vector", p0v2)

        f12 = ExitStack()
        xnfm = [f12.enter_context(nc.sbuf_tensor(f"xnfm{k}", [128, RH], F32))
                for k in range(KD)]
        f1 = ExitStack()
        xtile = f1.enter_context(nc.sbuf_tensor("xtile", [128, DM], F32))
        f1w = f1.enter_context(nc.sbuf_tensor("f1w", [128, DM], F32))
        f1q = f1.enter_context(nc.sbuf_tensor("f1q", [128, DM], BF16))
        f1h = f1.enter_context(nc.sbuf_tensor("f1h", [128, DM], F16))
        f1qs = f1.enter_context(nc.sbuf_tensor("f1qs", [128, DM], BF16))
        f1v = f1.enter_context(nc.sbuf_tensor("f1v", [128, DM], F32))
        f1r = f1.enter_context(nc.sbuf_tensor("f1r", [128, DM], F32))
        f1c = f1.enter_context(nc.sbuf_tensor("f1c", [128, 8], F32))

        for i in range(NT + 1):
            P = 128 if i < NT else 3

            def ld(scalar, i=i, P=P):
                src = (x_own.ap()[i * 128:(i + 1) * 128, :] if i < NT
                       else x_halo.ap()[:, :])
                return dma(scalar, "scalar", xtile[:P, :], src)
            step("scalar", ld, dmas=1)

            def v1(vector, P=P):
                ins = vector.reduce_sum(f1c[:P, 0:1], xtile[:P, :], AX,
                                        negate=True)
                sfence(vector, "vector", ins)
                ins = vector.tensor_scalar(f1c[:P, 1:2], f1c[:P, 0:1],
                                           1.0 / DM, None, OP.mult)
                sfence(vector, "vector", ins)
                return vector.tensor_scalar(f1w[:P, :], xtile[:P, :],
                                            f1c[:P, 1:2], None, OP.add)
            step("vector", v1)

            def s1(scalar, P=P):
                return scalar.activation(xtile[:P, :], f1w[:P, :], ACT.Square,
                                         accum_out=f1c[:P, 2:3])
            step("scalar", s1)

            def v2(vector, P=P):
                return vector.tensor_scalar(f1c[:P, 3:4], f1c[:P, 2:3],
                                            1.0 / DM, 1e-5, OP.mult, OP.add)
            step("vector", v2)

            def s2(scalar, P=P):
                scalar.activation(f1c[:P, 4:5], f1c[:P, 3:4], ACT.Ln)
                return scalar.activation(
                    f1c[:P, 5:6], f1c[:P, 4:5], ACT.Exp,
                    scale=sb_cconst[:P, C_NEGHALF:C_NEGHALF + 1])
            step("scalar", s2)

            def v3(vector, i=i, P=P):
                vector.tensor_scalar(f1w[:P, :], f1w[:P, :], f1c[:P, 5:6],
                                     None, OP.mult)
                if P == 128:
                    ins = vector.reduce_max(f1c[:P, 6:7], f1w[:P, :], AX,
                                            apply_absolute_value=True)
                    sfence(vector, "vector", ins)
                    vector.tensor_scalar(f1c[:P, 6:7], f1c[:P, 6:7], 1e-5,
                                         None, OP.max)
                    ins = vector.reciprocal(f1c[:P, 7:8], f1c[:P, 6:7])
                    sfence(vector, "vector", ins)
                    vector.tensor_scalar(f1c[:P, 7:8], f1c[:P, 7:8], 127.0,
                                         None, OP.mult)
                    ins = vector.tensor_scalar(sb_sxc[:P, i:i + 1],
                                               f1c[:P, 6:7], 1.0 / 127.0,
                                               None, OP.mult)
                    sfence(vector, "vector", ins)
                    vector.tensor_scalar(f1v[:P, :], f1w[:P, :], f1c[:P, 7:8],
                                         None, OP.mult)
                    vector.tensor_scalar(f1h[:P, :], f1v[:P, :], 1536.0,
                                         None, OP.add)
                    vector.tensor_scalar(f1r[:P, :], f1h[:P, :], 1536.0,
                                         None, OP.subtract)
                    vector.tensor_tensor(xtile[:P, :], f1v[:P, :],
                                         f1r[:P, :], OP.subtract)
                    vector.scalar_tensor_tensor(f1v[:P, :], xtile[:P, :], 0.5,
                                                f1r[:P, :], OP.is_gt, OP.add)
                    vector.scalar_tensor_tensor(f1r[:P, :], xtile[:P, :],
                                                -0.5, f1v[:P, :], OP.is_ge,
                                                OP.add)
                    return vector.tensor_scalar(f1q[:P, :], f1r[:P, :], 1.0,
                                                None, OP.subtract)
                return None
            step("vector", v3)

            col = 3 + i * 128 if i < NT else 0
            transpose_batch([(f1w[:P, k * 128:(k + 1) * 128],
                              xnfm[k][:, col:col + P]) for k in range(KD)])

            if i < NT:
                transpose_batch([(f1q[:, k * 128:(k + 1) * 128],
                                  f1qs[:, k * 128:(k + 1) * 128])
                                 for k in range(KD)], bf=True)

                def st(gpsimd, i=i):
                    return dma(
                        gpsimd, "gpsimd",
                        qx_scr.ap()[:, :, i * 128:(i + 1) * 128].rearrange(
                            "k p q -> p k q"),
                        f1qs[:, :].rearrange("p (k q) -> p k q", k=KD))
                step("gpsimd", st, dmas=1)

        f1.close()
        barrier()

        f2 = ExitStack()
        wbuf = [f2.enter_context(nc.sbuf_tensor(f"wbuf{j}", [128, KD * 128],
                                                F32)) for j in range(2)]
        f2s0 = f2.enter_context(nc.sbuf_tensor("f2s0", [128, RH], F32))
        f2s = [f2s0, f2s0]
        f2c = [f2.enter_context(nc.sbuf_tensor(f"f2c{j}", [128, R], F32))
               for j in range(2)]
        f2t = f2.enter_context(nc.sbuf_tensor("f2t", [128, 2048], F32))
        NPAW = 4
        paw = f2.enter_context(nc.sbuf_tensor("paw", [128, NPAW * 128], F32))

        pa_srcs = []
        for wi, wt in enumerate((w_gate_t, w_up_t, w_down_t)):
            flat = wt.ap().rearrange("a b p q -> (a b) p q")
            ntile = KD * MFF if wi < 2 else KDN * MDN
            for j in range(0, ntile, NPAW):
                pa_srcs.append((wi, flat[j:j + NPAW]))

        FB2Z = [(3, 512), (515, 512), (1027, 512), (1539, 512)]

        def f2_blocks(m):
            return FB2 if 32 <= m <= 64 else FB2Z
        mmcum = [0]
        for _m in range(M_IN):
            mmcum.append(mmcum[-1] + len(f2_blocks(_m)))
        bank_of = {}
        for _m in range(M_IN):
            _rot = [0, 1, 4, 5] if _m < 32 else [0, 1]
            for _fi in range(len(f2_blocks(_m))):
                bank_of[mmcum[_m] + _fi] = _rot[_fi % len(_rot)]
        s2mm = e(nc.semaphore("s2mm"))
        s2ev = e(nc.semaphore("s2ev"))
        s2cv = e(nc.semaphore("s2cv"))
        s2si = e(nc.semaphore("s2si"))
        s2tr = e(nc.semaphore("s2tr"))
        s2te = e(nc.semaphore("s2te"))
        sPAr = e(nc.semaphore("sPAr"))
        sPAv = e(nc.semaphore("sPAv"))
        NPA = len(pa_srcs)
        g_after = {}
        pa_after = {}
        gc = [dcnt["gpsimd"]]
        _gcc = gc[0]
        _pa = 0
        for _m in range(M_IN):
            if _m < 32:
                _gcc += 16
                g_after[("z", _m)] = _gcc
            elif 33 <= _m <= 64:
                _gcc += 16
                g_after[("xh", _m - 33)] = _gcc
            elif _m == 65:
                _gcc += 16
                g_after[("bc", 0)] = _gcc
            while _pa < min(NPA, (NPA * (_m + 1)) // M_IN):
                _gcc += 16
                pa_after[_pa] = _gcc
                _pa += 1

        @block.sync
        def _(sync):
            for m in range(M_IN):
                if m >= 2:
                    sync.wait_ge(s2mm, mmcum[m - 1])
                sync.dma_start(
                    out=wbuf[m % 2][:, :].rearrange("p (k q) -> p k q", k=KD),
                    in_=w_in_t.ap()[:, m].rearrange("k p q -> p k q"),
                ).then_inc(dsem["sync"], 16)
                dcnt["sync"] += 16
        w2_base = dcnt["sync"] - 16 * M_IN

        last_mm = {}

        @block.tensor
        def _(tensor):
            for m in range(M_IN):
                if 33 <= m <= 64:
                    mc1 = m - 33
                    tensor.wait_ge(s2si, mc1 + 1)
                    if mc1 >= 1:
                        tensor.wait_ge(s2te, mc1)
                    else:
                        need = max(last_mm.get(4, -1), last_mm.get(5, -1))
                        if need >= 0:
                            tensor.wait_ge(s2ev, need + 1)
                    for j in range(16):
                        bank = pb[2 + j // 4]
                        slot = (j % 4) * 128
                        tensor.transpose(
                            bank[:, slot:slot + 128],
                            f2c[(m - 1) % 2][:, j * 128:(j + 1) * 128],
                            sb_ident[:, :])
                    tensor.drain()
                    tensor.nop().then_inc(s2tr, 1)
                for fi, (f0, fw) in enumerate(f2_blocks(m)):
                    gidx = mmcum[m] + fi
                    b = bank_of[gidx]
                    if fi == 0:
                        tensor.wait_ge(dsem["sync"], w2_base + 16 * (m + 1))
                    if b in last_mm:
                        tensor.wait_ge(s2ev, last_mm[b] + 1)
                    if m == 65 and b in (4, 5):
                        tensor.wait_ge(s2te, 32)
                    last = None
                    for k in range(KD):
                        last = tensor.matmul(
                            pb[b][:, 0:fw],
                            wbuf[m % 2][:, k * 128:(k + 1) * 128],
                            xnfm[k][:, f0:f0 + fw],
                            start=(k == 0), stop=(k == KD - 1))
                    last.then_inc(s2mm, 1)
                    last_mm[b] = gidx

        pa_v = [0]

        @block.vector
        def _(vector):
            for m in range(M_IN):
                if 33 <= m <= 64:
                    mc1 = m - 33
                    vector.wait_ge(s2tr, mc1 + 1)
                    if mc1 >= 1:
                        vector.wait_ge(dsem["gpsimd"],
                                       g_after[("xh", mc1 - 1)])
                    for j in range(4):
                        ins = vector.tensor_copy(
                            f2t[:, j * 512:(j + 1) * 512], pb[2 + j][:, 0:512])
                    ins.then_inc(s2te, 1)
                P_EV = 128 if m < 65 else 64
                for fi, (f0, fw) in enumerate(f2_blocks(m)):
                    gidx = mmcum[m] + fi
                    vector.wait_ge(s2mm, gidx + 1)
                    if fi == 0 and m >= 1 and (m - 1) < 32:
                        vector.wait_ge(dsem["gpsimd"], g_after[("z", m - 1)])
                    vector.tensor_copy(f2s0[:P_EV, f0:f0 + fw],
                                       pb[bank_of[gidx]][:P_EV, 0:fw]
                                       ).then_inc(s2ev, 1)
                if 32 <= m < 65:
                    mc = m - 32
                    if mc >= 2 and mc - 2 < 32:
                        vector.wait_ge(s2tr, mc - 1)
                    vector.tensor_scalar(f2c[m % 2][:, :],
                                         f2s[m % 2][:, 0:R],
                                         sb_cw[:, mc * 4:mc * 4 + 1], None,
                                         OP.mult)
                    last = None
                    for k in (1, 2, 3):
                        last = vector.scalar_tensor_tensor(
                            f2c[m % 2][:, :], f2s[m % 2][:, k:k + R],
                            sb_cw[:, mc * 4 + k:mc * 4 + k + 1],
                            f2c[m % 2][:, :], OP.mult, OP.add)
                    last.then_inc(s2cv, 1)
                while pa_v[0] < min(NPA, (NPA * (m + 1)) // M_IN):
                    i = pa_v[0]
                    pa_v[0] += 1
                    wi = pa_srcs[i][0]
                    vector.wait_ge(sPAr, i + 1)
                    vector.tensor_tensor(sb_wsum[:, wi:wi + 1],
                                         sb_wsum[:, wi:wi + 1],
                                         sb_wsum[:, 3:4],
                                         OP.add).then_inc(sPAv, 1)

        pa_s = [0]

        @block.scalar
        def _(scalar):
            def pa_emit_s(upto):
                while pa_s[0] < min(NPA, upto):
                    i = pa_s[0]
                    pa_s[0] += 1
                    scalar.wait_ge(dsem["gpsimd"], pa_after[i])
                    if i >= 1:
                        scalar.wait_ge(sPAv, i)
                    scalar.activation(paw[:, :], paw[:, :], ACT.Abs,
                                      accum_out=sb_wsum[:, 3:4]).then_inc(
                                          sPAr, 1)

            for m in range(M_IN):
                if 32 <= m < 65:
                    mc = m - 32
                    scalar.wait_ge(s2cv, mc + 1)
                    scalar.activation(f2c[m % 2][:, :],
                                      f2c[m % 2][:, :], ACT.Silu,
                                      bias=sb_cb[:, mc:mc + 1]).then_inc(
                                          s2si, 1)
                pa_emit_s((NPA * (m + 1)) // M_IN)
            pa_emit_s(NPA)

        pa_g = [0]

        @block.gpsimd
        def _(gpsimd):
            def pa_emit(upto):
                while pa_g[0] < min(NPA, upto):
                    i = pa_g[0]
                    pa_g[0] += 1
                    if i >= 1:
                        gpsimd.wait_ge(sPAr, i)
                    wi, src = pa_srcs[i]
                    gpsimd.dma_start(
                        out=paw[:, :].rearrange("p (n q) -> p n q", n=NPAW),
                        in_=src.rearrange("n p q -> p n q"),
                    ).then_inc(dsem["gpsimd"], 16)
                    gc[0] += 16
                    assert pa_after[i] == gc[0]

            for m in range(M_IN):
                if m < 32:
                    gpsimd.wait_ge(s2ev, mmcum[m + 1])
                    gpsimd.dma_start(out=z_scr.ap()[m],
                                     in_=f2s[m % 2][:, 3:3 + R]).then_inc(
                                         dsem["gpsimd"], 16)
                    gc[0] += 16
                    assert g_after[("z", m)] == gc[0]
                elif 33 <= m <= 64:
                    mc = m - 33
                    gpsimd.wait_ge(s2te, mc + 1)
                    gpsimd.dma_start(
                        out=xh_scr.ap()[:, :, mc * 128:(mc + 1) * 128]
                        .rearrange("c p q -> p c q"),
                        in_=f2t[:, :].rearrange("p (c q) -> p c q", c=NT),
                    ).then_inc(dsem["gpsimd"], 16)
                    gc[0] += 16
                    assert g_after[("xh", mc)] == gc[0]
                elif m == 65:
                    gpsimd.wait_ge(s2si, 33)
                    gpsimd.dma_start(
                        out=bc_scr.ap()[:, :, :].rearrange("c p q -> p c q"),
                        in_=f2c[0][:, :].rearrange("p (c q) -> p c q", c=NT),
                    ).then_inc(dsem["gpsimd"], 16)
                    gc[0] += 16
                    assert g_after[("bc", 0)] == gc[0]
                pa_emit((NPA * (m + 1)) // M_IN)
            pa_emit(NPA)
            dcnt["gpsimd"] = gc[0]
            gpsimd.wait_ge(dsem["gpsimd"], gc[0])

        prev[0] = None
        barrier()

        def dtact(scalar):
            scalar.activation(f2c[1][:64, 0:R], f2s0[:64, 3:3 + R],
                              ACT.Exp, bias=sb_dtb[:, 0:1])
            return scalar.activation(sb_dt[:, :], f2c[1][:64, 0:R],
                                     ACT.Ln, bias=1.0)
        step("scalar", dtact)

        def wfin_mm(tensor):
            tensor.matmul(pb[4][:1, 0:3], sb_wsum[:, 0:3], sb_ones_c[:, 0:1],
                          start=True, stop=True)
            return None
        def wfin_mm2(tensor):
            tensor.matmul(pb[4][:1, 0:3], sb_ones_c[:, 0:1], sb_wsum[:, 0:3],
                          start=True, stop=True)
            return None
        step("tensor", wfin_mm2)

        def wfin_v(vector):
            return vector.tensor_scalar(f2s[0][:1, 0:3], pb[4][:1, 0:3],
                                        1.0 / (IFF * DM), 1e-5, OP.mult,
                                        OP.max)
        step("vector", wfin_v)

        def wfin_bc(tensor):
            tensor.matmul(pb[4][:, 64:67], sb_ones_r[:, :], f2s[0][:1, 0:3],
                          start=True, stop=True)
            return None
        step("tensor", wfin_bc)

        def wfin_v2(vector):
            ins = vector.tensor_copy(sb_wsum[:, 0:3], pb[4][:, 64:67])
            sfence(vector, "vector", ins)
            return vector.reciprocal(sb_swinv[:, 0:3], sb_wsum[:, 0:3])
        step("vector", wfin_v2)

        f2.close()
        f12.close()
        barrier()

        f3a = ExitStack()
        sb_state = f3a.enter_context(nc.sbuf_tensor("sb_state", [64, DIN],
                                                    F32))
        xtm = [f3a.enter_context(nc.sbuf_tensor(f"xtm{j}", [128, DIN], F32))
               for j in range(2)]
        bcB = [f3a.enter_context(nc.sbuf_tensor(f"bcB{j}", [64, 128], F32))
               for j in range(2)]
        f3ld = f3a.enter_context(nc.sbuf_tensor("f3ld", [128, 192], F32))
        f3fac = f3a.enter_context(nc.sbuf_tensor("f3fac", [128, 64], F32))
        f3btm = f3a.enter_context(nc.sbuf_tensor("f3btm", [128, 64], F32))
        f3bs = f3a.enter_context(nc.sbuf_tensor("f3bs", [128, 512], F32))
        f3row = f3a.enter_context(nc.sbuf_tensor("f3row", [1, 192], F32))
        f3crep = f3a.enter_context(nc.sbuf_tensor("f3crep", [64, 64], F32))

        def st0(vector):
            return vector.memset(sb_state[:, :], 0.0)
        step("vector", st0)

        def pf0(sync):
            dma(sync, "sync", xtm[0][:, :], xh_scr.ap()[0])
            return dma(sync, "sync", bcB[0][:, :], bc_scr.ap()[0, 0:64])
        step("sync", pf0, dmas=2)

        for c in range(NT):
            cur, nxt = c % 2, (c + 1) % 2
            if c + 1 < NT:
                def pf(sync, c=c, nxt=nxt):
                    dma(sync, "sync", xtm[nxt][:, :], xh_scr.ap()[c + 1])
                    return dma(sync, "sync", bcB[nxt][:, :],
                               bc_scr.ap()[c + 1, 0:64])
                step("sync", pf, dmas=2)

            def cum(vector, c=c):
                vector.tensor_scalar(sb_lat[:, :],
                                     sb_dt[:, c * 128:(c + 1) * 128],
                                     sb_acol[:, 0:1], None, OP.mult)
                ins = vector.tensor_tensor_scan(
                    sb_lall[:, c * 128:(c + 1) * 128],
                    sb_lat[:, :], sb_zero64[:, :],
                    0.0, OP.add, OP.add)
                sfence(vector, "vector", ins)
                vector.tensor_copy(sb_lcum_pre[:, c:c + 1], sb_lcum[:, 0:1])
                return vector.tensor_tensor(
                    sb_lcum[:, 0:1], sb_lcum[:, 0:1],
                    sb_lall[:, c * 128 + 127:c * 128 + 128], OP.add)
            step("vector", cum)

            def trs(tensor, c=c, cur=cur):
                tensor.transpose(pb[2][:128, 0:64],
                                 sb_lall[:, c * 128:(c + 1) * 128],
                                 sb_ident[:64, :64])
                tensor.transpose(pb[2][:128, 64:128],
                                 sb_dt[:, c * 128:(c + 1) * 128],
                                 sb_ident[:64, :64])
                tensor.transpose(pb[2][:128, 128:192], bcB[cur][:, :],
                                 sb_ident[:64, :64])
                tensor.transpose(pb[3][:1, 0:64],
                                 sb_lall[:, c * 128 + 127:c * 128 + 128],
                                 sb_ident[:64, :64])
                return None
            step("tensor", trs)

            def ev1(vector, c=c):
                vector.tensor_copy(f3ld[:, 0:128], pb[2][:, 0:128])
                vector.tensor_copy(f3btm[:, :], pb[2][:, 128:192])
                return vector.tensor_copy(f3row[:1, 0:64], pb[3][:1, 0:64])
            step("vector", ev1)

            def crepe(scalar, c=c):
                return scalar.activation(f3row[:1, 64:128], f3row[:1, 0:64],
                                         ACT.Exp)
            step("scalar", crepe)

            def lbc(tensor, c=c):
                tensor.matmul(pb[3][:, 64:128], sb_ones_r[:, :],
                              f3row[:1, 0:64], start=True, stop=True)
                tensor.matmul(pb[3][0:64, 128:192], sb_ones_r[:, 0:64],
                              f3row[:1, 64:128], start=True, stop=True)
                return None
            step("tensor", lbc)

            def fac(vector, c=c):
                vector.tensor_copy(f3crep[:, :], pb[3][0:64, 128:192])
                return vector.tensor_tensor(f3fac[:, :], pb[3][:, 64:128],
                                            f3ld[:, 0:64], OP.subtract)
            step("vector", fac)

            def face(scalar, c=c):
                return scalar.activation(f3fac[:, :], f3fac[:, :], ACT.Exp)
            step("scalar", face)

            def fac2(vector, c=c):
                return vector.tensor_tensor(f3fac[:, :], f3fac[:, :],
                                            f3ld[:, 64:128], OP.mult)
            step("vector", fac2)

            for g in range(8):
                def bs(vector, g=g, cur=cur):
                    last = None
                    for j in range(8):
                        hh = g * 8 + j
                        last = vector.tensor_scalar(
                            f3bs[:, j * 64:(j + 1) * 64], f3btm[:, 0:64],
                            f3fac[:, hh:hh + 1], None, OP.mult)
                    return last
                step("vector", bs)

                def mm3(tensor, g=g, cur=cur):
                    for j in range(8):
                        hh = g * 8 + j
                        tensor.matmul(
                            pb[4][0:64, j * 64:(j + 1) * 64],
                            f3bs[:, j * 64:(j + 1) * 64],
                            xtm[cur][:, hh * 64:(hh + 1) * 64],
                            start=True, stop=True)
                    return None
                step("tensor", mm3)

                def supd(vector, g=g):
                    last = None
                    for j in range(8):
                        hh = g * 8 + j
                        last = vector.scalar_tensor_tensor(
                            sb_state[:, hh * 64:(hh + 1) * 64],
                            sb_state[:, hh * 64:(hh + 1) * 64],
                            f3crep[:, hh:hh + 1],
                            pb[4][0:64, j * 64:(j + 1) * 64],
                            OP.mult, OP.add)
                    return last
                step("vector", supd)

            def stst(gpsimd, c=c):
                ins = dma(gpsimd, "gpsimd", sloc_scr.ap()[c], sb_state[:, :])
                if DBG:
                    ins = dma(gpsimd, "gpsimd", sloc_dbg.ap()[c],
                              sb_state[:, :])
                return ins
            step("gpsimd", stst, dmas=2 if DBG else 1)

        def coll(gpsimd):
            gpsimd.collective_compute(
                "AllGather", OP.bypass,
                replica_groups=[[2 * i, 2 * i + 1] for i in range(4)],
                ins=[sloc_scr.ap()[NT - 1]],
                outs=[cc_out.ap()[:, :]],
            ).then_inc(sC, 1)
            gpsimd.drain()
            return None
        step("gpsimd", coll)

        f3a.close()
        barrier()

        f4 = ExitStack()
        qxr = [f4.enter_context(nc.sbuf_tensor(f"qxr{k}", [128, R], BF16))
               for k in range(KD)]
        wgb = [f4.enter_context(nc.sbuf_tensor(f"wgb{j}", [128, KD * 128],
                                               F32)) for j in range(2)]
        wub = [f4.enter_context(nc.sbuf_tensor(f"wub{j}", [128, KD * 128],
                                               F32)) for j in range(2)]
        qwg = [f4.enter_context(nc.sbuf_tensor(f"qwg{j}", [128, KD * 128],
                                               BF16)) for j in range(2)]
        qwu = [f4.enter_context(nc.sbuf_tensor(f"qwu{j}", [128, KD * 128],
                                               BF16)) for j in range(2)]
        f4g = [f4.enter_context(nc.sbuf_tensor(f"f4g{j}", [128, 512], F32))
               for j in range(2)]
        f4u = [f4.enter_context(nc.sbuf_tensor(f"f4u{j}", [128, 512], F32))
               for j in range(2)]
        f4h = [f4.enter_context(nc.sbuf_tensor(f"f4h{j}", [128, 512], F32))
               for j in range(2)]
        f4t = [f4.enter_context(nc.sbuf_tensor(f"f4t{j}", [128, 512], F32))
               for j in range(2)]
        f4w = f4.enter_context(nc.sbuf_tensor("f4w", [128, 512], F32))
        f4qs = f4.enter_context(nc.sbuf_tensor("f4qs", [128, 512], F32))
        sxbc = f4.enter_context(nc.sbuf_tensor("sxbc", [128, R], F32))
        f4row = f4.enter_context(nc.sbuf_tensor("f4row", [1, 128], F32))

        @block.sync
        def _(sync):
            for k in range(KD):
                sync.dma_start(out=qxr[k][:, :], in_=qx_scr.ap()[k]).then_inc(
                    dsem["sync"], 16)
                dcnt["sync"] += 16
            for m in range(MFF):
                if m >= 2:
                    sync.wait_ge(sW4, m - 1)
                sync.dma_start(
                    out=wgb[m % 2][:, :].rearrange("p (k q) -> p k q", k=KD),
                    in_=w_gate_t.ap()[:, m].rearrange("k p q -> p k q"),
                ).then_inc(dsem["sync"], 16)
                sync.dma_start(
                    out=wub[m % 2][:, :].rearrange("p (k q) -> p k q", k=KD),
                    in_=w_up_t.ap()[:, m].rearrange("k p q -> p k q"),
                ).then_inc(dsem["sync"], 16)
                dcnt["sync"] += 32
        w4_base = dcnt["sync"] - 32 * MFF

        for i in range(NT):
            bcast_col(sb_sxc[:, i:i + 1], sxbc[:, i * 128:(i + 1) * 128],
                      f4row)

        NI4 = MFF * 4
        s4mm = e(nc.semaphore("s4mm"))
        s4ep = e(nc.semaphore("s4ep"))
        s4si = e(nc.semaphore("s4si"))
        s4hm = e(nc.semaphore("s4hm"))
        s4tr = e(nc.semaphore("s4tr"))
        s4te = e(nc.semaphore("s4te"))
        g4base = dcnt["gpsimd"]

        @block.tensor
        def _(tensor):
            def tr4(j):
                tensor.wait_ge(s4hm, j + 1)
                if j >= 2:
                    tensor.wait_ge(s4te, j - 1)
                for q in range(4):
                    tensor.transpose(pb[2 + j % 2][:, q * 128:(q + 1) * 128],
                                     f4h[j % 2][:, q * 128:(q + 1) * 128],
                                     sb_ident[:, :])
                tensor.drain()
                tensor.nop().then_inc(s4tr, 1)

            for m in range(MFF):
                for f in range(4):
                    it = m * 4 + f
                    if it >= 1:
                        tr4(it - 1)
                    if f == 0:
                        tensor.wait_ge(dsem["sync"], w4_base + 32 * (m + 1))
                        tensor.wait_ge(sW4, m + 1)
                    if it >= 2:
                        tensor.wait_ge(s4ep, it - 1)
                    gb = pb[(it % 2) * 4]
                    ub = pb[1 + (it % 2) * 4]
                    for k in range(KD):
                        tensor.matmul(
                            gb[:, :], qwg[m % 2][:, k * 128:(k + 1) * 128],
                            qxr[k][:, f * 512:(f + 1) * 512],
                            start=(k == 0), stop=(k == KD - 1))
                    last = None
                    for k in range(KD):
                        last = tensor.matmul(
                            ub[:, :], qwu[m % 2][:, k * 128:(k + 1) * 128],
                            qxr[k][:, f * 512:(f + 1) * 512],
                            start=(k == 0), stop=(k == KD - 1))
                    last.then_inc(s4mm, 1)
            tr4(NI4 - 1)

        @block.vector
        def _(vector):
            for m in range(MFF):
                vector.wait_ge(dsem["sync"], w4_base + 32 * (m + 1))
                if m >= 2:
                    vector.wait_ge(s4mm, 4 * (m - 1))
                last = None
                for src, qdst, wi in ((wgb[m % 2], qwg[m % 2], 0),
                                      (wub[m % 2], qwu[m % 2], 1)):
                    for blk in range(4):
                        sl = slice(blk * 512, (blk + 1) * 512)
                        vector.tensor_scalar(f4w[:, :], src[:, sl],
                                             sb_swinv[:, wi:wi + 1], None,
                                             OP.mult)
                        vector.tensor_scalar(f4qs[:, :], f4w[:, :], -0.5,
                                             None, OP.is_le)
                        last = vector.scalar_tensor_tensor(
                            qdst[:, sl], f4w[:, :], 0.5, f4qs[:, :],
                            OP.is_ge, OP.subtract)
                last.then_inc(sW4, 1)
                for f in range(4):
                    it = m * 4 + f
                    vector.wait_ge(s4mm, it + 1)
                    gb = pb[(it % 2) * 4]
                    ub = pb[1 + (it % 2) * 4]
                    vector.tensor_tensor(f4g[it % 2][:, :], gb[:, :],
                                         sxbc[:, f * 512:(f + 1) * 512],
                                         OP.mult)
                    vector.tensor_scalar(f4g[it % 2][:, :], f4g[it % 2][:, :],
                                         sb_wsum[:, 0:1], None, OP.mult)
                    vector.tensor_tensor(f4u[it % 2][:, :], ub[:, :],
                                         sxbc[:, f * 512:(f + 1) * 512],
                                         OP.mult)
                    vector.tensor_scalar(f4u[it % 2][:, :], f4u[it % 2][:, :],
                                         sb_wsum[:, 1:2], None,
                                         OP.mult).then_inc(s4ep, 1)
                    vector.wait_ge(s4si, it + 1)
                    vector.tensor_tensor(f4h[it % 2][:, :], f4g[it % 2][:, :],
                                         f4u[it % 2][:, :],
                                         OP.mult).then_inc(s4hm, 1)
                    if it >= 1:
                        j = it - 1
                        vector.wait_ge(s4tr, j + 1)
                        if j >= 2:
                            vector.wait_ge(dsem["gpsimd"],
                                           g4base + 16 * (j - 1))
                        vector.tensor_copy(f4t[j % 2][:, :],
                                           pb[2 + j % 2][:, 0:512]).then_inc(
                                               s4te, 1)
            j = NI4 - 1
            vector.wait_ge(s4tr, j + 1)
            vector.wait_ge(dsem["gpsimd"], g4base + 16 * (j - 1))
            vector.tensor_copy(f4t[j % 2][:, :],
                               pb[2 + j % 2][:, 0:512]).then_inc(s4te, 1)

        @block.scalar
        def _(scalar):
            for it in range(NI4):
                scalar.wait_ge(s4ep, it + 1)
                scalar.activation(f4g[it % 2][:, :], f4g[it % 2][:, :],
                                  ACT.Silu).then_inc(s4si, 1)

        @block.gpsimd
        def _(gpsimd):
            for it in range(NI4):
                m, f = it // 4, it % 4
                gpsimd.wait_ge(s4te, it + 1)
                gpsimd.dma_start(
                    out=h_scr.ap()[f * 4:(f + 1) * 4, :,
                                   m * 128:(m + 1) * 128]
                    .rearrange("c p q -> p c q"),
                    in_=f4t[it % 2][:, :].rearrange("p (c q) -> p c q", c=4),
                ).then_inc(dsem["gpsimd"], 16)
                dcnt["gpsimd"] += 16
            gpsimd.wait_ge(dsem["gpsimd"], dcnt["gpsimd"])

        prev[0] = None
        f4.close()
        barrier()

        f3b = ExitStack()
        sb_mask = f3b.enter_context(nc.sbuf_tensor("sb_mask", [128, 128],
                                                   F32))
        sb_sel64 = f3b.enter_context(nc.sbuf_tensor("sb_sel64",
                                                     [64, 64 * 128], F32))
        xtmB = [f3b.enter_context(nc.sbuf_tensor(f"xtmB{j}", [128, DIN], F32))
                for j in range(2)]
        bcBB = [f3b.enter_context(nc.sbuf_tensor(f"bcBB{j}", [64, 128], F32))
                for j in range(2)]
        bcCB = [f3b.enter_context(nc.sbuf_tensor(f"bcCB{j}", [64, 128], F32))
                for j in range(2)]
        zch = [f3b.enter_context(nc.sbuf_tensor(f"zch{j}", [128, DIN], F32))
               for j in range(2)]
        spre = [f3b.enter_context(nc.sbuf_tensor(f"spre{j}", [64, DIN], F32))
                for j in range(2)]
        sinit = f3b.enter_context(nc.sbuf_tensor("sinit", [64, DIN], F32))
        cbt = f3b.enter_context(nc.sbuf_tensor("cbt", [128, 128], F32))
        mht = f3b.enter_context(nc.sbuf_tensor("mht", [128, 8 * 128], F32))
        ych = f3b.enter_context(nc.sbuf_tensor("ych", [128, DIN], F32))
        lamr = f3b.enter_context(nc.sbuf_tensor("lamr", [64, 64], F32))
        yst = f3b.enter_context(nc.sbuf_tensor("yst", [128, 512], F32))
        rows64 = f3b.enter_context(nc.sbuf_tensor("rows64", [64, 128],
                                                   F32))
        f3brow = f3b.enter_context(nc.sbuf_tensor("f3brow", [1, 128], F32))
        f3bld = f3b.enter_context(nc.sbuf_tensor("f3bld", [128, 128], F32))
        sqacc = f3b.enter_context(nc.sbuf_tensor("sqacc", [128, 128], F32))
        yfm = f3b.enter_context(nc.sbuf_tensor("yfm", [128, DIN], F32))

        def ldsel(sync):
            dma(sync, "sync", sb_mask[:], maskneg.ap())
            return dma(sync, "sync",
                       sb_sel64[:].rearrange("p (j q) -> p j q", j=64),
                       sel64.ap().rearrange("j p q -> p j q"))
        step("sync", ldsel, dmas=2)

        def getinit(gpsimd):
            gpsimd.wait_ge(sC, 1)
            return dma(gpsimd, "gpsimd", sinit[:, :], cc_out.ap()[0:64, :])
        step("gpsimd", getinit, dmas=1)

        def maskinit(vector):
            return vector.tensor_scalar(sinit[:, :], sinit[:, :],
                                        sb_smask[:, 0:1], None, OP.mult)
        step("vector", maskinit)

        def pfb0(sync):
            dma(sync, "sync", xtmB[0][:, :], xh_scr.ap()[0])
            dma(sync, "sync", bcBB[0][:, :], bc_scr.ap()[0, 0:64])
            dma(sync, "sync", bcCB[0][:, :], bc_scr.ap()[0, 64:128])
            return dma(sync, "sync",
                       zch[0][:, :].rearrange("p (mq q) -> p mq q", mq=32),
                       z_scr.ap()[:, :, 0:128].rearrange("m p q -> p m q"))
        step("sync", pfb0, dmas=4)

        for c in range(NT):
            cur, nxt = c % 2, (c + 1) % 2
            if c + 1 < NT:
                def pfb(sync, c=c, nxt=nxt):
                    dma(sync, "sync", xtmB[nxt][:, :], xh_scr.ap()[c + 1])
                    dma(sync, "sync", bcBB[nxt][:, :],
                        bc_scr.ap()[c + 1, 0:64])
                    dma(sync, "sync", bcCB[nxt][:, :],
                        bc_scr.ap()[c + 1, 64:128])
                    dma(sync, "sync",
                        zch[nxt][:, :].rearrange("p (mq q) -> p mq q", mq=32),
                        z_scr.ap()[:, :, (c + 1) * 128:(c + 2) * 128]
                        .rearrange("m p q -> p m q"))
                    return dma(sync, "sync", spre[nxt][:, :],
                               sloc_scr.ap()[c])
                step("sync", pfb, dmas=5)
            if c == 0:
                def z0(vector):
                    return vector.memset(spre[0][:, :], 0.0)
                step("vector", z0)

            def lam(scalar, c=c):
                return scalar.activation(lamr[:, 0:1],
                                         sb_lcum_pre[:, c:c + 1], ACT.Exp)
            step("scalar", lam)

            def lamt(tensor, c=c):
                tensor.transpose(pb[2][:1, 0:64], lamr[:, 0:1],
                                 sb_ident[:64, :64])
                return None
            step("tensor", lamt)

            def lamev(vector, c=c):
                return vector.tensor_copy(f3brow[:1, 0:64], pb[2][:1, 0:64])
            step("vector", lamev)

            def lamb(tensor, c=c):
                tensor.matmul(pb[2][0:64, 64:128], sb_ones_r[:, 0:64],
                              f3brow[:1, 0:64], start=True, stop=True)
                return None
            step("tensor", lamb)

            def lamev2(vector, c=c):
                return vector.tensor_copy(lamr[:, :], pb[2][0:64, 64:128])
            step("vector", lamev2)

            def sfull(vector, c=c, cur=cur):
                last = None
                for hh in range(H):
                    last = vector.scalar_tensor_tensor(
                        spre[cur][:, hh * 64:(hh + 1) * 64],
                        sinit[:, hh * 64:(hh + 1) * 64],
                        lamr[:, hh:hh + 1],
                        spre[cur][:, hh * 64:(hh + 1) * 64],
                        OP.mult, OP.add)
                return last
            step("vector", sfull)

            def trsb(tensor, c=c, cur=cur):
                tensor.transpose(pb[2][:128, 0:64],
                                 sb_lall[:, c * 128:(c + 1) * 128],
                                 sb_ident[:64, :64])
                tensor.transpose(pb[2][:128, 64:128],
                                 sb_dt[:, c * 128:(c + 1) * 128],
                                 sb_ident[:64, :64])
                tensor.matmul(pb[3][:, 0:128], bcBB[cur][:, :],
                              bcCB[cur][:, :], start=True, stop=True)
                return None
            step("tensor", trsb)

            def evb(vector, c=c):
                vector.tensor_copy(f3bld[:, 0:128], pb[2][:, 0:128])
                return vector.tensor_copy(cbt[:, :], pb[3][:, 0:128])
            step("vector", evb)

            def gamm(scalar, c=c):
                return scalar.activation(sb_gam[:, :], f3bld[:, 0:64],
                                         ACT.Exp)
            step("scalar", gamm)

            def ltr64(tensor, c=c):
                tensor.transpose(pb[2][:64, 0:128], f3bld[:, 0:64],
                                 sb_ident[:, :])
                return None
            step("tensor", ltr64)

            def lev64(vector, c=c):
                return vector.tensor_copy(rows64[:, :], pb[2][:64, 0:128])
            step("vector", lev64)

            for g in range(8):
                def lb(tensor, c=c, g=g):
                    for j in range(8):
                        bank = pb[2 + j // 4]
                        slot = (j % 4) * 128
                        hh = g * 8 + j
                        tensor.matmul(bank[:, slot:slot + 128],
                                      sb_sel64[:, hh * 128:(hh + 1) * 128],
                                      rows64[:, :], start=True, stop=True)
                    return None
                step("tensor", lb)

                def mbuild(vector, c=c, g=g):
                    last = None
                    for j in range(8):
                        hh = g * 8 + j
                        bank = pb[2 + j // 4]
                        slot = (j % 4) * 128
                        last = vector.scalar_tensor_tensor(
                            mht[:, j * 128:(j + 1) * 128],
                            bank[:, slot:slot + 128],
                            f3bld[:, hh:hh + 1], sb_mask[:, :],
                            OP.subtract, OP.add)
                    return last
                step("vector", mbuild)

                def mexp(scalar, c=c, g=g):
                    return scalar.activation(mht[:, :], mht[:, :], ACT.Exp)
                step("scalar", mexp)

                def mfin(vector, c=c, g=g):
                    last = None
                    for j in range(8):
                        hh = g * 8 + j
                        last = vector.scalar_tensor_tensor(
                            mht[:, j * 128:(j + 1) * 128],
                            mht[:, j * 128:(j + 1) * 128],
                            f3bld[:, 64 + hh:64 + hh + 1], cbt[:, :],
                            OP.mult, OP.mult)
                    return last
                step("vector", mfin)

                def ymm(tensor, c=c, g=g, cur=cur):
                    for j in range(8):
                        hh = g * 8 + j
                        tensor.matmul(
                            pb[4][:, j * 64:(j + 1) * 64],
                            mht[:, j * 128:(j + 1) * 128],
                            xtmB[cur][:, hh * 64:(hh + 1) * 64],
                            start=True, stop=True)
                    tensor.matmul(
                        pb[5][:, 0:512], bcCB[cur][:, :],
                        spre[cur][:, g * 512:(g + 1) * 512],
                        start=True, stop=True)
                    return None
                step("tensor", ymm)

                def yev(scalar, c=c, g=g):
                    last = None
                    for j in range(8):
                        hh = g * 8 + j
                        last = scalar.activation(
                            yst[:, j * 64:(j + 1) * 64],
                            pb[5][:, j * 64:(j + 1) * 64], ACT.Copy,
                            scale=sb_gam[:, hh:hh + 1])
                    return last
                step("scalar", yev)

                def yev2(vector, c=c, g=g, cur=cur):
                    last = None
                    for j in range(8):
                        hh = g * 8 + j
                        vector.scalar_tensor_tensor(
                            ych[:, hh * 64:(hh + 1) * 64],
                            xtmB[cur][:, hh * 64:(hh + 1) * 64],
                            sb_dbc[:, hh:hh + 1],
                            pb[4][:, j * 64:(j + 1) * 64],
                            OP.mult, OP.add)
                        last = vector.tensor_tensor(
                            ych[:, hh * 64:(hh + 1) * 64],
                            ych[:, hh * 64:(hh + 1) * 64],
                            yst[:, j * 64:(j + 1) * 64], OP.add)
                    return last
                step("vector", yev2)

            transpose_batch([(ych[:, q * 128:(q + 1) * 128],
                              yfm[:, q * 128:(q + 1) * 128])
                             for q in range(32)])

            def gsil(scalar, c=c, cur=cur):
                return scalar.activation(zch[cur][:, :], zch[cur][:, :],
                                         ACT.Silu)
            step("scalar", gsil)

            def gmul(vector, c=c, cur=cur):
                vector.memset(sqacc[:, :], 0.0)
                last = None
                for q in range(32):
                    sl = slice(q * 128, (q + 1) * 128)
                    vector.tensor_tensor(yfm[:, sl], yfm[:, sl],
                                         zch[cur][:, sl], OP.mult)
                    vector.tensor_tensor(yst[:, 0:128], yfm[:, sl],
                                         yfm[:, sl], OP.mult)
                    last = vector.tensor_tensor(sqacc[:, :], sqacc[:, :],
                                                yst[:, 0:128], OP.add)
                return last
            step("vector", gmul)

            def gred(tensor, c=c):
                tensor.matmul(pb[2][:1, 0:128], sb_ones_c[:, 0:1],
                              sqacc[:, :], start=True, stop=True)
                return None
            step("tensor", gred)

            def grr(vector, c=c):
                return vector.tensor_scalar(f3brow[:1, 0:128],
                                            pb[2][:1, 0:128], 1.0 / DIN,
                                            1e-5, OP.mult, OP.add)
            step("vector", grr)

            def grs(scalar, c=c):
                scalar.activation(f3brow[:1, 0:128], f3brow[:1, 0:128],
                                  ACT.Ln)
                return scalar.activation(
                    f3brow[:1, 0:128], f3brow[:1, 0:128], ACT.Exp,
                    scale=sb_cconst[:1, C_NEGHALF:C_NEGHALF + 1])
            step("scalar", grs)

            def grb(tensor, c=c):
                tensor.matmul(pb[2][:, 128:256], sb_ones_r[:, :],
                              f3brow[:1, 0:128], start=True, stop=True)
                return None
            step("tensor", grb)

            def grbe(vector, c=c):
                return vector.tensor_copy(sqacc[:, :], pb[2][:, 128:256])
            step("vector", grbe)

            def yrm(vector, c=c):
                last = None
                for q in range(32):
                    sl = slice(q * 128, (q + 1) * 128)
                    last = vector.scalar_tensor_tensor(
                        yfm[:, sl], yfm[:, sl], sb_normw[:, q:q + 1],
                        sqacc[:, :], OP.mult, OP.mult)
                return last
            step("vector", yrm)

            def styr(gpsimd, c=c):
                return dma(gpsimd, "gpsimd",
                           yr_scr.ap()[c].rearrange("k p q -> p k q"),
                           yfm[:, :].rearrange("p (k q) -> p k q", k=KWO))
            step("gpsimd", styr, dmas=1)

        f3b.close()
        barrier()

        f3c = ExitStack()
        yrb = f3c.enter_context(nc.sbuf_tensor("yrb", [128, KWO * 512], F32))
        wob = [f3c.enter_context(nc.sbuf_tensor(f"wob{j}", [128, KWO * 128],
                                                F32)) for j in range(2)]
        f3cs = [f3c.enter_context(nc.sbuf_tensor(f"f3cs{j}", [128, 512],
                                                 F32)) for j in range(2)]
        f3ct = [f3c.enter_context(nc.sbuf_tensor(f"f3ct{j}", [128, 512],
                                                 F32)) for j in range(2)]
        wdp = [f3c.enter_context(nc.sbuf_tensor(f"wdp{j}", [128, MDN * 128],
                                                F32)) for j in range(2)]
        qwp = [f3c.enter_context(nc.sbuf_tensor(f"qwp{j}", [128, MDN * 128],
                                                BF16)) for j in range(2)]
        f5pc = f3c.enter_context(nc.sbuf_tensor("f5pc", [128, 512], F32))
        f5pw = f3c.enter_context(nc.sbuf_tensor("f5pw", [128, 512], F32))

        wob_after = {}
        wd_after = {}
        store_after = {}

        @block.sync
        def _(sync):
            for f in range(4):
                for m in range(MWO):
                    it = f * MWO + m
                    if it >= 2:
                        sync.wait_ge(sWC, it - 1)
                    sync.dma_start(
                        out=wob[it % 2][:, :].rearrange("p (k q) -> p k q",
                                                        k=KWO),
                        in_=w_out_t.ap()[:, m].rearrange("k p q -> p k q"),
                    ).then_inc(dsem["sync"], 16)
                    dcnt["sync"] += 16
                    wob_after[it] = dcnt["sync"]
                    k = it
                    if k < KDN:
                        if k >= 2:
                            sync.wait_ge(sWP, k - 1)
                        sync.dma_start(
                            out=wdp[k % 2][:, :].rearrange("p (m q) -> p m q",
                                                           m=MDN),
                            in_=w_down_t.ap()[k].rearrange("m p q -> p m q"),
                        ).then_inc(dsem["sync"], 16)
                        dcnt["sync"] += 16
                        wd_after[k] = dcnt["sync"]
                    if 2 <= it < KDN + 2:
                        ks = it - 2
                        sync.wait_ge(sWP, ks + 1)
                        sync.dma_start(
                            out=qwd_scr.ap()[ks].rearrange("m p q -> p m q"),
                            in_=qwp[ks % 2][:, :].rearrange(
                                "p (m q) -> p m q", m=MDN),
                        ).then_inc(dsem["sync"], 16)
                        dcnt["sync"] += 16
                        store_after[ks] = dcnt["sync"]

        NIC = 4 * MWO
        scmm = e(nc.semaphore("scmm"))
        sctr = e(nc.semaphore("sctr"))
        scte = e(nc.semaphore("scte"))
        scld = e(nc.semaphore("scld"))
        gcbase = dcnt["gpsimd"]
        acbase = dcnt["scalar"]

        @block.scalar
        def _(scalar):
            for f in range(4):
                if f >= 1:
                    scalar.wait_ge(scmm, f * MWO)
                for k in range(KWO):
                    scalar.dma_start(
                        out=yrb[:, k * 512:(k + 1) * 512].rearrange(
                            "p (cs q) -> p cs q", cs=4),
                        in_=yr_scr.ap()[4 * f:4 * f + 4, k].rearrange(
                            "cs p q -> p cs q"),
                    ).then_inc(dsem["scalar"], 16)
                    dcnt["scalar"] += 16

        @block.tensor
        def _(tensor):
            def trc(j):
                tensor.wait_ge(sWC, j + 1)
                if j >= 2:
                    tensor.wait_ge(scte, j - 1)
                for q in range(4):
                    tensor.transpose(pb[2 + 2 * (j % 2)][:,
                                                         q * 128:(q + 1) * 128],
                                     f3cs[j % 2][:, q * 128:(q + 1) * 128],
                                     sb_ident[:, :])
                tensor.drain()
                tensor.nop().then_inc(sctr, 1)

            for it in range(NIC):
                f, m = it // MWO, it % MWO
                if it >= 1:
                    trc(it - 1)
                if m == 0:
                    tensor.wait_ge(dsem["scalar"],
                                   acbase + 16 * KWO * (f + 1))
                tensor.wait_ge(dsem["sync"], wob_after[it])
                if it >= 3:
                    tensor.wait_ge(sWC, it - 2)
                last = None
                for k in range(KWO):
                    last = tensor.matmul(
                        pb[(0, 1, 5)[it % 3]][:, :],
                        wob[it % 2][:, k * 128:(k + 1) * 128],
                        yrb[:, k * 512:(k + 1) * 512],
                        start=(k == 0), stop=(k == KWO - 1))
                last.then_inc(scmm, 1)
            trc(NIC - 1)

        @block.sync
        def _(sync):
            for ks in (KDN - 2, KDN - 1):
                sync.wait_ge(sWP, ks + 1)
                sync.dma_start(
                    out=qwd_scr.ap()[ks].rearrange("m p q -> p m q"),
                    in_=qwp[ks % 2][:, :].rearrange("p (m q) -> p m q",
                                                    m=MDN),
                ).then_inc(dsem["sync"], 16)
                dcnt["sync"] += 16
                store_after[ks] = dcnt["sync"]
            sync.wait_ge(dsem["sync"], dcnt["sync"])

        @block.vector
        def _(vector):
            for it in range(NIC):
                vector.wait_ge(scmm, it + 1)
                ins = vector.tensor_copy(f3cs[it % 2][:, :],
                                         pb[(0, 1, 5)[it % 3]][:, :])
                ins.then_inc(sWC, 1)
                k = it
                if k < KDN:
                    vector.wait_ge(dsem["sync"], wd_after[k])
                    if k >= 2:
                        vector.wait_ge(dsem["sync"], store_after[k - 2])
                    last = None
                    for blk in range(4):
                        sl = slice(blk * 512, (blk + 1) * 512)
                        vector.tensor_scalar(f5pw[:, :], wdp[k % 2][:, sl],
                                             sb_swinv[:, 2:3], None, OP.mult)
                        vector.tensor_scalar(f5pc[:, :], f5pw[:, :], -0.5,
                                             None, OP.is_le)
                        last = vector.scalar_tensor_tensor(
                            qwp[k % 2][:, sl], f5pw[:, :], 0.5, f5pc[:, :],
                            OP.is_ge, OP.subtract)
                    last.then_inc(sWP, 1)
                if it >= 1:
                    j = it - 1
                    vector.wait_ge(sctr, j + 1)
                    if j >= 2:
                        vector.wait_ge(dsem["gpsimd"], gcbase + 16 * (j - 1))
                    vector.tensor_copy(f3ct[j % 2][:, :],
                                       pb[2 + 2 * (j % 2)][:, 0:512]
                                       ).then_inc(scte, 1)
            j = NIC - 1
            vector.wait_ge(sctr, j + 1)
            vector.wait_ge(dsem["gpsimd"], gcbase + 16 * (j - 1))
            vector.tensor_copy(f3ct[j % 2][:, :],
                               pb[2 + 2 * (j % 2)][:, 0:512]).then_inc(scte,
                                                                       1)

        @block.gpsimd
        def _(gpsimd):
            for it in range(NIC):
                f, m = it // MWO, it % MWO
                gpsimd.wait_ge(scte, it + 1)
                gpsimd.dma_start(
                    out=s_scr.ap()[f * 4:(f + 1) * 4, :,
                                   m * 128:(m + 1) * 128]
                    .rearrange("c p q -> p c q"),
                    in_=f3ct[it % 2][:, :].rearrange("p (c q) -> p c q", c=4),
                ).then_inc(dsem["gpsimd"], 16)
                dcnt["gpsimd"] += 16
            gpsimd.wait_ge(dsem["gpsimd"], dcnt["gpsimd"])

        prev[0] = None
        f3c.close()
        barrier()

        f5a = ExitStack()
        hch = [f5a.enter_context(nc.sbuf_tensor("hch0", [128, IFF], F32))]
        sb_lnw = f5a.enter_context(nc.sbuf_tensor("sb_lnw", [128, IFF], F32))
        f5w = f5a.enter_context(nc.sbuf_tensor("f5w", [128, IFF], F32))
        f5v = f5a.enter_context(nc.sbuf_tensor("f5v", [128, IFF], F32))
        f5h = f5a.enter_context(nc.sbuf_tensor("f5h", [128, IFF], F16))
        f5qs = f5a.enter_context(nc.sbuf_tensor("f5qs", [128, IFF], BF16))
        f5c = f5a.enter_context(nc.sbuf_tensor("f5c", [128, 8], F32))

        def pf50(sync):
            return dma(sync, "sync", sb_lnw[:, :], lnw_bc.ap())
        step("sync", pf50, dmas=1)

        for c in range(NT):
            cur = 0
            def pf5(sync, c=c):
                return dma(sync, "sync", hch[0][:, :], h_scr.ap()[c])
            step("sync", pf5, dmas=1)

            def rms(scalar, cur=cur):
                return scalar.activation(f5w[:, :], hch[cur][:, :],
                                         ACT.Square, accum_out=f5c[:, 0:1])
            step("scalar", rms)

            def rr(vector, cur=cur):
                return vector.tensor_scalar(f5c[:, 1:2], f5c[:, 0:1],
                                            1.0 / IFF, 1e-6, OP.mult, OP.add)
            step("vector", rr)

            def rr2(scalar):
                scalar.activation(f5c[:, 2:3], f5c[:, 1:2], ACT.Ln)
                return scalar.activation(
                    f5c[:, 3:4], f5c[:, 2:3], ACT.Exp,
                    scale=sb_cconst[:, C_NEGHALF:C_NEGHALF + 1])
            step("scalar", rr2)

            def h2(vector, cur=cur):
                vector.tensor_scalar(f5w[:, :], hch[cur][:, :], f5c[:, 3:4],
                                     None, OP.mult)
                last = None
                for q in range(4):
                    sl = slice(q * 2048, (q + 1) * 2048)
                    last = vector.tensor_tensor(f5w[:, sl], f5w[:, sl],
                                                sb_lnw[:, sl], OP.mult)
                return last
            step("vector", h2)

            def qs(vector, c=c):
                ins = vector.reduce_max(f5c[:, 4:5], f5w[:, :], AX,
                                        apply_absolute_value=True)
                sfence(vector, "vector", ins)
                vector.tensor_scalar(f5c[:, 4:5], f5c[:, 4:5], 1e-5, None,
                                     OP.max)
                ins = vector.reciprocal(f5c[:, 5:6], f5c[:, 4:5])
                sfence(vector, "vector", ins)
                vector.tensor_scalar(f5c[:, 5:6], f5c[:, 5:6], 127.0, None,
                                     OP.mult)
                ins = vector.tensor_scalar(sb_s2c[:, c:c + 1], f5c[:, 4:5],
                                            1.0 / 127.0, None, OP.mult)
                sfence(vector, "vector", ins)
                vector.tensor_scalar(f5v[:, :], f5w[:, :], f5c[:, 5:6],
                                     None, OP.mult)
                vector.tensor_scalar(f5h[:, :], f5v[:, :], 1536.0, None,
                                     OP.add)
                vector.tensor_scalar(f5w[:, :], f5h[:, :], 1536.0, None,
                                     OP.subtract)
                vector.tensor_tensor(hch[0][:, :], f5v[:, :], f5w[:, :],
                                     OP.subtract)
                vector.scalar_tensor_tensor(f5v[:, :], hch[0][:, :], 0.5,
                                            f5w[:, :], OP.is_gt, OP.add)
                vector.scalar_tensor_tensor(f5w[:, :], hch[0][:, :], -0.5,
                                            f5v[:, :], OP.is_ge, OP.add)
                return vector.tensor_scalar(f5v[:, :], f5w[:, :], 1.0, None,
                                            OP.subtract)
            step("vector", qs)

            transpose_batch([(f5v[:, k * 128:(k + 1) * 128],
                              f5qs[:, k * 128:(k + 1) * 128])
                             for k in range(KDN)])

            def stq5(gpsimd, c=c):
                return dma(
                    gpsimd, "gpsimd",
                    qh_scr.ap()[:, :, c * 128:(c + 1) * 128].rearrange(
                        "k p q -> p k q"),
                    f5qs[:, :].rearrange("p (k q) -> p k q", k=KDN))
            step("gpsimd", stq5, dmas=1)

        f5a.close()
        barrier()

        f5b = ExitStack()
        qhb = f5b.enter_context(nc.sbuf_tensor("qhb", [128, KDN * 512], BF16))
        qwdb = [f5b.enter_context(nc.sbuf_tensor(f"qwdb{j}",
                                                 [128, KDN * 128], BF16))
                for j in range(2)]
        s2bc = f5b.enter_context(nc.sbuf_tensor("s2bc", [128, R], F32))
        f5d = [f5b.enter_context(nc.sbuf_tensor(f"f5d{j}", [128, 512], F32))
               for j in range(2)]
        gtm = f5b.enter_context(nc.sbuf_tensor("gtm", [128, 4 * 2048], F32))
        xr2 = [f5b.enter_context(nc.sbuf_tensor(f"xr2{j}", [128, 2048], F32))
               for j in range(2)]
        sr2 = [f5b.enter_context(nc.sbuf_tensor(f"sr2{j}", [128, 2048], F32))
               for j in range(2)]
        f5row = f5b.enter_context(nc.sbuf_tensor("f5row", [1, 128], F32))

        @block.sync
        def _(sync):
            for f in range(4):
                for m in range(MDN):
                    it = f * MDN + m
                    if it >= 2:
                        sync.wait_ge(sWD, it - 1)
                    sync.dma_start(
                        out=qwdb[it % 2][:, :].rearrange("p (k q) -> p k q",
                                                         k=KDN),
                        in_=qwd_scr.ap()[:, m].rearrange("k p q -> p k q"),
                    ).then_inc(dsem["sync"], 16)
                    dcnt["sync"] += 16
        wd_base = dcnt["sync"] - 16 * 4 * MDN

        for i in range(NT):
            bcast_col(sb_s2c[:, i:i + 1], s2bc[:, i * 128:(i + 1) * 128],
                      f5row)

        NID = 4 * MDN
        sdmm = e(nc.semaphore("sdmm"))
        sdep = e(nc.semaphore("sdep"))
        sdsi = e(nc.semaphore("sdsi"))
        sdtr = e(nc.semaphore("sdtr"))
        sdte = e(nc.semaphore("sdte"))
        sdrs = e(nc.semaphore("sdrs"))
        adbase = dcnt["scalar"]

        @block.scalar
        def _(scalar):
            for f in range(4):
                if f >= 1:
                    scalar.wait_ge(sdmm, f * MDN)
                scalar.dma_start(
                    out=qhb[:, :].rearrange("p (k q) -> p k q", k=KDN),
                    in_=qh_scr.ap()[:, :, f * 512:(f + 1) * 512]
                    .rearrange("k p q -> p k q"),
                ).then_inc(dsem["scalar"], 16)
                dcnt["scalar"] += 16
                for it in range(f * MDN, (f + 1) * MDN):
                    scalar.wait_ge(sdep, it + 1)
                    scalar.activation(f5d[it % 2][:, :], f5d[it % 2][:, :],
                                      ACT.Silu).then_inc(sdsi, 1)
                for q in range(4):
                    r = f * 4 + q
                    cch = f * 4 + q
                    if r >= 2:
                        scalar.wait_ge(sOut, 16 * (r - 1))
                    scalar.dma_start(
                        out=xr2[q % 2][:, :],
                        in_=x_own.ap()[cch * 128:(cch + 1) * 128, :],
                    ).then_inc(dsem["scalar"], 16)
                    scalar.dma_start(
                        out=sr2[q % 2][:, :],
                        in_=s_scr.ap()[cch]).then_inc(dsem["scalar"], 16)
                    dcnt["scalar"] += 32

        @block.tensor
        def _(tensor):
            def trd(j):
                tensor.wait_ge(sdsi, j + 1)
                if j >= 1:
                    tensor.wait_ge(sdte, j)
                for q in range(4):
                    tensor.transpose(pb[2][:, q * 128:(q + 1) * 128],
                                     f5d[j % 2][:, q * 128:(q + 1) * 128],
                                     sb_ident[:, :])
                tensor.drain()
                tensor.nop().then_inc(sdtr, 1)

            for it in range(NID):
                f, m = it // MDN, it % MDN
                if it >= 1:
                    trd(it - 1)
                if m == 0:
                    tensor.wait_ge(dsem["scalar"], adbase + 144 * f + 16)
                tensor.wait_ge(dsem["sync"], wd_base + 16 * (it + 1))
                if it >= 2:
                    tensor.wait_ge(sdep, it - 1)
                last = None
                for k in range(KDN):
                    last = tensor.matmul(
                        pb[it % 2][:, :],
                        qwdb[it % 2][:, k * 128:(k + 1) * 128],
                        qhb[:, k * 512:(k + 1) * 512],
                        start=(k == 0), stop=(k == KDN - 1))
                last.then_inc(sdmm, 1)
            trd(NID - 1)

        @block.vector
        def _(vector):
            def tev(j):
                jm = j % MDN
                vector.wait_ge(sdtr, j + 1)
                last = None
                for q in range(4):
                    last = vector.tensor_copy(
                        gtm[:, q * 2048 + jm * 128:q * 2048 + (jm + 1) * 128],
                        pb[2][:, q * 128:(q + 1) * 128])
                last.then_inc(sdte, 1)
                if (j + 1) % MDN == 0:
                    fdone = j // MDN
                    for q in range(4):
                        r = fdone * 4 + q
                        vector.wait_ge(dsem["scalar"],
                                       adbase + 144 * fdone + 48 + 32 * q)
                        vector.tensor_tensor(
                            sr2[q % 2][:, :], sr2[q % 2][:, :],
                            gtm[:, q * 2048:(q + 1) * 2048], OP.mult)
                        vector.tensor_tensor(
                            sr2[q % 2][:, :], xr2[q % 2][:, :],
                            sr2[q % 2][:, :], OP.add).then_inc(sdrs, 1)

            for it in range(NID):
                f = it // MDN
                if it >= 1:
                    tev(it - 1)
                vector.wait_ge(sdmm, it + 1)
                ins = vector.tensor_tensor(
                    f5d[it % 2][:, :], pb[it % 2][:, :],
                    s2bc[:, f * 512:f * 512 + 512], OP.mult)
                ins.then_inc(sWD, 1)
                vector.tensor_scalar(f5d[it % 2][:, :], f5d[it % 2][:, :],
                                     sb_wsum[:, 2:3], None,
                                     OP.mult).then_inc(sdep, 1)
            tev(NID - 1)

        @block.gpsimd
        def _(gpsimd):
            for r in range(16):
                cch = r
                gpsimd.wait_ge(sdrs, r + 1)
                gpsimd.dma_start(
                    out=out.ap()[cch * 128:(cch + 1) * 128, :],
                    in_=sr2[r % 2][:, :]).then_inc(sOut, 16)

        prev[0] = None
        f5b.close()

        @block.gpsimd
        def _(gpsimd):
            gpsimd.wait_ge(sOut, 16 * NT)

        @block.sync
        def _(sync):
            sync.wait_ge(sOut, 16 * NT)

    return nc


_NC_CACHE = [None]


def kernel(**inputs):
    import ml_dtypes

    def tile4(w):
        K, M = w.shape[0] // 128, w.shape[1] // 128
        return np.ascontiguousarray(
            w.reshape(K, 128, M, 128).transpose(0, 2, 1, 3))

    x = np.asarray(inputs["L0_attn_x"], np.float32)
    w_in = np.asarray(inputs["L0_attn_W_in"], np.float32)
    conv_w = np.asarray(inputs["L0_attn_conv_w"], np.float32)
    conv_b = np.asarray(inputs["L0_attn_conv_b"], np.float32)
    dt_bias = np.asarray(inputs["L0_attn_dt_bias"], np.float32)
    a_log = np.asarray(inputs["L0_attn_A_log"], np.float32)
    d_vec = np.asarray(inputs["L0_attn_D"], np.float32)
    norm_w = np.asarray(inputs["L0_attn_norm_w"], np.float32)
    w_out = np.asarray(inputs["L0_attn_W_out"], np.float32)
    w_gate = np.asarray(inputs["L0_ffn_w_gate"], np.float32)
    w_up = np.asarray(inputs["L0_ffn_w_up"], np.float32)
    w_down = np.asarray(inputs["L0_ffn_w_down"], np.float32)
    ln_w = np.asarray(inputs["L0_ffn_ln_weight"], np.float32)

    w_in_t = np.zeros((DM, M_IN * 128), np.float32)
    w_in_t[:, :8384] = w_in.T
    w_in_t = tile4(w_in_t)
    w_out_t = tile4(np.ascontiguousarray(w_out.T))
    w_gate_t = tile4(np.ascontiguousarray(w_gate.T))
    w_up_t = tile4(np.ascontiguousarray(w_up.T))
    w_down_t = tile4(np.ascontiguousarray(w_down.T))

    cw_cols = np.ascontiguousarray(
        conv_w.reshape(33, 128, 4).transpose(1, 0, 2).reshape(128, 132))
    cb_cols = np.ascontiguousarray(conv_b.reshape(33, 128).T)
    normw_cols = np.ascontiguousarray(norm_w.reshape(KWO, 128).T)
    lnw_bcast = np.ascontiguousarray(np.broadcast_to(ln_w, (128, IFF)))
    d_bcast = np.ascontiguousarray(np.broadcast_to(d_vec, (128, H)))
    identity = np.eye(128, dtype=np.float32)
    identityb = np.eye(128, dtype=ml_dtypes.bfloat16)
    mneg = np.where(np.arange(128)[None, :] >= np.arange(128)[:, None],
                    0.0, -1e9).astype(np.float32)
    ones_row = np.ones((1, 128), np.float32)
    ones_col = np.ones((128, 1), np.float32)
    sel64np = np.zeros((64, 64, 128), np.float32)
    for hh in range(64):
        sel64np[hh, hh, :] = 1.0
    sel = np.zeros((8, 8, 128), np.float32)
    for j in range(8):
        sel[j, j, :] = 1.0
    cconsts = np.zeros((128, 4), np.float32)
    cconsts[:, C_NEGHALF] = -0.5

    common = dict(
        w_in_t=w_in_t, w_out_t=w_out_t, w_gate_t=w_gate_t, w_up_t=w_up_t,
        w_down_t=w_down_t, cw_cols=cw_cols, cb_cols=cb_cols,
        normw_cols=normw_cols, lnw_bc=lnw_bcast,
        dtb_col=dt_bias.reshape(64, 1), alog_col=a_log.reshape(64, 1),
        d_bc=d_bcast, ident=identity, identb=identityb, maskneg=mneg,
        ones_r=ones_row, ones_c=ones_col, selmat=sel, sel64=sel64np,
        cconst=cconsts)

    in_maps = []
    for c in range(NCORE):
        b, half = c // 2, c % 2
        x_own_c = np.ascontiguousarray(x[b, half * R:(half + 1) * R])
        if half == 0:
            x_halo_c = np.zeros((3, DM), np.float32)
        else:
            x_halo_c = np.ascontiguousarray(x[b, R - 3:R])
        smask_c = np.full((64, 1), 1.0 if half == 1 else 0.0, np.float32)
        in_maps.append(dict(common, x_own=x_own_c, x_halo=x_halo_c,
                            smask=smask_c))

    if _NC_CACHE[0] is None:
        _NC_CACHE[0] = build_program()
    res = run_bass_kernel_spmd(_NC_CACHE[0], in_maps,
                               core_ids=list(range(NCORE)))

    full = np.empty((B, T, DM), np.float32)
    for c in range(NCORE):
        b, half = c // 2, c % 2
        full[b, half * R:(half + 1) * R] = np.asarray(res.results[c]["out"])
    return full
